# version 02
# baseline (speedup 1.0000x reference)
import ctypes
import ctypes.util
from contextlib import ExitStack

import numpy as np
import ml_dtypes

import concourse.bass as bass
import concourse.mybir as mybir
from concourse.bass_utils import run_bass_kernel_spmd

F32 = mybir.dt.float32
BF16 = mybir.dt.bfloat16
AF = mybir.ActivationFunctionType
ALU = mybir.AluOpType

NCORES = 8
T = 2048
NT = 16
D = 1024
NH = 16
DK = 64
RANK = 256
DFF = 4096
NSLAB = 4
FSLAB = 8

ENGS = ("sync", "gpsimd", "tensor", "vector", "scalar")
DEBUG_A = False


class Sched:

    def __init__(self, nc, es):
        self.nc = nc
        self.ops = {e: [] for e in ENGS}
        self.cnt = {e: 0 for e in ENGS}
        self.hw = {e: {s: 0 for s in ENGS} for e in ENGS}
        self.sems = {e: es.enter_context(nc.semaphore(f"sem_{e}")) for e in ENGS}

    def mult(self, e):
        return 16 if e in ("sync", "gpsimd") else 1

    def throttle(self, eng):
        v = self.cnt[eng] * self.mult(eng)
        if v > 0:
            sem = self.sems[eng]
            self.ops[eng].append(lambda e, sem=sem, v=v: e.wait_ge(sem, v))

    def wait(self, eng, dep):
        src, val = dep
        if src == eng or val <= 0:
            return
        if self.hw[eng][src] >= val:
            return
        self.hw[eng][src] = val
        sem = self.sems[src]
        v = val * self.mult(src)
        self.ops[eng].append(lambda e, sem=sem, v=v: e.wait_ge(sem, v))

    def emit(self, eng, fn, deps=(), inc=True):
        for d in deps:
            self.wait(eng, d)
        if eng in ("sync", "gpsimd"):
            self.throttle(eng)
        elif inc and self.cnt[eng] > 0:
            sem, v = self.sems[eng], self.cnt[eng]
            self.ops[eng].append(lambda e, sem=sem, v=v: e.wait_ge(sem, v))
        if inc:
            self.cnt[eng] += 1
            sem, m = self.sems[eng], self.mult(eng)
            self.ops[eng].append(lambda e, fn=fn, sem=sem, m=m: fn(e).then_inc(sem, m))
        else:
            self.ops[eng].append(fn)
        return (eng, self.cnt[eng])

    def marker(self, eng):
        return (eng, self.cnt[eng])


def build_nc():
    nc = bass.Bass("TRN2", target_bir_lowering=False)
    es_top = ExitStack()

    d_xt = nc.dram_tensor("xt", [D, T], F32, kind="ExternalInput")
    d_xs = nc.dram_tensor("xs", [T, D], F32, kind="ExternalInput")
    d_wq = nc.dram_tensor("wq", [D, D], F32, kind="ExternalInput")
    d_wk = nc.dram_tensor("wk", [D, D], F32, kind="ExternalInput")
    d_wv = nc.dram_tensor("wv", [D, D], F32, kind="ExternalInput")
    d_bq = nc.dram_tensor("bqp", [128, 8], F32, kind="ExternalInput")
    d_bk = nc.dram_tensor("bkp", [128, 8], F32, kind="ExternalInput")
    d_bv = nc.dram_tensor("bvp", [128, 8], F32, kind="ExternalInput")
    d_u1 = nc.dram_tensor("u1p", [64, 16 * RANK], F32, kind="ExternalInput")
    d_v1 = nc.dram_tensor("v1r", [RANK, DFF], F32, kind="ExternalInput")
    d_b1 = nc.dram_tensor("b1p", [128, 32], F32, kind="ExternalInput")
    d_u2 = nc.dram_tensor("u2r", [DFF, RANK], F32, kind="ExternalInput")
    d_v2 = nc.dram_tensor("v2p", [RANK, D], F32, kind="ExternalInput")
    d_b2 = nc.dram_tensor("b2p", [128, 8], F32, kind="ExternalInput")
    d_cos = nc.dram_tensor("cos128", [128, T], F32, kind="ExternalInput")
    d_sin = nc.dram_tensor("sin128", [128, T], F32, kind="ExternalInput")
    d_mask = nc.dram_tensor("mask01", [64, 128], BF16, kind="ExternalInput")
    d_sel8 = nc.dram_tensor("sel8", [128, 8], F32, kind="ExternalInput")
    d_sel8t = nc.dram_tensor("sel8t", [8, 128], F32, kind="ExternalInput")
    d_id16 = nc.dram_tensor("id16", [128, 64], BF16, kind="ExternalInput")
    d_idf = nc.dram_tensor("idf", [128, 64], F32, kind="ExternalInput")
    d_eps = nc.dram_tensor("epsc", [128, 1], F32, kind="ExternalInput")
    d_out = nc.dram_tensor("outs", [T, D], F32, kind="ExternalOutput")

    u = es_top.enter_context(nc.sbuf_tensor("u", [128, NT * 1024], F32))
    usq = es_top.enter_context(nc.sbuf_tensor("usq", [128, 1024], F32))
    w128 = es_top.enter_context(nc.sbuf_tensor("w128", [128, T], F32))
    sb_bq = es_top.enter_context(nc.sbuf_tensor("sb_bq", [128, 8], F32))
    sb_bk = es_top.enter_context(nc.sbuf_tensor("sb_bk", [128, 8], F32))
    sb_bv = es_top.enter_context(nc.sbuf_tensor("sb_bv", [128, 8], F32))
    sb_b1 = es_top.enter_context(nc.sbuf_tensor("sb_b1", [128, 32], F32))
    sb_b2 = es_top.enter_context(nc.sbuf_tensor("sb_b2", [128, 8], F32))
    sb_mask = es_top.enter_context(nc.sbuf_tensor("sb_mask", [64, 128], BF16))
    sb_sel8 = es_top.enter_context(nc.sbuf_tensor("sb_sel8", [128, 8], F32))
    sb_sel8t = es_top.enter_context(nc.sbuf_tensor("sb_sel8t", [8, 128], F32))
    sb_id16 = es_top.enter_context(nc.sbuf_tensor("sb_id16", [128, 64], BF16))
    sb_idf = es_top.enter_context(nc.sbuf_tensor("sb_idf", [128, 64], F32))
    sb_eps = es_top.enter_context(nc.sbuf_tensor("sb_eps", [128, 1], F32))
    lnS = es_top.enter_context(nc.sbuf_tensor("lnS", [128, 32], F32))
    lnT = es_top.enter_context(nc.sbuf_tensor("lnT", [8, 32], F32))
    lnM = es_top.enter_context(nc.sbuf_tensor("lnM", [8, 16], F32))
    bcS = es_top.enter_context(nc.sbuf_tensor("bcS", [128, 32], F32))
    zrec = es_top.enter_context(nc.sbuf_tensor("zrec", [128, 4], F32))
    outS = es_top.enter_context(nc.sbuf_tensor("outS", [128, 2 * 1024], F32))

    S = Sched(nc, es_top)

    esA = ExitStack()
    w16 = {t: esA.enter_context(nc.sbuf_tensor(f"w16_{t}", [128, 8 * D], BF16))
           for t in "qkv"}
    xt16 = [esA.enter_context(nc.sbuf_tensor(f"xt16_{i}", [128, 8 * 512], BF16))
            for i in range(2)]
    Qs = esA.enter_context(nc.sbuf_tensor("Qs", [128, 4096], BF16))
    Ks = esA.enter_context(nc.sbuf_tensor("Ks", [128, 4096], BF16))
    Q2s = esA.enter_context(nc.sbuf_tensor("Q2s", [128, 4096], BF16))
    VTs = esA.enter_context(nc.sbuf_tensor("VTs", [128, 4096], BF16))
    E2L = esA.enter_context(nc.sbuf_tensor("E2L", [64, 2048], BF16))
    E2H = esA.enter_context(nc.sbuf_tensor("E2H", [64, 2048], BF16))
    VsLO = esA.enter_context(nc.sbuf_tensor("VsLO", [64, 16 * 65], BF16))
    VsHI = esA.enter_context(nc.sbuf_tensor("VsHI", [64, 16 * 65], BF16))

    p_qkv = [esA.enter_context(nc.psum_tensor(f"p_qkv{i}", [128, 512], F32)) for i in range(2)]
    p_scL = esA.enter_context(nc.psum_tensor("p_scL", [64, 512], F32))
    p_scH = esA.enter_context(nc.psum_tensor("p_scH", [64, 512], F32))
    p_av = esA.enter_context(nc.psum_tensor("p_av", [128, 260], F32))
    p_tr = esA.enter_context(nc.psum_tensor("p_tr", [64, 256], F32))
    p_st = esA.enter_context(nc.psum_tensor("p_st", [8, 32], F32))
    p_bc = esA.enter_context(nc.psum_tensor("p_bc", [128, 32], F32))

    for dst, src in [(sb_bq, d_bq), (sb_bk, d_bk), (sb_bv, d_bv), (sb_b1, d_b1),
                     (sb_b2, d_b2), (sb_mask, d_mask), (sb_sel8, d_sel8),
                     (sb_sel8t, d_sel8t), (sb_id16, d_id16), (sb_idf, d_idf),
                     (sb_eps, d_eps), (w128, d_cos)]:
        S.emit("sync", lambda e, dst=dst, src=src: e.dma_start(dst[:, :], src.ap()))
    m_consts = S.marker("sync")
    S.emit("sync", lambda e: e.dma_start(
        bass.AP(u, 0, [[NT * 1024, 128], [1024, NT], [1, 1024]]),
        bass.AP(d_xs, 0, [[1024, 128], [128 * 1024, NT], [1, 1024]])))
    m_xs = S.marker("sync")

    for t, d_w in [("q", d_wq), ("k", d_wk), ("v", d_wv)]:
        S.emit("gpsimd", lambda e, t=t, d_w=d_w: e.dma_start(
            bass.AP(w16[t], 0, [[8 * D, 128], [D, 8], [1, D]]),
            bass.AP(d_w, 0, [[D, 128], [128 * D, 8], [1, D]]),
        ))
    m_wqkv = S.marker("gpsimd")

    S.emit("vector", lambda e: e.tensor_mul(w128[:, :], w128[:, :], w128[:, :]),
           deps=[m_consts])
    m_usq_free = ("vector", 0)
    for h in range(2):
        S.emit("sync", lambda e, h=h: e.dma_start(
            usq[:, :], bass.AP(d_sin, 1024 * h, [[T, 128], [1, 1024]])),
            deps=[m_usq_free])
        m_sinh = S.marker("sync")
        S.emit("vector", lambda e: e.tensor_mul(usq[:, :], usq[:, :], usq[:, :]),
               deps=[m_sinh])
        S.emit("vector", lambda e, h=h: e.tensor_add(
            w128[:, 1024 * h: 1024 * h + 1024],
            w128[:, 1024 * h: 1024 * h + 1024], usq[:, :]))
        m_usq_free = S.marker("vector")
    m_w128 = S.marker("vector")

    S.emit("vector", lambda e: e.memset(bass.AP(VsLO, 64, [[16 * 65, 64], [65, 16]]), 1.0))
    S.emit("vector", lambda e: e.memset(bass.AP(VsHI, 64, [[16 * 65, 64], [65, 16]]), 1.0))
    m_vsones = S.marker("vector")

    m_xt_used = [("tensor", 0), ("tensor", 0)]
    m_stacks_used = ("tensor", 0)
    m_evac_done_v = ("vector", 0)
    m_evac_done_s = ("scalar", 0)
    m_psum_used = [("vector", 0), ("vector", 0)]
    m_q2_dma = ("sync", 0)
    m_e2_used = ("tensor", 0)
    m_vs_used = ("tensor", 0)
    m_av_used = ("vector", 0)
    m_sc_used = ("scalar", 0)
    m_tr_used = {"vector": ("vector", 0), "scalar": ("scalar", 0)}
    sync_q_prev = ("sync", 0)

    for s in range(NSLAB):
        buf = s % 2
        S.emit("gpsimd", lambda e, s=s, buf=buf: e.dma_start(
            bass.AP(xt16[buf], 0, [[8 * 512, 128], [512, 8], [1, 512]]),
            bass.AP(d_xt, 512 * s, [[T, 128], [128 * T, 8], [1, 512]]),
        ), deps=[m_xt_used[buf]])
        m_xt = S.marker("gpsimd")

        pp = 0
        for tname in "qkv":
            for ch in range(8):
                ps = p_qkv[pp % 2]
                dep_evac = m_psum_used[pp % 2]
                for kc in range(8):
                    S.emit("tensor", lambda e, tname=tname, ch=ch, kc=kc, ps=ps, buf=buf:
                           e.matmul(ps[:, :],
                                    w16[tname][:, D * kc + 128 * ch: D * kc + 128 * ch + 128],
                                    xt16[buf][:, 512 * kc: 512 * kc + 512],
                                    start=(kc == 0), stop=(kc == 7)),
                           deps=([m_wqkv, m_xt, dep_evac] if kc == 0 else []),
                           inc=(kc == 7))
                m_mm = S.marker("tensor")
                for half in range(2):
                    rows = slice(64 * half, 64 * half + 64)
                    ps_ap = lambda ps=ps, half=half: bass.AP(
                        ps, half * 64 * 512, [[512, 64], [128, 4], [8, 16], [1, 8]])
                    if tname == "k":
                        S.emit("vector", lambda e, ps_ap=ps_ap, rows=rows, ch=ch, half=half, s=s:
                               e.scalar_tensor_tensor(
                                   bass.AP(Ks, half * (64 * 4096) + ch,
                                           [[4096, 64], [1024, 4], [64, 16], [8, 8]]),
                                   ps_ap(),
                                   sb_bk[rows, ch: ch + 1],
                                   bass.AP(w128, half * 64 * T + 512 * s,
                                           [[T, 64], [128, 4], [8, 16], [1, 8]]),
                                   ALU.add, ALU.mult),
                               deps=[m_mm, m_w128, m_stacks_used, m_q2_dma, m_consts])
                    else:
                        dst = Qs if tname == "q" else VTs
                        bt = sb_bq if tname == "q" else sb_bv
                        S.emit("scalar", lambda e, ps_ap=ps_ap, rows=rows, dst=dst, ch=ch, half=half, bt=bt:
                               e.activation(
                                   bass.AP(dst, half * (64 * 4096) + ch,
                                           [[4096, 64], [1024, 4], [64, 16], [8, 8]]),
                                   ps_ap(), AF.Identity, bias=bt[rows, ch: ch + 1]),
                               deps=[m_mm, m_stacks_used, m_q2_dma, m_consts])
                if tname == "k":
                    m_evac_done_v = S.marker("vector")
                    m_psum_used[pp % 2] = m_evac_done_v
                else:
                    m_evac_done_s = S.marker("scalar")
                    m_psum_used[pp % 2] = m_evac_done_s
                pp += 1
        m_qevac = m_evac_done_s
        m_kevac = m_evac_done_v

        S.wait("sync", sync_q_prev)
        S.emit("sync", lambda e: e.dma_start(Q2s[0:64, :], Qs[64:128, :]),
               deps=[m_qevac, m_stacks_used])
        S.emit("sync", lambda e: e.dma_start(Q2s[64:128, :], Qs[0:64, :]))
        m_q2_dma = S.marker("sync")
        sync_q_prev = m_q2_dma

        for ti in range(4):
            tile = 4 * s + ti
            w0 = 1024 * ti

            for hi in range(2):
                vsd = VsLO if hi == 0 else VsHI
                for gb in range(4):
                    for gg in range(4):
                        g = 4 * gb + gg
                        S.emit("tensor", lambda e, hi=hi, g=g, gg=gg, w0=w0:
                               e.matmul(p_tr[:, 64 * gg: 64 * gg + 64],
                                        VTs[64 * hi: 64 * hi + 64, w0 + 64 * g: w0 + 64 * g + 64],
                                        sb_id16[64 * hi: 64 * hi + 64, :],
                                        start=True, stop=True),
                               deps=([m_kevac, m_qevac, m_consts,
                                      m_tr_used["vector"], m_vs_used] if gg == 0 else []),
                               inc=(gg == 3))
                    m_trp = S.marker("tensor")
                    S.emit("vector", lambda e, vsd=vsd, gb=gb:
                           e.tensor_copy(
                               bass.AP(vsd, 65 * 4 * gb, [[16 * 65, 64], [65, 4], [1, 64]]),
                               p_tr[:, 0:256]),
                           deps=[m_trp, m_vsones, m_av_used])
                    m_tr_used["vector"] = S.marker("vector")

            m_vs_ready = S.marker("vector")

            for b in range(4):
                for q4 in range(4):
                    g = 4 * b + q4
                    wg = w0 + 64 * g
                    S.emit("tensor", lambda e, wg=wg, q4=q4:
                           e.matmul(p_scL[:, 128 * q4: 128 * q4 + 64],
                                    Ks[0:64, wg: wg + 64], Qs[0:64, wg: wg + 64],
                                    start=True, stop=True),
                           deps=([m_q2_dma, m_sc_used] if q4 == 0 else []), inc=False)
                    S.emit("tensor", lambda e, wg=wg, q4=q4:
                           e.matmul(p_scL[:, 128 * q4 + 64: 128 * q4 + 128],
                                    Ks[0:64, wg: wg + 64], Q2s[0:64, wg: wg + 64],
                                    start=True, stop=True), inc=False)
                    S.emit("tensor", lambda e, wg=wg, q4=q4:
                           e.matmul(p_scH[:, 128 * q4: 128 * q4 + 64],
                                    Ks[64:128, wg: wg + 64], Q2s[64:128, wg: wg + 64],
                                    start=True, stop=True), inc=False)
                    S.emit("tensor", lambda e, wg=wg, q4=q4:
                           e.matmul(p_scH[:, 128 * q4 + 64: 128 * q4 + 128],
                                    Ks[64:128, wg: wg + 64], Qs[64:128, wg: wg + 64],
                                    start=True, stop=True), inc=(q4 == 3))
                m_scmm = S.marker("tensor")
                S.emit("scalar", lambda e, b=b: e.activation(
                    E2L[:, 512 * b: 512 * b + 512], p_scL[:, :], AF.Exp, scale=0.125),
                    deps=[m_scmm, m_e2_used])
                S.emit("scalar", lambda e, b=b: e.activation(
                    E2H[:, 512 * b: 512 * b + 512], p_scH[:, :], AF.Exp, scale=0.125),
                    )
                m_sc_used = S.marker("scalar")
                S.emit("vector", lambda e, b=b: e.tensor_tensor(
                    E2L[:, 512 * b: 512 * b + 512], E2L[:, 512 * b: 512 * b + 512],
                    bass.AP(sb_mask, 0, [[128, 64], [0, 4], [1, 128]]), ALU.mult),
                    deps=[m_sc_used, m_consts])
                S.emit("vector", lambda e, b=b: e.tensor_tensor(
                    E2H[:, 512 * b: 512 * b + 512], E2H[:, 512 * b: 512 * b + 512],
                    bass.AP(sb_mask, 0, [[128, 64], [0, 4], [1, 128]]), ALU.mult))
            m_mask = S.marker("vector")

            for b in range(4):
                for q4 in range(4):
                    g = 4 * b + q4
                    S.emit("tensor", lambda e, g=g, q4=q4:
                           e.matmul(p_av[:, 65 * q4: 65 * q4 + 65],
                                    E2L[:, 128 * g: 128 * g + 128],
                                    VsLO[:, 65 * g: 65 * g + 65],
                                    start=True, stop=False),
                           deps=([m_mask, m_vs_ready, m_av_used] if q4 == 0 else []),
                           inc=False)
                    S.emit("tensor", lambda e, g=g, q4=q4:
                           e.matmul(p_av[:, 65 * q4: 65 * q4 + 65],
                                    E2H[:, 128 * g: 128 * g + 128],
                                    VsHI[:, 65 * g: 65 * g + 65],
                                    start=False, stop=True), inc=(q4 == 3))
                m_avmm = S.marker("tensor")
                S.emit("vector", lambda e: e.reciprocal(
                    zrec[:, 0:4], bass.AP(p_av, 64, [[260, 128], [65, 4]])),
                    deps=[m_avmm, m_xs])
                for q4 in range(4):
                    g = 4 * b + q4
                    S.emit("vector", lambda e, g=g, q4=q4, tile=tile:
                           e.scalar_tensor_tensor(
                               u[:, 1024 * tile + 64 * g: 1024 * tile + 64 * g + 64],
                               p_av[:, 65 * q4: 65 * q4 + 64],
                               zrec[:, q4: q4 + 1],
                               u[:, 1024 * tile + 64 * g: 1024 * tile + 64 * g + 64],
                               ALU.mult, ALU.add))
                m_av_used = S.marker("vector")
            m_e2_used = m_avmm

            S.emit("vector", lambda e, tile=tile: e.reduce_sum(
                lnS[:, 0:16],
                bass.AP(u, 1024 * tile, [[NT * 1024, 128], [64, 16], [1, 64]]),
                mybir.AxisListType.X))
            m_s1 = S.marker("vector")
            S.emit("scalar", lambda e, tile=tile: e.activation(
                usq[:, :], u[:, 1024 * tile: 1024 * tile + 1024], AF.Square),
                deps=[m_av_used])
            m_usq = S.marker("scalar")
            S.emit("vector", lambda e: e.reduce_sum(
                lnS[:, 16:32], bass.AP(usq, 0, [[1024, 128], [64, 16], [1, 64]]),
                mybir.AxisListType.X), deps=[m_usq])
            m_s2 = S.marker("vector")
            S.emit("tensor", lambda e: e.matmul(
                p_st[:, :], sb_sel8[:, :], lnS[:, :], start=True, stop=True),
                deps=[m_s2, m_consts])
            m_stmm = S.marker("tensor")
            S.emit("vector", lambda e: e.tensor_scalar_mul(lnT[:, :], p_st[:, :], 1.0 / 1024.0),
                   deps=[m_stmm])
            S.emit("vector", lambda e: e.tensor_mul(lnM[:, :], lnT[:, 0:16], lnT[:, 0:16]))
            S.emit("vector", lambda e: e.tensor_sub(lnM[:, :], lnT[:, 16:32], lnM[:, :]))
            m_var = S.marker("vector")
            S.emit("scalar", lambda e: e.activation(
                lnM[:, :], lnM[:, :], AF.Sqrt, bias=sb_eps[0:8, 0:1]), deps=[m_var, m_consts])
            m_std = S.marker("scalar")
            S.emit("vector", lambda e: e.reciprocal(lnT[:, 16:32], lnM[:, :]),
                   deps=[m_std])
            S.emit("vector", lambda e: e.tensor_mul(lnT[:, 0:16], lnT[:, 0:16], lnT[:, 16:32]))
            m_mrs = S.marker("vector")
            S.emit("tensor", lambda e: e.matmul(
                p_bc[:, :], sb_sel8t[:, :], lnT[:, :], start=True, stop=True),
                deps=[m_mrs, m_consts])
            m_bcmm = S.marker("tensor")
            S.emit("vector", lambda e: e.tensor_copy(bcS[:, :], p_bc[:, :]), deps=[m_bcmm])
            S.emit("vector", lambda e, tile=tile: e.tensor_tensor(
                bass.AP(u, 1024 * tile, [[NT * 1024, 128], [64, 16], [1, 64]]),
                bass.AP(u, 1024 * tile, [[NT * 1024, 128], [64, 16], [1, 64]]),
                bass.AP(bcS, 16, [[32, 128], [1, 16], [0, 64]]), ALU.mult))
            S.emit("vector", lambda e, tile=tile: e.tensor_tensor(
                bass.AP(u, 1024 * tile, [[NT * 1024, 128], [64, 16], [1, 64]]),
                bass.AP(u, 1024 * tile, [[NT * 1024, 128], [64, 16], [1, 64]]),
                bass.AP(bcS, 0, [[32, 128], [1, 16], [0, 64]]), ALU.subtract))

        m_stacks_used = S.marker("tensor")
        m_xt_used[buf] = S.marker("tensor")

    m_attn_pe = S.marker("tensor")
    m_attn_v = S.marker("vector")
    m_attn_s = S.marker("scalar")

    esA.close()

    if DEBUG_A:
        total_sync = S.cnt["sync"]
        if total_sync:
            S.ops["sync"].append(lambda e, v=16 * total_sync: e.wait_ge(S.sems["sync"], v))
        with nc.Block() as block:
            @block.sync
            def _(eng):
                for fn in S.ops["sync"]:
                    fn(eng)

            @block.gpsimd
            def _(eng):
                for fn in S.ops["gpsimd"]:
                    fn(eng)

            @block.tensor
            def _(eng):
                for fn in S.ops["tensor"]:
                    fn(eng)

            @block.vector
            def _(eng):
                for fn in S.ops["vector"]:
                    fn(eng)

            @block.scalar
            def _(eng):
                for fn in S.ops["scalar"]:
                    fn(eng)
        esA.close()
        es_top.close()
        return nc
    esB = ExitStack()
    u1p16 = esB.enter_context(nc.sbuf_tensor("u1p16", [64, 16 * RANK], BF16))
    v116 = esB.enter_context(nc.sbuf_tensor("v116", [128, 2 * DFF], BF16))
    u216 = esB.enter_context(nc.sbuf_tensor("u216", [128, 32 * RANK], BF16))
    v2p16 = esB.enter_context(nc.sbuf_tensor("v2p16", [128, 2 * 1024], BF16))
    y16 = esB.enter_context(nc.sbuf_tensor("y16", [128, 2048], BF16))
    YTl = esB.enter_context(nc.sbuf_tensor("YTl", [64, 2048], BF16))
    YTh = esB.enter_context(nc.sbuf_tensor("YTh", [64, 2048], BF16))
    P16 = esB.enter_context(nc.sbuf_tensor("P16", [128, 512], BF16))
    H16 = esB.enter_context(nc.sbuf_tensor("H16", [128, 32 * 256], BF16))
    Q2T = esB.enter_context(nc.sbuf_tensor("Q2T", [128, 512], BF16))
    FTs = esB.enter_context(nc.sbuf_tensor("FTs", [128, 2048], F32))

    p_a = esB.enter_context(nc.psum_tensor("p_a", [128, 256], F32))
    p_b = esB.enter_context(nc.psum_tensor("p_b", [128, 256], F32))
    p_c = esB.enter_context(nc.psum_tensor("p_c", [128, 256], F32))
    p_d = esB.enter_context(nc.psum_tensor("p_d", [128, 256], F32))
    p_fs = esB.enter_context(nc.psum_tensor("p_fs", [128, 256], F32))
    p_yt = esB.enter_context(nc.psum_tensor("p_yt", [64, 256], F32))
    p_st2 = esB.enter_context(nc.psum_tensor("p_st2", [8, 32], F32))
    p_bc2 = esB.enter_context(nc.psum_tensor("p_bc2", [128, 32], F32))

    gate = [m_attn_pe, m_attn_v, m_attn_s]
    S.emit("gpsimd", lambda e: e.dma_start(u1p16[:, :], d_u1.ap()), deps=gate)
    S.emit("gpsimd", lambda e: e.dma_start(
        bass.AP(v116, 0, [[2 * DFF, 128], [DFF, 2], [1, DFF]]),
        bass.AP(d_v1, 0, [[DFF, 128], [128 * DFF, 2], [1, DFF]])))
    S.emit("gpsimd", lambda e: e.dma_start(
        bass.AP(u216, 0, [[32 * RANK, 128], [RANK, 32], [1, RANK]]),
        bass.AP(d_u2, 0, [[RANK, 128], [128 * RANK, 32], [1, RANK]])))
    S.emit("gpsimd", lambda e: e.dma_start(
        bass.AP(v2p16, 0, [[2 * 1024, 128], [1024, 2], [1, 1024]]),
        bass.AP(d_v2, 0, [[1024, 128], [128 * 1024, 2], [1, 1024]])))
    m_wffn = S.marker("gpsimd")

    m_y16_used = ("tensor", 0)
    m_yt_used = ("vector", 0)
    m_yts_used = ("tensor", 0)
    m_p16_used = ("tensor", 0)
    m_h16_used = ("tensor", 0)
    m_q2t_used = ("tensor", 0)
    m_ft_used = ("tensor", 0)
    m_pab_used = ("vector", 0)
    m_pcd_used = ("scalar", 0)
    m_fs_used = ("vector", 0)
    m_outS_used = [("sync", 0), ("sync", 0)]
    n_store = 0

    for f in range(FSLAB):
        c0 = 2048 * f
        S.emit("vector", lambda e, c0=c0: e.tensor_copy(
            y16[:, :], u[:, c0: c0 + 2048]), deps=[m_y16_used])
        m_y16 = S.marker("vector")

        for hi in range(2):
            ytd = YTl if hi == 0 else YTh
            for blk in range(8):
                for gg in range(4):
                    gidx = 4 * blk + gg
                    S.emit("tensor", lambda e, hi=hi, gidx=gidx, gg=gg:
                           e.matmul(p_yt[:, 64 * gg: 64 * gg + 64],
                                    y16[64 * hi: 64 * hi + 64, 64 * gidx: 64 * gidx + 64],
                                    sb_id16[64 * hi: 64 * hi + 64, :],
                                    start=True, stop=True),
                           deps=([m_y16, m_yt_used, m_consts] if gg == 0 else []), inc=(gg == 3))
                m_ytp = S.marker("tensor")
                S.emit("vector", lambda e, ytd=ytd, blk=blk: e.tensor_copy(
                    ytd[:, 256 * blk: 256 * blk + 256], p_yt[:, :]),
                    deps=[m_ytp, m_yts_used])
                m_yt_used = S.marker("vector")
        m_yt_ready = S.marker("vector")

        for rh in range(2):
            ps = p_a if rh == 0 else p_b
            for hc in range(16):
                hi, ch = hc // 8, hc % 8
                yts = YTl if hi == 0 else YTh
                S.emit("tensor", lambda e, rh=rh, hi=hi, ch=ch, ps=ps, yts=yts, hc=hc:
                       e.matmul(ps[:, :],
                                u1p16[:, RANK * (ch + 8 * hi) + 128 * rh:
                                      RANK * (ch + 8 * hi) + 128 * rh + 128],
                                bass.AP(yts, ch, [[2048, 64], [64, 32], [8, 8]]),
                                start=(hc == 0), stop=(hc == 15)),
                       deps=([m_yt_ready, m_wffn, m_pab_used, m_consts] if hc == 0 else []),
                       inc=(hc == 15))
            m_pmm = S.marker("tensor")
            S.emit("vector", lambda e, rh=rh, ps=ps: e.tensor_copy(
                P16[:, 256 * rh: 256 * rh + 256], ps[:, :]), deps=[m_pmm, m_p16_used])
        m_pab_used = S.marker("vector")
        m_p16 = S.marker("vector")

        for ffc in range(32):
            ps = p_c if ffc % 2 == 0 else p_d
            for rc in range(2):
                S.emit("tensor", lambda e, ffc=ffc, rc=rc, ps=ps:
                       e.matmul(ps[:, :],
                                v116[:, DFF * rc + 128 * ffc: DFF * rc + 128 * ffc + 128],
                                P16[:, 256 * rc: 256 * rc + 256],
                                start=(rc == 0), stop=(rc == 1)),
                       deps=([m_p16, m_pcd_used] if rc == 0 else []), inc=(rc == 1))
            m_hmm = S.marker("tensor")
            S.emit("scalar", lambda e, ffc=ffc, ps=ps: e.activation(
                H16[:, 256 * ffc: 256 * ffc + 256], ps[:, :], AF.Gelu_apprx_tanh,
                bias=sb_b1[:, ffc: ffc + 1]), deps=[m_hmm, m_h16_used, m_consts])
            if ffc % 2 == 1:
                m_pcd_used = S.marker("scalar")
        m_h16 = S.marker("scalar")

        for rh in range(2):
            ps = p_a if rh == 0 else p_b
            for ffc in range(32):
                S.emit("tensor", lambda e, rh=rh, ffc=ffc, ps=ps:
                       e.matmul(ps[:, :],
                                u216[:, RANK * ffc + 128 * rh: RANK * ffc + 128 * rh + 128],
                                H16[:, 256 * ffc: 256 * ffc + 256],
                                start=(ffc == 0), stop=(ffc == 31)),
                       deps=([m_h16, m_pab_used] if ffc == 0 else []), inc=(ffc == 31))
            m_qmm = S.marker("tensor")
            S.emit("vector", lambda e, rh=rh, ps=ps: e.tensor_copy(
                Q2T[:, 256 * rh: 256 * rh + 256], ps[:, :]), deps=[m_qmm, m_q2t_used])
        m_pab_used = S.marker("vector")
        m_q2t = S.marker("vector")

        for ch in range(8):
            ps = p_c if ch % 2 == 0 else p_d
            for rc in range(2):
                S.emit("tensor", lambda e, ch=ch, rc=rc, ps=ps:
                       e.matmul(ps[:, :],
                                v2p16[:, 1024 * rc + 128 * ch: 1024 * rc + 128 * ch + 128],
                                Q2T[:, 256 * rc: 256 * rc + 256],
                                start=(rc == 0), stop=(rc == 1)),
                       deps=([m_q2t, m_pcd_used] if rc == 0 else []), inc=(rc == 1))
            m_fmm = S.marker("tensor")
            for half in range(2):
                rows = slice(64 * half, 64 * half + 64)
                S.emit("scalar", lambda e, ch=ch, half=half, rows=rows, ps=ps:
                       e.activation(
                           bass.AP(FTs, half * (64 * 2048) + ch,
                                   [[2048, 64], [1024, 2], [64, 16], [8, 8]]),
                           ps[rows, :], AF.Identity, bias=sb_b2[rows, ch: ch + 1]),
                       deps=[m_fmm, m_ft_used, m_consts])
            if ch % 2 == 1:
                m_pcd_used = S.marker("scalar")
        m_ft = S.marker("scalar")

        for tp in range(2):
            tile = 2 * f + tp
            for gb in range(4):
                for hi in range(2):
                    for gg in range(4):
                        g = 4 * gb + gg
                        S.emit("tensor", lambda e, hi=hi, g=g, gg=gg, tp=tp:
                               e.matmul(p_fs[64 * hi: 64 * hi + 64, 64 * gg: 64 * gg + 64],
                                        FTs[64 * hi: 64 * hi + 64,
                                            1024 * tp + 64 * g: 1024 * tp + 64 * g + 64],
                                        sb_idf[64 * hi: 64 * hi + 64, :],
                                        start=True, stop=True),
                               deps=([m_ft, m_fs_used, m_consts] if (hi == 0 and gg == 0) else []),
                               inc=(hi == 1 and gg == 3))
                m_fsmm = S.marker("tensor")
                S.emit("vector", lambda e, tile=tile, gb=gb: e.scalar_tensor_tensor(
                    u[:, 1024 * tile + 256 * gb: 1024 * tile + 256 * gb + 256],
                    p_fs[:, :], 1.0,
                    u[:, 1024 * tile + 256 * gb: 1024 * tile + 256 * gb + 256],
                    ALU.mult, ALU.add), deps=[m_fsmm])
                m_fs_used = S.marker("vector")

            S.emit("vector", lambda e, tile=tile: e.reduce_sum(
                lnS[:, 0:16],
                bass.AP(u, 1024 * tile, [[NT * 1024, 128], [64, 16], [1, 64]]),
                mybir.AxisListType.X))
            S.emit("scalar", lambda e, tile=tile: e.activation(
                usq[:, :], u[:, 1024 * tile: 1024 * tile + 1024], AF.Square),
                deps=[S.marker("vector")])
            m_usq = S.marker("scalar")
            S.emit("vector", lambda e: e.reduce_sum(
                lnS[:, 16:32], bass.AP(usq, 0, [[1024, 128], [64, 16], [1, 64]]),
                mybir.AxisListType.X), deps=[m_usq])
            m_s2 = S.marker("vector")
            S.emit("tensor", lambda e: e.matmul(
                p_st2[:, :], sb_sel8[:, :], lnS[:, :], start=True, stop=True),
                deps=[m_s2, m_consts])
            m_stmm = S.marker("tensor")
            S.emit("vector", lambda e: e.tensor_scalar_mul(lnT[:, :], p_st2[:, :], 1.0 / 1024.0),
                   deps=[m_stmm])
            S.emit("vector", lambda e: e.tensor_mul(lnM[:, :], lnT[:, 0:16], lnT[:, 0:16]))
            S.emit("vector", lambda e: e.tensor_sub(lnM[:, :], lnT[:, 16:32], lnM[:, :]))
            m_var = S.marker("vector")
            S.emit("scalar", lambda e: e.activation(
                lnM[:, :], lnM[:, :], AF.Sqrt, bias=sb_eps[0:8, 0:1]), deps=[m_var, m_consts])
            m_std = S.marker("scalar")
            S.emit("vector", lambda e: e.reciprocal(lnT[:, 16:32], lnM[:, :]),
                   deps=[m_std])
            S.emit("vector", lambda e: e.tensor_mul(lnT[:, 0:16], lnT[:, 0:16], lnT[:, 16:32]))
            m_mrs = S.marker("vector")
            S.emit("tensor", lambda e: e.matmul(
                p_bc2[:, :], sb_sel8t[:, :], lnT[:, :], start=True, stop=True),
                deps=[m_mrs, m_consts])
            m_bcmm = S.marker("tensor")
            S.emit("vector", lambda e: e.tensor_copy(bcS[:, :], p_bc2[:, :]), deps=[m_bcmm])
            ob = tile % 2
            S.emit("vector", lambda e, tile=tile, ob=ob: e.tensor_tensor(
                bass.AP(outS, 1024 * ob, [[2048, 128], [64, 16], [1, 64]]),
                bass.AP(u, 1024 * tile, [[NT * 1024, 128], [64, 16], [1, 64]]),
                bass.AP(bcS, 16, [[32, 128], [1, 16], [0, 64]]), ALU.mult),
                deps=[m_outS_used[ob]])
            S.emit("vector", lambda e, ob=ob: e.tensor_tensor(
                bass.AP(outS, 1024 * ob, [[2048, 128], [64, 16], [1, 64]]),
                bass.AP(outS, 1024 * ob, [[2048, 128], [64, 16], [1, 64]]),
                bass.AP(bcS, 0, [[32, 128], [1, 16], [0, 64]]), ALU.subtract))
            m_norm = S.marker("vector")
            S.emit("sync", lambda e, tile=tile, ob=ob: e.dma_start(
                bass.AP(d_out, 128 * 1024 * tile, [[1024, 128], [1, 1024]]),
                outS[:, 1024 * ob: 1024 * ob + 1024]), deps=[m_norm])
            n_store += 1
            m_outS_used[ob] = S.marker("sync")

    total_sync = S.cnt["sync"]
    S.ops["sync"].append(lambda e, v=16 * total_sync: e.wait_ge(S.sems["sync"], v))

    with nc.Block() as block:
        @block.sync
        def _(eng):
            for fn in S.ops["sync"]:
                fn(eng)

        @block.gpsimd
        def _(eng):
            for fn in S.ops["gpsimd"]:
                fn(eng)

        @block.tensor
        def _(eng):
            for fn in S.ops["tensor"]:
                fn(eng)

        @block.vector
        def _(eng):
            for fn in S.ops["vector"]:
                fn(eng)

        @block.scalar
        def _(eng):
            for fn in S.ops["scalar"]:
                fn(eng)

    esB.close()
    es_top.close()
    return nc



def _rope_tables():
    freqs = (10000.0 ** (-(np.arange(0, DK, 2, dtype=np.float64) / D))).astype(np.float32)
    ang = (np.arange(NCORES * T, dtype=np.float32)[:, None] * freqs[None, :]).astype(np.float32)
    libm = ctypes.CDLL(ctypes.util.find_library("m"))
    libm.cosf.restype = ctypes.c_float
    libm.cosf.argtypes = [ctypes.c_float]
    libm.sinf.restype = ctypes.c_float
    libm.sinf.argtypes = [ctypes.c_float]
    flat = ang.ravel()
    cos = np.fromiter((libm.cosf(float(v)) for v in flat), np.float32, flat.size).reshape(ang.shape)
    sin = np.fromiter((libm.sinf(float(v)) for v in flat), np.float32, flat.size).reshape(ang.shape)
    return cos, sin


_NC_CACHE = {}


def kernel(L0_attn_x, L0_attn_Wq, L0_attn_bq, L0_attn_Wk, L0_attn_bk,
           L0_attn_Wv, L0_attn_bv, L0_ffn_U1, L0_ffn_V1, L0_ffn_b1,
           L0_ffn_U2, L0_ffn_V2, L0_ffn_b2):
    x = np.ascontiguousarray(np.asarray(L0_attn_x, dtype=np.float32).reshape(NCORES * T, D))

    perm = np.concatenate([
        np.concatenate([np.arange(64 * c, 64 * c + 64),
                        np.arange(64 * (c + 8), 64 * (c + 8) + 64)])
        for c in range(8)])
    wq = np.ascontiguousarray(np.asarray(L0_attn_Wq, np.float32)[:, perm])
    wk = np.ascontiguousarray(np.asarray(L0_attn_Wk, np.float32)[:, perm])
    wv = np.ascontiguousarray(np.asarray(L0_attn_Wv, np.float32)[:, perm])
    v2p = np.ascontiguousarray(np.asarray(L0_ffn_V2, np.float32)[:, perm])

    def bias_chunks(b):
        return np.ascontiguousarray(np.asarray(b, np.float32)[perm].reshape(8, 128).T)

    bqp, bkp, bvp = (bias_chunks(b) for b in (L0_attn_bq, L0_attn_bk, L0_attn_bv))
    b2p = bias_chunks(L0_ffn_b2)
    u1p = np.ascontiguousarray(
        np.asarray(L0_ffn_U1, np.float32).reshape(16, 64, RANK).transpose(1, 0, 2).reshape(64, 16 * RANK))
    v1r = np.ascontiguousarray(np.asarray(L0_ffn_V1, np.float32))
    u2r = np.ascontiguousarray(np.asarray(L0_ffn_U2, np.float32))
    b1p = np.ascontiguousarray(np.asarray(L0_ffn_b1, np.float32).reshape(32, 128).T)

    cos, sin = _rope_tables()

    mask01 = np.zeros((64, 128), np.float32)
    for r in range(64):
        for n in range(128):
            if (r >> 3) == ((n % 64) >> 3):
                mask01[r, n] = 1.0
    mask01 = mask01.astype(ml_dtypes.bfloat16)
    sel8 = np.zeros((128, 8), np.float32)
    for row in range(128):
        sel8[row, (row % 64) >> 3] = 1.0
    sel8t = np.ascontiguousarray(sel8.T)
    id16 = np.concatenate([np.eye(64, dtype=ml_dtypes.bfloat16)] * 2, 0)
    idf = np.concatenate([np.eye(64, dtype=np.float32)] * 2, 0)
    epsc = np.full((128, 1), 1e-5, np.float32)

    if "nc" not in _NC_CACHE:
        _NC_CACHE["nc"] = build_nc()
    nc = _NC_CACHE["nc"]

    in_maps = []
    for c in range(NCORES):
        xc = x[T * c: T * (c + 1)]
        xt = np.ascontiguousarray(xc.T)
        v = xc.reshape(NT, 16, 8, 2, 8, 64)
        xs = np.ascontiguousarray(v.transpose(0, 3, 2, 4, 1, 5).reshape(T, D))
        cos128 = np.ascontiguousarray(
            np.tile(np.repeat(cos[T * c: T * (c + 1)].T, 2, axis=0), (2, 1)))
        sin128 = np.ascontiguousarray(
            np.tile(np.repeat(sin[T * c: T * (c + 1)].T, 2, axis=0), (2, 1)))
        in_maps.append({
            "xt": xt, "xs": xs, "wq": wq, "wk": wk, "wv": wv,
            "bqp": bqp, "bkp": bkp, "bvp": bvp,
            "u1p": u1p, "v1r": v1r, "b1p": b1p, "u2r": u2r, "v2p": v2p, "b2p": b2p,
            "cos128": cos128, "sin128": sin128,
            "mask01": mask01, "sel8": sel8, "sel8t": sel8t,
            "id16": id16, "idf": idf, "epsc": epsc,
        })

    res = run_bass_kernel_spmd(nc, in_maps, core_ids=list(range(NCORES)))

    out = np.empty((NCORES * T, D), np.float32)
    for c in range(NCORES):
        o = np.asarray(res.results[c]["outs"], np.float32)
        v = o.reshape(NT, 2, 8, 8, 16, 64)
        out[T * c: T * (c + 1)] = v.transpose(0, 4, 2, 1, 3, 5).reshape(T, D)
    return out.reshape(1, NCORES * T, D)


# version 04
# speedup vs baseline: 1.0948x; 1.0948x over previous
import ctypes
import ctypes.util
from contextlib import ExitStack

import numpy as np
import ml_dtypes

import concourse.bass as bass
import concourse.mybir as mybir
from concourse.bass_utils import run_bass_kernel_spmd

F32 = mybir.dt.float32
BF16 = mybir.dt.bfloat16
AF = mybir.ActivationFunctionType
ALU = mybir.AluOpType

NCORES = 8
T = 2048
NT = 16
D = 1024
NH = 16
DK = 64
RANK = 256
DFF = 4096
NSLAB = 4
FSLAB = 8

ENGS = ("sync", "gpsimd", "tensor", "vector", "scalar")
DEBUG_A = False


class Sched:

    def __init__(self, nc, es):
        self.nc = nc
        self.ops = {e: [] for e in ENGS}
        self.cnt = {e: 0 for e in ENGS}
        self.hw = {e: {s: 0 for s in ENGS} for e in ENGS}
        self.sems = {e: es.enter_context(nc.semaphore(f"sem_{e}")) for e in ENGS}

    def mult(self, e):
        return 16 if e in ("sync", "gpsimd") else 1

    def throttle(self, eng):
        v = self.cnt[eng] * self.mult(eng)
        if v > 0:
            sem = self.sems[eng]
            self.ops[eng].append(lambda e, sem=sem, v=v: e.wait_ge(sem, v))

    def wait(self, eng, dep):
        src, val = dep
        if src == eng or val <= 0:
            return
        if self.hw[eng][src] >= val:
            return
        self.hw[eng][src] = val
        sem = self.sems[src]
        v = val * self.mult(src)
        self.ops[eng].append(lambda e, sem=sem, v=v: e.wait_ge(sem, v))

    def emit(self, eng, fn, deps=(), inc=True):
        for d in deps:
            self.wait(eng, d)
        if eng in ("sync", "gpsimd"):
            self.throttle(eng)
        elif inc and eng != "tensor" and self.cnt[eng] > 0:
            sem, v = self.sems[eng], self.cnt[eng]
            self.ops[eng].append(lambda e, sem=sem, v=v: e.wait_ge(sem, v))
        if inc:
            self.cnt[eng] += 1
            sem, m = self.sems[eng], self.mult(eng)
            self.ops[eng].append(lambda e, fn=fn, sem=sem, m=m: fn(e).then_inc(sem, m))
        else:
            self.ops[eng].append(fn)
        return (eng, self.cnt[eng])

    def marker(self, eng):
        return (eng, self.cnt[eng])


def build_nc():
    nc = bass.Bass("TRN2", target_bir_lowering=False)
    es_top = ExitStack()

    d_xt = nc.dram_tensor("xt", [D, T], F32, kind="ExternalInput")
    d_xs = nc.dram_tensor("xs", [T, D], F32, kind="ExternalInput")
    d_wq = nc.dram_tensor("wq", [D, D], F32, kind="ExternalInput")
    d_wk = nc.dram_tensor("wk", [D, D], F32, kind="ExternalInput")
    d_wv = nc.dram_tensor("wv", [D, D], F32, kind="ExternalInput")
    d_bq = nc.dram_tensor("bqp", [128, 8], F32, kind="ExternalInput")
    d_bk = nc.dram_tensor("bkp", [128, 8], F32, kind="ExternalInput")
    d_bv = nc.dram_tensor("bvp", [128, 8], F32, kind="ExternalInput")
    d_u1 = nc.dram_tensor("u1p", [64, 16 * RANK], F32, kind="ExternalInput")
    d_v1 = nc.dram_tensor("v1r", [RANK, DFF], F32, kind="ExternalInput")
    d_b1 = nc.dram_tensor("b1p", [128, 32], F32, kind="ExternalInput")
    d_u2 = nc.dram_tensor("u2r", [DFF, RANK], F32, kind="ExternalInput")
    d_v2 = nc.dram_tensor("v2p", [RANK, D], F32, kind="ExternalInput")
    d_b2 = nc.dram_tensor("b2p", [128, 8], F32, kind="ExternalInput")
    d_cos = nc.dram_tensor("cos128", [128, T], F32, kind="ExternalInput")
    d_sin = nc.dram_tensor("sin128", [128, T], F32, kind="ExternalInput")
    d_mask = nc.dram_tensor("mask01", [128, 128], BF16, kind="ExternalInput")
    d_sel8 = nc.dram_tensor("sel8", [128, 8], F32, kind="ExternalInput")
    d_sel8t = nc.dram_tensor("sel8t", [8, 128], F32, kind="ExternalInput")
    d_id16 = nc.dram_tensor("id16", [128, 64], BF16, kind="ExternalInput")
    d_idf = nc.dram_tensor("idf", [128, 64], F32, kind="ExternalInput")
    d_eps = nc.dram_tensor("epsc", [128, 1], F32, kind="ExternalInput")
    d_out = nc.dram_tensor("outs", [T, D], F32, kind="ExternalOutput")

    u = es_top.enter_context(nc.sbuf_tensor("u", [128, NT * 1024], F32))
    usq = es_top.enter_context(nc.sbuf_tensor("usq", [128, 1024], F32))
    w128 = es_top.enter_context(nc.sbuf_tensor("w128", [128, T], F32))
    sb_bq = es_top.enter_context(nc.sbuf_tensor("sb_bq", [128, 8], F32))
    sb_bk = es_top.enter_context(nc.sbuf_tensor("sb_bk", [128, 8], F32))
    sb_bv = es_top.enter_context(nc.sbuf_tensor("sb_bv", [128, 8], F32))
    sb_b1 = es_top.enter_context(nc.sbuf_tensor("sb_b1", [128, 32], F32))
    sb_b2 = es_top.enter_context(nc.sbuf_tensor("sb_b2", [128, 8], F32))
    sb_mask = es_top.enter_context(nc.sbuf_tensor("sb_mask", [128, 128], BF16))
    sb_sel8 = es_top.enter_context(nc.sbuf_tensor("sb_sel8", [128, 8], F32))
    sb_sel8t = es_top.enter_context(nc.sbuf_tensor("sb_sel8t", [8, 128], F32))
    sb_id16 = es_top.enter_context(nc.sbuf_tensor("sb_id16", [128, 64], BF16))
    sb_idf = es_top.enter_context(nc.sbuf_tensor("sb_idf", [128, 64], F32))
    sb_eps = es_top.enter_context(nc.sbuf_tensor("sb_eps", [128, 1], F32))
    lnS = es_top.enter_context(nc.sbuf_tensor("lnS", [128, 32], F32))
    lnT = es_top.enter_context(nc.sbuf_tensor("lnT", [8, 32], F32))
    lnM = es_top.enter_context(nc.sbuf_tensor("lnM", [8, 16], F32))
    bcS = es_top.enter_context(nc.sbuf_tensor("bcS", [128, 32], F32))
    zrec = es_top.enter_context(nc.sbuf_tensor("zrec", [128, 4], F32))
    outS = es_top.enter_context(nc.sbuf_tensor("outS", [128, 2 * 1024], F32))

    S = Sched(nc, es_top)

    esA = ExitStack()
    w16 = {t: esA.enter_context(nc.sbuf_tensor(f"w16_{t}", [128, 8 * D], BF16))
           for t in "qkv"}
    xt16 = [esA.enter_context(nc.sbuf_tensor(f"xt16_{i}", [128, 8 * 512], BF16))
            for i in range(2)]
    Qs = esA.enter_context(nc.sbuf_tensor("Qs", [128, 4096], BF16))
    Ks = esA.enter_context(nc.sbuf_tensor("Ks", [128, 4096], BF16))
    Q2s = esA.enter_context(nc.sbuf_tensor("Q2s", [128, 4096], BF16))
    VTs = esA.enter_context(nc.sbuf_tensor("VTs", [128, 4096], BF16))
    E2 = esA.enter_context(nc.sbuf_tensor("E2", [128, 2048], BF16))
    Vs = esA.enter_context(nc.sbuf_tensor("Vs", [128, 16 * 65], BF16))

    p_qkv = [esA.enter_context(nc.psum_tensor(f"p_qkv{i}", [128, 512], F32)) for i in range(2)]
    p_sc = [esA.enter_context(nc.psum_tensor(f"p_sc{i}", [128, 512], F32)) for i in range(2)]
    p_avs = [esA.enter_context(nc.psum_tensor(f"p_av{i}", [128, 260], F32)) for i in range(2)]
    p_tr = esA.enter_context(nc.psum_tensor("p_tr", [128, 256], F32))
    p_stbc = esA.enter_context(nc.psum_tensor("p_stbc", [128, 64], F32))

    for dst, src in [(sb_bq, d_bq), (sb_bk, d_bk), (sb_bv, d_bv), (sb_b1, d_b1),
                     (sb_b2, d_b2), (sb_mask, d_mask), (sb_sel8, d_sel8),
                     (sb_sel8t, d_sel8t), (sb_id16, d_id16), (sb_idf, d_idf),
                     (sb_eps, d_eps), (w128, d_cos)]:
        S.emit("sync", lambda e, dst=dst, src=src: e.dma_start(dst[:, :], src.ap()))
    m_consts = S.marker("sync")
    S.emit("sync", lambda e: e.dma_start(
        bass.AP(u, 0, [[NT * 1024, 128], [1024, NT], [1, 1024]]),
        bass.AP(d_xs, 0, [[1024, 128], [128 * 1024, NT], [1, 1024]])))
    m_xs = S.marker("sync")

    for t, d_w in [("q", d_wq), ("k", d_wk), ("v", d_wv)]:
        S.emit("gpsimd", lambda e, t=t, d_w=d_w: e.dma_start(
            bass.AP(w16[t], 0, [[8 * D, 128], [D, 8], [1, D]]),
            bass.AP(d_w, 0, [[D, 128], [128 * D, 8], [1, D]]),
        ))
    m_wqkv = S.marker("gpsimd")

    S.emit("vector", lambda e: e.tensor_mul(w128[:, :], w128[:, :], w128[:, :]),
           deps=[m_consts])
    m_usq_free = ("vector", 0)
    for h in range(2):
        S.emit("sync", lambda e, h=h: e.dma_start(
            usq[:, :], bass.AP(d_sin, 1024 * h, [[T, 128], [1, 1024]])),
            deps=[m_usq_free])
        m_sinh = S.marker("sync")
        S.emit("vector", lambda e: e.tensor_mul(usq[:, :], usq[:, :], usq[:, :]),
               deps=[m_sinh])
        S.emit("vector", lambda e, h=h: e.tensor_add(
            w128[:, 1024 * h: 1024 * h + 1024],
            w128[:, 1024 * h: 1024 * h + 1024], usq[:, :]))
        m_usq_free = S.marker("vector")
    m_w128 = S.marker("vector")

    S.emit("vector", lambda e: e.memset(bass.AP(Vs, 64, [[16 * 65, 128], [65, 16]]), 1.0))
    m_vsones = S.marker("vector")

    m_xt_used = [("tensor", 0), ("tensor", 0)]
    m_stacks_used = ("tensor", 0)
    m_evac_done_v = ("vector", 0)
    m_evac_done_s = ("scalar", 0)
    m_psum_used = [("vector", 0), ("vector", 0)]
    m_q2_dma = ("sync", 0)
    m_e2_used = ("tensor", 0)
    m_vs_used = ("tensor", 0)
    m_av_used = [("vector", 0), ("vector", 0)]
    m_sc_used = [("scalar", 0), ("scalar", 0)]
    m_tr_used = {"vector": ("vector", 0), "scalar": ("scalar", 0)}
    sync_q_prev = ("sync", 0)

    for s in range(NSLAB):
        buf = s % 2
        S.emit("gpsimd", lambda e, s=s, buf=buf: e.dma_start(
            bass.AP(xt16[buf], 0, [[8 * 512, 128], [512, 8], [1, 512]]),
            bass.AP(d_xt, 512 * s, [[T, 128], [128 * T, 8], [1, 512]]),
        ), deps=[m_xt_used[buf]])
        m_xt = S.marker("gpsimd")

        pp = 0
        for tname in "qkv":
            for ch in range(8):
                ps = p_qkv[pp % 2]
                dep_evac = m_psum_used[pp % 2]
                for kc in range(8):
                    S.emit("tensor", lambda e, tname=tname, ch=ch, kc=kc, ps=ps, buf=buf:
                           e.matmul(ps[:, :],
                                    w16[tname][:, D * kc + 128 * ch: D * kc + 128 * ch + 128],
                                    xt16[buf][:, 512 * kc: 512 * kc + 512],
                                    start=(kc == 0), stop=(kc == 7)),
                           deps=([m_wqkv, m_xt, dep_evac] if kc == 0 else []),
                           inc=(kc == 7))
                m_mm = S.marker("tensor")
                for half in range(2):
                    rows = slice(64 * half, 64 * half + 64)
                    ps_ap = lambda ps=ps, half=half: bass.AP(
                        ps, half * 64 * 512, [[512, 64], [128, 4], [8, 16], [1, 8]])
                    if tname == "k":
                        S.emit("vector", lambda e, ps_ap=ps_ap, rows=rows, ch=ch, half=half, s=s:
                               e.scalar_tensor_tensor(
                                   bass.AP(Ks, half * (64 * 4096) + ch,
                                           [[4096, 64], [1024, 4], [64, 16], [8, 8]]),
                                   ps_ap(),
                                   sb_bk[rows, ch: ch + 1],
                                   bass.AP(w128, half * 64 * T + 512 * s,
                                           [[T, 64], [128, 4], [8, 16], [1, 8]]),
                                   ALU.add, ALU.mult),
                               deps=[m_mm, m_w128, m_stacks_used, m_q2_dma, m_consts])
                    else:
                        dst = Qs if tname == "q" else VTs
                        bt = sb_bq if tname == "q" else sb_bv
                        S.emit("scalar", lambda e, ps_ap=ps_ap, rows=rows, dst=dst, ch=ch, half=half, bt=bt:
                               e.activation(
                                   bass.AP(dst, half * (64 * 4096) + ch,
                                           [[4096, 64], [1024, 4], [64, 16], [8, 8]]),
                                   ps_ap(), AF.Identity, bias=bt[rows, ch: ch + 1]),
                               deps=[m_mm, m_stacks_used, m_q2_dma, m_consts])
                if tname == "k":
                    m_evac_done_v = S.marker("vector")
                    m_psum_used[pp % 2] = m_evac_done_v
                else:
                    m_evac_done_s = S.marker("scalar")
                    m_psum_used[pp % 2] = m_evac_done_s
                pp += 1
        m_qevac = m_evac_done_s
        m_kevac = m_evac_done_v

        S.wait("sync", sync_q_prev)
        S.emit("sync", lambda e: e.dma_start(Q2s[0:64, :], Qs[64:128, :]),
               deps=[m_qevac, m_stacks_used])
        S.emit("sync", lambda e: e.dma_start(Q2s[64:128, :], Qs[0:64, :]))
        m_q2_dma = S.marker("sync")
        sync_q_prev = m_q2_dma

        for ti in range(4):
            tile = 4 * s + ti
            w0 = 1024 * ti

            for gb in range(4):
                for hi in range(2):
                    for gg in range(4):
                        g = 4 * gb + gg
                        S.emit("tensor", lambda e, hi=hi, g=g, gg=gg, w0=w0:
                               e.matmul(p_tr[64 * hi: 64 * hi + 64, 64 * gg: 64 * gg + 64],
                                        VTs[64 * hi: 64 * hi + 64, w0 + 64 * g: w0 + 64 * g + 64],
                                        sb_id16[64 * hi: 64 * hi + 64, :],
                                        start=True, stop=True),
                               deps=([m_kevac, m_qevac, m_consts,
                                      m_tr_used["vector"]] if (gg == 0 and hi == 0) else []),
                               inc=(gg == 3 and hi == 1))
                m_trp = S.marker("tensor")
                S.emit("vector", lambda e, gb=gb:
                       e.tensor_copy(
                           bass.AP(Vs, 65 * 4 * gb, [[16 * 65, 128], [65, 4], [1, 64]]),
                           p_tr[:, 0:256]),
                       deps=[m_trp, m_vsones, m_av_used[0], m_av_used[1]])
                m_tr_used["vector"] = S.marker("vector")

            m_vs_ready = S.marker("vector")

            for b in range(4):
                psb = p_sc[b % 2]
                for q4 in range(4):
                    g = 4 * b + q4
                    wg = w0 + 64 * g
                    S.emit("tensor", lambda e, wg=wg, q4=q4, psb=psb:
                           e.matmul(psb[0:64, 128 * q4: 128 * q4 + 64],
                                    Ks[0:64, wg: wg + 64], Qs[0:64, wg: wg + 64],
                                    start=True, stop=True),
                           deps=([m_q2_dma, m_sc_used[b % 2]] if q4 == 0 else []), inc=False)
                    S.emit("tensor", lambda e, wg=wg, q4=q4, psb=psb:
                           e.matmul(psb[0:64, 128 * q4 + 64: 128 * q4 + 128],
                                    Ks[0:64, wg: wg + 64], Q2s[0:64, wg: wg + 64],
                                    start=True, stop=True), inc=False)
                    S.emit("tensor", lambda e, wg=wg, q4=q4, psb=psb:
                           e.matmul(psb[64:128, 128 * q4: 128 * q4 + 64],
                                    Ks[64:128, wg: wg + 64], Q2s[64:128, wg: wg + 64],
                                    start=True, stop=True), inc=False)
                    S.emit("tensor", lambda e, wg=wg, q4=q4, psb=psb:
                           e.matmul(psb[64:128, 128 * q4 + 64: 128 * q4 + 128],
                                    Ks[64:128, wg: wg + 64], Qs[64:128, wg: wg + 64],
                                    start=True, stop=True), inc=(q4 == 3))
                m_scmm = S.marker("tensor")
                S.emit("scalar", lambda e, b=b, psb=psb: e.activation(
                    E2[:, 512 * b: 512 * b + 512], psb[:, :], AF.Exp, scale=0.125),
                    deps=[m_scmm, m_e2_used])
                m_sc_used[b % 2] = S.marker("scalar")
                S.emit("vector", lambda e, b=b: e.tensor_tensor(
                    E2[:, 512 * b: 512 * b + 512], E2[:, 512 * b: 512 * b + 512],
                    bass.AP(sb_mask, 0, [[128, 128], [0, 4], [1, 128]]), ALU.mult),
                    deps=[m_sc_used[b % 2], m_consts])
            m_mask = S.marker("vector")

            for b in range(4):
                pav = p_avs[b % 2]
                for q4 in range(4):
                    g = 4 * b + q4
                    S.emit("tensor", lambda e, g=g, q4=q4, pav=pav:
                           e.matmul(pav[:, 65 * q4: 65 * q4 + 65],
                                    E2[:, 128 * g: 128 * g + 128],
                                    Vs[:, 65 * g: 65 * g + 65],
                                    start=True, stop=True),
                           deps=([m_mask, m_vs_ready, m_av_used[b % 2]] if q4 == 0 else []),
                           inc=(q4 == 3))
                m_avmm = S.marker("tensor")
                S.emit("vector", lambda e, pav=pav: e.reciprocal(
                    zrec[:, 0:4], bass.AP(pav, 64, [[260, 128], [65, 4]])),
                    deps=[m_avmm, m_xs])
                for q4 in range(4):
                    g = 4 * b + q4
                    S.emit("vector", lambda e, g=g, q4=q4, tile=tile, pav=pav:
                           e.scalar_tensor_tensor(
                               u[:, 1024 * tile + 64 * g: 1024 * tile + 64 * g + 64],
                               pav[:, 65 * q4: 65 * q4 + 64],
                               zrec[:, q4: q4 + 1],
                               u[:, 1024 * tile + 64 * g: 1024 * tile + 64 * g + 64],
                               ALU.mult, ALU.add))
                m_av_used[b % 2] = S.marker("vector")
            m_e2_used = m_avmm

            S.emit("vector", lambda e, tile=tile: e.reduce_sum(
                lnS[:, 0:16],
                bass.AP(u, 1024 * tile, [[NT * 1024, 128], [64, 16], [1, 64]]),
                mybir.AxisListType.X))
            m_s1 = S.marker("vector")
            S.emit("scalar", lambda e, tile=tile: e.activation(
                usq[:, :], u[:, 1024 * tile: 1024 * tile + 1024], AF.Square),
                deps=[m_av_used[0], m_av_used[1]])
            m_usq = S.marker("scalar")
            S.emit("vector", lambda e: e.reduce_sum(
                lnS[:, 16:32], bass.AP(usq, 0, [[1024, 128], [64, 16], [1, 64]]),
                mybir.AxisListType.X), deps=[m_usq])
            m_s2 = S.marker("vector")
            S.emit("tensor", lambda e: e.matmul(
                p_stbc[0:8, 0:32], sb_sel8[:, :], lnS[:, :], start=True, stop=True),
                deps=[m_s2, m_consts])
            m_stmm = S.marker("tensor")
            S.emit("vector", lambda e: e.tensor_scalar_mul(lnT[:, :], p_stbc[0:8, 0:32], 1.0 / 1024.0),
                   deps=[m_stmm])
            S.emit("vector", lambda e: e.tensor_mul(lnM[:, :], lnT[:, 0:16], lnT[:, 0:16]))
            S.emit("vector", lambda e: e.tensor_sub(lnM[:, :], lnT[:, 16:32], lnM[:, :]))
            m_var = S.marker("vector")
            S.emit("scalar", lambda e: e.activation(
                lnM[:, :], lnM[:, :], AF.Sqrt, bias=sb_eps[0:8, 0:1]), deps=[m_var, m_consts])
            m_std = S.marker("scalar")
            S.emit("vector", lambda e: e.reciprocal(lnT[:, 16:32], lnM[:, :]),
                   deps=[m_std])
            S.emit("vector", lambda e: e.tensor_mul(lnT[:, 0:16], lnT[:, 0:16], lnT[:, 16:32]))
            m_mrs = S.marker("vector")
            S.emit("tensor", lambda e: e.matmul(
                p_stbc[:, 32:64], sb_sel8t[:, :], lnT[:, :], start=True, stop=True),
                deps=[m_mrs, m_consts])
            m_bcmm = S.marker("tensor")
            S.emit("vector", lambda e: e.tensor_copy(bcS[:, :], p_stbc[:, 32:64]), deps=[m_bcmm])
            S.emit("vector", lambda e, tile=tile: e.tensor_tensor(
                bass.AP(u, 1024 * tile, [[NT * 1024, 128], [64, 16], [1, 64]]),
                bass.AP(u, 1024 * tile, [[NT * 1024, 128], [64, 16], [1, 64]]),
                bass.AP(bcS, 16, [[32, 128], [1, 16], [0, 64]]), ALU.mult))
            S.emit("vector", lambda e, tile=tile: e.tensor_tensor(
                bass.AP(u, 1024 * tile, [[NT * 1024, 128], [64, 16], [1, 64]]),
                bass.AP(u, 1024 * tile, [[NT * 1024, 128], [64, 16], [1, 64]]),
                bass.AP(bcS, 0, [[32, 128], [1, 16], [0, 64]]), ALU.subtract))

        m_stacks_used = S.marker("tensor")
        m_xt_used[buf] = S.marker("tensor")

    m_attn_pe = S.marker("tensor")
    m_attn_v = S.marker("vector")
    m_attn_s = S.marker("scalar")

    esA.close()

    if DEBUG_A:
        total_sync = S.cnt["sync"]
        if total_sync:
            S.ops["sync"].append(lambda e, v=16 * total_sync: e.wait_ge(S.sems["sync"], v))
        with nc.Block() as block:
            @block.sync
            def _(eng):
                for fn in S.ops["sync"]:
                    fn(eng)

            @block.gpsimd
            def _(eng):
                for fn in S.ops["gpsimd"]:
                    fn(eng)

            @block.tensor
            def _(eng):
                for fn in S.ops["tensor"]:
                    fn(eng)

            @block.vector
            def _(eng):
                for fn in S.ops["vector"]:
                    fn(eng)

            @block.scalar
            def _(eng):
                for fn in S.ops["scalar"]:
                    fn(eng)
        esA.close()
        es_top.close()
        return nc
    esB = ExitStack()
    u1p16 = esB.enter_context(nc.sbuf_tensor("u1p16", [64, 16 * RANK], BF16))
    v116 = esB.enter_context(nc.sbuf_tensor("v116", [128, 2 * DFF], BF16))
    u216 = esB.enter_context(nc.sbuf_tensor("u216", [128, 32 * RANK], BF16))
    v2p16 = esB.enter_context(nc.sbuf_tensor("v2p16", [128, 2 * 1024], BF16))
    y16 = esB.enter_context(nc.sbuf_tensor("y16", [128, 2048], BF16))
    YTl = esB.enter_context(nc.sbuf_tensor("YTl", [64, 2048], BF16))
    YTh = esB.enter_context(nc.sbuf_tensor("YTh", [64, 2048], BF16))
    P16 = esB.enter_context(nc.sbuf_tensor("P16", [128, 512], BF16))
    H16 = esB.enter_context(nc.sbuf_tensor("H16", [128, 32 * 256], BF16))
    Q2T = esB.enter_context(nc.sbuf_tensor("Q2T", [128, 512], BF16))
    FTs = esB.enter_context(nc.sbuf_tensor("FTs", [128, 2048], F32))

    p_a = esB.enter_context(nc.psum_tensor("p_a", [128, 256], F32))
    p_b = esB.enter_context(nc.psum_tensor("p_b", [128, 256], F32))
    p_c = esB.enter_context(nc.psum_tensor("p_c", [128, 256], F32))
    p_d = esB.enter_context(nc.psum_tensor("p_d", [128, 256], F32))
    p_fs = esB.enter_context(nc.psum_tensor("p_fs", [128, 256], F32))
    p_yt = esB.enter_context(nc.psum_tensor("p_yt", [64, 256], F32))
    p_st2 = esB.enter_context(nc.psum_tensor("p_st2", [8, 32], F32))
    p_bc2 = esB.enter_context(nc.psum_tensor("p_bc2", [128, 32], F32))

    gate = [m_attn_pe, m_attn_v, m_attn_s]
    S.emit("gpsimd", lambda e: e.dma_start(u1p16[:, :], d_u1.ap()), deps=gate)
    S.emit("gpsimd", lambda e: e.dma_start(
        bass.AP(v116, 0, [[2 * DFF, 128], [DFF, 2], [1, DFF]]),
        bass.AP(d_v1, 0, [[DFF, 128], [128 * DFF, 2], [1, DFF]])))
    S.emit("gpsimd", lambda e: e.dma_start(
        bass.AP(u216, 0, [[32 * RANK, 128], [RANK, 32], [1, RANK]]),
        bass.AP(d_u2, 0, [[RANK, 128], [128 * RANK, 32], [1, RANK]])))
    S.emit("gpsimd", lambda e: e.dma_start(
        bass.AP(v2p16, 0, [[2 * 1024, 128], [1024, 2], [1, 1024]]),
        bass.AP(d_v2, 0, [[1024, 128], [128 * 1024, 2], [1, 1024]])))
    m_wffn = S.marker("gpsimd")

    m_y16_used = ("tensor", 0)
    m_yt_used = ("vector", 0)
    m_yts_used = ("tensor", 0)
    m_p16_used = ("tensor", 0)
    m_h16_used = ("tensor", 0)
    m_q2t_used = ("tensor", 0)
    m_ft_used = ("tensor", 0)
    m_pab_used = ("vector", 0)
    m_pcd_used = ("scalar", 0)
    m_fs_used = ("vector", 0)
    m_outS_used = [("sync", 0), ("sync", 0)]
    n_store = 0

    for f in range(FSLAB):
        c0 = 2048 * f
        S.emit("vector", lambda e, c0=c0: e.tensor_copy(
            y16[:, :], u[:, c0: c0 + 2048]), deps=[m_y16_used])
        m_y16 = S.marker("vector")

        for hi in range(2):
            ytd = YTl if hi == 0 else YTh
            for blk in range(8):
                for gg in range(4):
                    gidx = 4 * blk + gg
                    S.emit("tensor", lambda e, hi=hi, gidx=gidx, gg=gg:
                           e.matmul(p_yt[:, 64 * gg: 64 * gg + 64],
                                    y16[64 * hi: 64 * hi + 64, 64 * gidx: 64 * gidx + 64],
                                    sb_id16[64 * hi: 64 * hi + 64, :],
                                    start=True, stop=True),
                           deps=([m_y16, m_yt_used, m_consts] if gg == 0 else []), inc=(gg == 3))
                m_ytp = S.marker("tensor")
                S.emit("vector", lambda e, ytd=ytd, blk=blk: e.tensor_copy(
                    ytd[:, 256 * blk: 256 * blk + 256], p_yt[:, :]),
                    deps=[m_ytp, m_yts_used])
                m_yt_used = S.marker("vector")
        m_yt_ready = S.marker("vector")

        for rh in range(2):
            ps = p_a if rh == 0 else p_b
            for hc in range(16):
                hi, ch = hc // 8, hc % 8
                yts = YTl if hi == 0 else YTh
                S.emit("tensor", lambda e, rh=rh, hi=hi, ch=ch, ps=ps, yts=yts, hc=hc:
                       e.matmul(ps[:, :],
                                u1p16[:, RANK * (ch + 8 * hi) + 128 * rh:
                                      RANK * (ch + 8 * hi) + 128 * rh + 128],
                                bass.AP(yts, ch, [[2048, 64], [64, 32], [8, 8]]),
                                start=(hc == 0), stop=(hc == 15)),
                       deps=([m_yt_ready, m_wffn, m_pab_used, m_consts] if hc == 0 else []),
                       inc=(hc == 15))
            m_pmm = S.marker("tensor")
            S.emit("vector", lambda e, rh=rh, ps=ps: e.tensor_copy(
                P16[:, 256 * rh: 256 * rh + 256], ps[:, :]), deps=[m_pmm, m_p16_used])
        m_pab_used = S.marker("vector")
        m_p16 = S.marker("vector")

        for ffc in range(32):
            ps = p_c if ffc % 2 == 0 else p_d
            for rc in range(2):
                S.emit("tensor", lambda e, ffc=ffc, rc=rc, ps=ps:
                       e.matmul(ps[:, :],
                                v116[:, DFF * rc + 128 * ffc: DFF * rc + 128 * ffc + 128],
                                P16[:, 256 * rc: 256 * rc + 256],
                                start=(rc == 0), stop=(rc == 1)),
                       deps=([m_p16, m_pcd_used] if rc == 0 else []), inc=(rc == 1))
            m_hmm = S.marker("tensor")
            S.emit("scalar", lambda e, ffc=ffc, ps=ps: e.activation(
                H16[:, 256 * ffc: 256 * ffc + 256], ps[:, :], AF.Gelu_apprx_tanh,
                bias=sb_b1[:, ffc: ffc + 1]), deps=[m_hmm, m_h16_used, m_consts])
            if ffc % 2 == 1:
                m_pcd_used = S.marker("scalar")
        m_h16 = S.marker("scalar")

        for rh in range(2):
            ps = p_a if rh == 0 else p_b
            for ffc in range(32):
                S.emit("tensor", lambda e, rh=rh, ffc=ffc, ps=ps:
                       e.matmul(ps[:, :],
                                u216[:, RANK * ffc + 128 * rh: RANK * ffc + 128 * rh + 128],
                                H16[:, 256 * ffc: 256 * ffc + 256],
                                start=(ffc == 0), stop=(ffc == 31)),
                       deps=([m_h16, m_pab_used] if ffc == 0 else []), inc=(ffc == 31))
            m_qmm = S.marker("tensor")
            S.emit("vector", lambda e, rh=rh, ps=ps: e.tensor_copy(
                Q2T[:, 256 * rh: 256 * rh + 256], ps[:, :]), deps=[m_qmm, m_q2t_used])
        m_pab_used = S.marker("vector")
        m_q2t = S.marker("vector")

        for ch in range(8):
            ps = p_c if ch % 2 == 0 else p_d
            for rc in range(2):
                S.emit("tensor", lambda e, ch=ch, rc=rc, ps=ps:
                       e.matmul(ps[:, :],
                                v2p16[:, 1024 * rc + 128 * ch: 1024 * rc + 128 * ch + 128],
                                Q2T[:, 256 * rc: 256 * rc + 256],
                                start=(rc == 0), stop=(rc == 1)),
                       deps=([m_q2t, m_pcd_used] if rc == 0 else []), inc=(rc == 1))
            m_fmm = S.marker("tensor")
            for half in range(2):
                rows = slice(64 * half, 64 * half + 64)
                S.emit("scalar", lambda e, ch=ch, half=half, rows=rows, ps=ps:
                       e.activation(
                           bass.AP(FTs, half * (64 * 2048) + ch,
                                   [[2048, 64], [1024, 2], [64, 16], [8, 8]]),
                           ps[rows, :], AF.Identity, bias=sb_b2[rows, ch: ch + 1]),
                       deps=[m_fmm, m_ft_used, m_consts])
            if ch % 2 == 1:
                m_pcd_used = S.marker("scalar")
        m_ft = S.marker("scalar")

        for tp in range(2):
            tile = 2 * f + tp
            for gb in range(4):
                for hi in range(2):
                    for gg in range(4):
                        g = 4 * gb + gg
                        S.emit("tensor", lambda e, hi=hi, g=g, gg=gg, tp=tp:
                               e.matmul(p_fs[64 * hi: 64 * hi + 64, 64 * gg: 64 * gg + 64],
                                        FTs[64 * hi: 64 * hi + 64,
                                            1024 * tp + 64 * g: 1024 * tp + 64 * g + 64],
                                        sb_idf[64 * hi: 64 * hi + 64, :],
                                        start=True, stop=True),
                               deps=([m_ft, m_fs_used, m_consts] if (hi == 0 and gg == 0) else []),
                               inc=(hi == 1 and gg == 3))
                m_fsmm = S.marker("tensor")
                S.emit("vector", lambda e, tile=tile, gb=gb: e.scalar_tensor_tensor(
                    u[:, 1024 * tile + 256 * gb: 1024 * tile + 256 * gb + 256],
                    p_fs[:, :], 1.0,
                    u[:, 1024 * tile + 256 * gb: 1024 * tile + 256 * gb + 256],
                    ALU.mult, ALU.add), deps=[m_fsmm])
                m_fs_used = S.marker("vector")

            S.emit("vector", lambda e, tile=tile: e.reduce_sum(
                lnS[:, 0:16],
                bass.AP(u, 1024 * tile, [[NT * 1024, 128], [64, 16], [1, 64]]),
                mybir.AxisListType.X))
            S.emit("scalar", lambda e, tile=tile: e.activation(
                usq[:, :], u[:, 1024 * tile: 1024 * tile + 1024], AF.Square),
                deps=[S.marker("vector")])
            m_usq = S.marker("scalar")
            S.emit("vector", lambda e: e.reduce_sum(
                lnS[:, 16:32], bass.AP(usq, 0, [[1024, 128], [64, 16], [1, 64]]),
                mybir.AxisListType.X), deps=[m_usq])
            m_s2 = S.marker("vector")
            S.emit("tensor", lambda e: e.matmul(
                p_st2[:, :], sb_sel8[:, :], lnS[:, :], start=True, stop=True),
                deps=[m_s2, m_consts])
            m_stmm = S.marker("tensor")
            S.emit("vector", lambda e: e.tensor_scalar_mul(lnT[:, :], p_st2[:, :], 1.0 / 1024.0),
                   deps=[m_stmm])
            S.emit("vector", lambda e: e.tensor_mul(lnM[:, :], lnT[:, 0:16], lnT[:, 0:16]))
            S.emit("vector", lambda e: e.tensor_sub(lnM[:, :], lnT[:, 16:32], lnM[:, :]))
            m_var = S.marker("vector")
            S.emit("scalar", lambda e: e.activation(
                lnM[:, :], lnM[:, :], AF.Sqrt, bias=sb_eps[0:8, 0:1]), deps=[m_var, m_consts])
            m_std = S.marker("scalar")
            S.emit("vector", lambda e: e.reciprocal(lnT[:, 16:32], lnM[:, :]),
                   deps=[m_std])
            S.emit("vector", lambda e: e.tensor_mul(lnT[:, 0:16], lnT[:, 0:16], lnT[:, 16:32]))
            m_mrs = S.marker("vector")
            S.emit("tensor", lambda e: e.matmul(
                p_bc2[:, :], sb_sel8t[:, :], lnT[:, :], start=True, stop=True),
                deps=[m_mrs, m_consts])
            m_bcmm = S.marker("tensor")
            S.emit("vector", lambda e: e.tensor_copy(bcS[:, :], p_bc2[:, :]), deps=[m_bcmm])
            ob = tile % 2
            S.emit("vector", lambda e, tile=tile, ob=ob: e.tensor_tensor(
                bass.AP(outS, 1024 * ob, [[2048, 128], [64, 16], [1, 64]]),
                bass.AP(u, 1024 * tile, [[NT * 1024, 128], [64, 16], [1, 64]]),
                bass.AP(bcS, 16, [[32, 128], [1, 16], [0, 64]]), ALU.mult),
                deps=[m_outS_used[ob]])
            S.emit("vector", lambda e, ob=ob: e.tensor_tensor(
                bass.AP(outS, 1024 * ob, [[2048, 128], [64, 16], [1, 64]]),
                bass.AP(outS, 1024 * ob, [[2048, 128], [64, 16], [1, 64]]),
                bass.AP(bcS, 0, [[32, 128], [1, 16], [0, 64]]), ALU.subtract))
            m_norm = S.marker("vector")
            S.emit("sync", lambda e, tile=tile, ob=ob: e.dma_start(
                bass.AP(d_out, 128 * 1024 * tile, [[1024, 128], [1, 1024]]),
                outS[:, 1024 * ob: 1024 * ob + 1024]), deps=[m_norm])
            n_store += 1
            m_outS_used[ob] = S.marker("sync")

    total_sync = S.cnt["sync"]
    S.ops["sync"].append(lambda e, v=16 * total_sync: e.wait_ge(S.sems["sync"], v))

    with nc.Block() as block:
        @block.sync
        def _(eng):
            for fn in S.ops["sync"]:
                fn(eng)

        @block.gpsimd
        def _(eng):
            for fn in S.ops["gpsimd"]:
                fn(eng)

        @block.tensor
        def _(eng):
            for fn in S.ops["tensor"]:
                fn(eng)

        @block.vector
        def _(eng):
            for fn in S.ops["vector"]:
                fn(eng)

        @block.scalar
        def _(eng):
            for fn in S.ops["scalar"]:
                fn(eng)

    esB.close()
    es_top.close()
    return nc



def _rope_tables():
    freqs = (10000.0 ** (-(np.arange(0, DK, 2, dtype=np.float64) / D))).astype(np.float32)
    ang = (np.arange(NCORES * T, dtype=np.float32)[:, None] * freqs[None, :]).astype(np.float32)
    libm = ctypes.CDLL(ctypes.util.find_library("m"))
    libm.cosf.restype = ctypes.c_float
    libm.cosf.argtypes = [ctypes.c_float]
    libm.sinf.restype = ctypes.c_float
    libm.sinf.argtypes = [ctypes.c_float]
    flat = ang.ravel()
    cos = np.fromiter((libm.cosf(float(v)) for v in flat), np.float32, flat.size).reshape(ang.shape)
    sin = np.fromiter((libm.sinf(float(v)) for v in flat), np.float32, flat.size).reshape(ang.shape)
    return cos, sin


_NC_CACHE = {}


def kernel(L0_attn_x, L0_attn_Wq, L0_attn_bq, L0_attn_Wk, L0_attn_bk,
           L0_attn_Wv, L0_attn_bv, L0_ffn_U1, L0_ffn_V1, L0_ffn_b1,
           L0_ffn_U2, L0_ffn_V2, L0_ffn_b2):
    x = np.ascontiguousarray(np.asarray(L0_attn_x, dtype=np.float32).reshape(NCORES * T, D))

    perm = np.concatenate([
        np.concatenate([np.arange(64 * c, 64 * c + 64),
                        np.arange(64 * (c + 8), 64 * (c + 8) + 64)])
        for c in range(8)])
    wq = np.ascontiguousarray(np.asarray(L0_attn_Wq, np.float32)[:, perm])
    wk = np.ascontiguousarray(np.asarray(L0_attn_Wk, np.float32)[:, perm])
    wv = np.ascontiguousarray(np.asarray(L0_attn_Wv, np.float32)[:, perm])
    v2p = np.ascontiguousarray(np.asarray(L0_ffn_V2, np.float32)[:, perm])

    def bias_chunks(b):
        return np.ascontiguousarray(np.asarray(b, np.float32)[perm].reshape(8, 128).T)

    bqp, bkp, bvp = (bias_chunks(b) for b in (L0_attn_bq, L0_attn_bk, L0_attn_bv))
    b2p = bias_chunks(L0_ffn_b2)
    u1p = np.ascontiguousarray(
        np.asarray(L0_ffn_U1, np.float32).reshape(16, 64, RANK).transpose(1, 0, 2).reshape(64, 16 * RANK))
    v1r = np.ascontiguousarray(np.asarray(L0_ffn_V1, np.float32))
    u2r = np.ascontiguousarray(np.asarray(L0_ffn_U2, np.float32))
    b1p = np.ascontiguousarray(np.asarray(L0_ffn_b1, np.float32).reshape(32, 128).T)

    cos, sin = _rope_tables()

    mask01 = np.zeros((128, 128), np.float32)
    for r in range(128):
        for n in range(128):
            if ((r % 64) >> 3) == ((n % 64) >> 3):
                mask01[r, n] = 1.0
    mask01 = mask01.astype(ml_dtypes.bfloat16)
    sel8 = np.zeros((128, 8), np.float32)
    for row in range(128):
        sel8[row, (row % 64) >> 3] = 1.0
    sel8t = np.ascontiguousarray(sel8.T)
    id16 = np.concatenate([np.eye(64, dtype=ml_dtypes.bfloat16)] * 2, 0)
    idf = np.concatenate([np.eye(64, dtype=np.float32)] * 2, 0)
    epsc = np.full((128, 1), 1e-5, np.float32)

    if "nc" not in _NC_CACHE:
        _NC_CACHE["nc"] = build_nc()
    nc = _NC_CACHE["nc"]

    in_maps = []
    for c in range(NCORES):
        xc = x[T * c: T * (c + 1)]
        xt = np.ascontiguousarray(xc.T)
        v = xc.reshape(NT, 16, 8, 2, 8, 64)
        xs = np.ascontiguousarray(v.transpose(0, 3, 2, 4, 1, 5).reshape(T, D))
        cos128 = np.ascontiguousarray(
            np.tile(np.repeat(cos[T * c: T * (c + 1)].T, 2, axis=0), (2, 1)))
        sin128 = np.ascontiguousarray(
            np.tile(np.repeat(sin[T * c: T * (c + 1)].T, 2, axis=0), (2, 1)))
        in_maps.append({
            "xt": xt, "xs": xs, "wq": wq, "wk": wk, "wv": wv,
            "bqp": bqp, "bkp": bkp, "bvp": bvp,
            "u1p": u1p, "v1r": v1r, "b1p": b1p, "u2r": u2r, "v2p": v2p, "b2p": b2p,
            "cos128": cos128, "sin128": sin128,
            "mask01": mask01, "sel8": sel8, "sel8t": sel8t,
            "id16": id16, "idf": idf, "epsc": epsc,
        })

    res = run_bass_kernel_spmd(nc, in_maps, core_ids=list(range(NCORES)))

    out = np.empty((NCORES * T, D), np.float32)
    for c in range(NCORES):
        o = np.asarray(res.results[c]["outs"], np.float32)
        v = o.reshape(NT, 2, 8, 8, 16, 64)
        out[T * c: T * (c + 1)] = v.transpose(0, 4, 2, 1, 3, 5).reshape(T, D)
    return out.reshape(1, NCORES * T, D)


# version 05
# speedup vs baseline: 1.1302x; 1.0324x over previous
import ctypes
import ctypes.util
from contextlib import ExitStack

import numpy as np
import ml_dtypes

import concourse.bass as bass
import concourse.mybir as mybir
from concourse.bass_utils import run_bass_kernel_spmd

F32 = mybir.dt.float32
BF16 = mybir.dt.bfloat16
AF = mybir.ActivationFunctionType
ALU = mybir.AluOpType

NCORES = 8
T = 2048
NT = 16
D = 1024
NH = 16
DK = 64
RANK = 256
DFF = 4096
NSLAB = 4
FSLAB = 8

ENGS = ("sync", "gpsimd", "tensor", "vector", "scalar")
DEBUG_A = False


class Sched:

    def __init__(self, nc, es):
        self.nc = nc
        self.ops = {e: [] for e in ENGS}
        self.cnt = {e: 0 for e in ENGS}
        self.hw = {e: {s: 0 for s in ENGS} for e in ENGS}
        self.sems = {e: es.enter_context(nc.semaphore(f"sem_{e}")) for e in ENGS}

    def mult(self, e):
        return 16 if e in ("sync", "gpsimd") else 1

    def throttle(self, eng):
        v = self.cnt[eng] * self.mult(eng)
        if v > 0:
            sem = self.sems[eng]
            self.ops[eng].append(lambda e, sem=sem, v=v: e.wait_ge(sem, v))

    def wait(self, eng, dep):
        src, val = dep
        if src == eng or val <= 0:
            return
        if self.hw[eng][src] >= val:
            return
        self.hw[eng][src] = val
        sem = self.sems[src]
        v = val * self.mult(src)
        self.ops[eng].append(lambda e, sem=sem, v=v: e.wait_ge(sem, v))

    def emit(self, eng, fn, deps=(), inc=True):
        for d in deps:
            self.wait(eng, d)
        if eng in ("sync", "gpsimd"):
            self.throttle(eng)
        elif inc and eng != "tensor" and self.cnt[eng] > 0:
            sem, v = self.sems[eng], self.cnt[eng]
            self.ops[eng].append(lambda e, sem=sem, v=v: e.wait_ge(sem, v))
        if inc:
            self.cnt[eng] += 1
            sem, m = self.sems[eng], self.mult(eng)
            self.ops[eng].append(lambda e, fn=fn, sem=sem, m=m: fn(e).then_inc(sem, m))
        else:
            self.ops[eng].append(fn)
        return (eng, self.cnt[eng])

    def marker(self, eng):
        return (eng, self.cnt[eng])


def build_nc():
    nc = bass.Bass("TRN2", target_bir_lowering=False)
    es_top = ExitStack()

    d_xt = nc.dram_tensor("xt", [D, T], F32, kind="ExternalInput")
    d_xs = nc.dram_tensor("xs", [T, D], F32, kind="ExternalInput")
    d_wq = nc.dram_tensor("wq", [D, D], F32, kind="ExternalInput")
    d_wk = nc.dram_tensor("wk", [D, D], F32, kind="ExternalInput")
    d_wv = nc.dram_tensor("wv", [D, D], F32, kind="ExternalInput")
    d_bq = nc.dram_tensor("bqp", [128, 8], F32, kind="ExternalInput")
    d_bk = nc.dram_tensor("bkp", [128, 8], F32, kind="ExternalInput")
    d_bv = nc.dram_tensor("bvp", [128, 8], F32, kind="ExternalInput")
    d_u1 = nc.dram_tensor("u1p", [64, 16 * RANK], F32, kind="ExternalInput")
    d_v1 = nc.dram_tensor("v1r", [RANK, DFF], F32, kind="ExternalInput")
    d_b1 = nc.dram_tensor("b1p", [128, 32], F32, kind="ExternalInput")
    d_u2 = nc.dram_tensor("u2r", [DFF, RANK], F32, kind="ExternalInput")
    d_v2 = nc.dram_tensor("v2p", [RANK, D], F32, kind="ExternalInput")
    d_b2 = nc.dram_tensor("b2p", [128, 8], F32, kind="ExternalInput")
    d_cos = nc.dram_tensor("cos128", [128, T], F32, kind="ExternalInput")
    d_sin = nc.dram_tensor("sin128", [128, T], F32, kind="ExternalInput")
    d_mask = nc.dram_tensor("mask01", [128, 128], BF16, kind="ExternalInput")
    d_sel8 = nc.dram_tensor("sel8", [128, 8], F32, kind="ExternalInput")
    d_sel8t = nc.dram_tensor("sel8t", [8, 128], F32, kind="ExternalInput")
    d_id16 = nc.dram_tensor("id16", [128, 64], BF16, kind="ExternalInput")
    d_idf = nc.dram_tensor("idf", [128, 64], F32, kind="ExternalInput")
    d_eps = nc.dram_tensor("epsc", [128, 1], F32, kind="ExternalInput")
    d_out = nc.dram_tensor("outs", [T, D], F32, kind="ExternalOutput")

    u = es_top.enter_context(nc.sbuf_tensor("u", [128, NT * 1024], F32))
    usq = es_top.enter_context(nc.sbuf_tensor("usq", [128, 1024], F32))
    w128 = es_top.enter_context(nc.sbuf_tensor("w128", [128, T], F32))
    sb_bq = es_top.enter_context(nc.sbuf_tensor("sb_bq", [128, 8], F32))
    sb_bk = es_top.enter_context(nc.sbuf_tensor("sb_bk", [128, 8], F32))
    sb_bv = es_top.enter_context(nc.sbuf_tensor("sb_bv", [128, 8], F32))
    sb_b1 = es_top.enter_context(nc.sbuf_tensor("sb_b1", [128, 32], F32))
    sb_b2 = es_top.enter_context(nc.sbuf_tensor("sb_b2", [128, 8], F32))
    sb_mask = es_top.enter_context(nc.sbuf_tensor("sb_mask", [128, 128], BF16))
    sb_sel8 = es_top.enter_context(nc.sbuf_tensor("sb_sel8", [128, 8], F32))
    sb_sel8t = es_top.enter_context(nc.sbuf_tensor("sb_sel8t", [8, 128], F32))
    sb_id16 = es_top.enter_context(nc.sbuf_tensor("sb_id16", [128, 64], BF16))
    sb_idf = es_top.enter_context(nc.sbuf_tensor("sb_idf", [128, 64], F32))
    sb_eps = es_top.enter_context(nc.sbuf_tensor("sb_eps", [128, 1], F32))
    lnS = es_top.enter_context(nc.sbuf_tensor("lnS", [128, 32], F32))
    lnT = es_top.enter_context(nc.sbuf_tensor("lnT", [8, 32], F32))
    lnM = es_top.enter_context(nc.sbuf_tensor("lnM", [8, 16], F32))
    bcS = es_top.enter_context(nc.sbuf_tensor("bcS", [128, 32], F32))
    zrec = es_top.enter_context(nc.sbuf_tensor("zrec", [128, 4], F32))
    outS = es_top.enter_context(nc.sbuf_tensor("outS", [128, 2 * 1024], F32))

    S = Sched(nc, es_top)

    esA = ExitStack()
    w16 = {t: esA.enter_context(nc.sbuf_tensor(f"w16_{t}", [128, 8 * D], BF16))
           for t in "qkv"}
    xt16 = [esA.enter_context(nc.sbuf_tensor(f"xt16_{i}", [128, 8 * 512], BF16))
            for i in range(2)]
    Qs = esA.enter_context(nc.sbuf_tensor("Qs", [128, 4096], BF16))
    Ks = esA.enter_context(nc.sbuf_tensor("Ks", [128, 4096], BF16))
    Q2s = esA.enter_context(nc.sbuf_tensor("Q2s", [128, 4096], BF16))
    VTs = esA.enter_context(nc.sbuf_tensor("VTs", [128, 4096], BF16))
    E2 = esA.enter_context(nc.sbuf_tensor("E2", [128, 2048], BF16))
    Vs = esA.enter_context(nc.sbuf_tensor("Vs", [128, 16 * 65], BF16))

    p_qkv = [esA.enter_context(nc.psum_tensor(f"p_qkv{i}", [128, 512], F32)) for i in range(2)]
    p_sc = [esA.enter_context(nc.psum_tensor(f"p_sc{i}", [128, 512], F32)) for i in range(2)]
    p_avs = [esA.enter_context(nc.psum_tensor(f"p_av{i}", [128, 260], F32)) for i in range(2)]
    p_tr = esA.enter_context(nc.psum_tensor("p_tr", [128, 256], F32))
    p_stbc = esA.enter_context(nc.psum_tensor("p_stbc", [128, 64], F32))

    for dst, src in [(sb_bq, d_bq), (sb_bk, d_bk), (sb_bv, d_bv), (sb_b1, d_b1),
                     (sb_b2, d_b2), (sb_mask, d_mask), (sb_sel8, d_sel8),
                     (sb_sel8t, d_sel8t), (sb_id16, d_id16), (sb_idf, d_idf),
                     (sb_eps, d_eps), (w128, d_cos)]:
        S.emit("sync", lambda e, dst=dst, src=src: e.dma_start(dst[:, :], src.ap()))
    m_consts = S.marker("sync")
    S.emit("sync", lambda e: e.dma_start(
        bass.AP(u, 0, [[NT * 1024, 128], [1024, NT], [1, 1024]]),
        bass.AP(d_xs, 0, [[1024, 128], [128 * 1024, NT], [1, 1024]])))
    m_xs = S.marker("sync")

    for t, d_w in [("q", d_wq), ("k", d_wk), ("v", d_wv)]:
        S.emit("gpsimd", lambda e, t=t, d_w=d_w: e.dma_start(
            bass.AP(w16[t], 0, [[8 * D, 128], [D, 8], [1, D]]),
            bass.AP(d_w, 0, [[D, 128], [128 * D, 8], [1, D]]),
        ))
    m_wqkv = S.marker("gpsimd")

    S.emit("vector", lambda e: e.tensor_mul(w128[:, :], w128[:, :], w128[:, :]),
           deps=[m_consts])
    m_usq_free = ("vector", 0)
    for h in range(2):
        S.emit("sync", lambda e, h=h: e.dma_start(
            usq[:, :], bass.AP(d_sin, 1024 * h, [[T, 128], [1, 1024]])),
            deps=[m_usq_free])
        m_sinh = S.marker("sync")
        S.emit("vector", lambda e: e.tensor_mul(usq[:, :], usq[:, :], usq[:, :]),
               deps=[m_sinh])
        S.emit("vector", lambda e, h=h: e.tensor_add(
            w128[:, 1024 * h: 1024 * h + 1024],
            w128[:, 1024 * h: 1024 * h + 1024], usq[:, :]))
        m_usq_free = S.marker("vector")
    m_w128 = S.marker("vector")

    S.emit("vector", lambda e: e.memset(bass.AP(Vs, 64, [[16 * 65, 128], [65, 16]]), 1.0))
    m_vsones = S.marker("vector")

    m_xt_used = [("tensor", 0), ("tensor", 0)]
    m_stacks_used = ("tensor", 0)
    m_evac_done_v = ("vector", 0)
    m_evac_done_s = ("scalar", 0)
    m_psum_used = [("vector", 0), ("vector", 0)]
    m_q2_dma = ("sync", 0)
    m_e2_used = ("tensor", 0)
    m_vs_used = ("tensor", 0)
    m_av_used = [("vector", 0), ("vector", 0)]
    m_sc_used = [("scalar", 0), ("scalar", 0)]
    m_tr_used = {"vector": ("vector", 0), "scalar": ("scalar", 0)}
    sync_q_prev = ("sync", 0)

    for s in range(NSLAB):
        buf = s % 2
        S.emit("gpsimd", lambda e, s=s, buf=buf: e.dma_start(
            bass.AP(xt16[buf], 0, [[8 * 512, 128], [512, 8], [1, 512]]),
            bass.AP(d_xt, 512 * s, [[T, 128], [128 * T, 8], [1, 512]]),
        ), deps=[m_xt_used[buf]])
        m_xt = S.marker("gpsimd")

        pp = 0
        for tname in "qkv":
            for ch in range(8):
                ps = p_qkv[pp % 2]
                dep_evac = m_psum_used[pp % 2]
                for kc in range(8):
                    S.emit("tensor", lambda e, tname=tname, ch=ch, kc=kc, ps=ps, buf=buf:
                           e.matmul(ps[:, :],
                                    w16[tname][:, D * kc + 128 * ch: D * kc + 128 * ch + 128],
                                    xt16[buf][:, 512 * kc: 512 * kc + 512],
                                    start=(kc == 0), stop=(kc == 7)),
                           deps=([m_wqkv, m_xt, dep_evac] if kc == 0 else []),
                           inc=(kc == 7))
                m_mm = S.marker("tensor")
                for half in range(2):
                    rows = slice(64 * half, 64 * half + 64)
                    ps_ap = lambda ps=ps, half=half: bass.AP(
                        ps, half * 64 * 512, [[512, 64], [128, 4], [1, 128]])
                    if tname == "k":
                        S.emit("vector", lambda e, ps_ap=ps_ap, rows=rows, ch=ch, half=half, s=s:
                               e.scalar_tensor_tensor(
                                   bass.AP(Ks, half * (64 * 4096) + ch,
                                           [[4096, 64], [1024, 4], [8, 128]]),
                                   ps_ap(),
                                   sb_bk[rows, ch: ch + 1],
                                   bass.AP(w128, half * 64 * T + 512 * s,
                                           [[T, 64], [128, 4], [1, 128]]),
                                   ALU.add, ALU.mult),
                               deps=[m_mm, m_w128, m_stacks_used, m_q2_dma, m_consts])
                    else:
                        dst = Qs if tname == "q" else VTs
                        bt = sb_bq if tname == "q" else sb_bv
                        S.emit("scalar", lambda e, ps_ap=ps_ap, rows=rows, dst=dst, ch=ch, half=half, bt=bt:
                               e.activation(
                                   bass.AP(dst, half * (64 * 4096) + ch,
                                           [[4096, 64], [1024, 4], [8, 128]]),
                                   ps_ap(), AF.Identity, bias=bt[rows, ch: ch + 1]),
                               deps=[m_mm, m_stacks_used, m_q2_dma, m_consts])
                if tname == "k":
                    m_evac_done_v = S.marker("vector")
                    m_psum_used[pp % 2] = m_evac_done_v
                else:
                    m_evac_done_s = S.marker("scalar")
                    m_psum_used[pp % 2] = m_evac_done_s
                pp += 1
        m_qevac = m_evac_done_s
        m_kevac = m_evac_done_v

        S.wait("sync", sync_q_prev)
        S.emit("sync", lambda e: e.dma_start(Q2s[0:64, :], Qs[64:128, :]),
               deps=[m_qevac, m_stacks_used])
        S.emit("sync", lambda e: e.dma_start(Q2s[64:128, :], Qs[0:64, :]))
        m_q2_dma = S.marker("sync")
        sync_q_prev = m_q2_dma

        for ti in range(4):
            tile = 4 * s + ti
            w0 = 1024 * ti

            for gb in range(4):
                for hi in range(2):
                    for gg in range(4):
                        g = 4 * gb + gg
                        S.emit("tensor", lambda e, hi=hi, g=g, gg=gg, w0=w0:
                               e.matmul(p_tr[64 * hi: 64 * hi + 64, 64 * gg: 64 * gg + 64],
                                        VTs[64 * hi: 64 * hi + 64, w0 + 64 * g: w0 + 64 * g + 64],
                                        sb_id16[64 * hi: 64 * hi + 64, :],
                                        start=True, stop=True),
                               deps=([m_kevac, m_qevac, m_consts,
                                      m_tr_used["vector"]] if (gg == 0 and hi == 0) else []),
                               inc=(gg == 3 and hi == 1))
                m_trp = S.marker("tensor")
                S.emit("vector", lambda e, gb=gb:
                       e.tensor_copy(
                           bass.AP(Vs, 65 * 4 * gb, [[16 * 65, 128], [65, 4], [1, 64]]),
                           p_tr[:, 0:256]),
                       deps=[m_trp, m_vsones, m_av_used[0], m_av_used[1]])
                m_tr_used["vector"] = S.marker("vector")

            m_vs_ready = S.marker("vector")

            for b in range(4):
                psb = p_sc[b % 2]
                for q4 in range(4):
                    g = 4 * b + q4
                    wg = w0 + 64 * g
                    S.emit("tensor", lambda e, wg=wg, q4=q4, psb=psb:
                           e.matmul(psb[0:64, 128 * q4: 128 * q4 + 64],
                                    Ks[0:64, wg: wg + 64], Qs[0:64, wg: wg + 64],
                                    start=True, stop=True),
                           deps=([m_q2_dma, m_sc_used[b % 2]] if q4 == 0 else []), inc=False)
                    S.emit("tensor", lambda e, wg=wg, q4=q4, psb=psb:
                           e.matmul(psb[0:64, 128 * q4 + 64: 128 * q4 + 128],
                                    Ks[0:64, wg: wg + 64], Q2s[0:64, wg: wg + 64],
                                    start=True, stop=True), inc=False)
                    S.emit("tensor", lambda e, wg=wg, q4=q4, psb=psb:
                           e.matmul(psb[64:128, 128 * q4: 128 * q4 + 64],
                                    Ks[64:128, wg: wg + 64], Q2s[64:128, wg: wg + 64],
                                    start=True, stop=True), inc=False)
                    S.emit("tensor", lambda e, wg=wg, q4=q4, psb=psb:
                           e.matmul(psb[64:128, 128 * q4 + 64: 128 * q4 + 128],
                                    Ks[64:128, wg: wg + 64], Qs[64:128, wg: wg + 64],
                                    start=True, stop=True), inc=(q4 == 3))
                m_scmm = S.marker("tensor")
                S.emit("scalar", lambda e, b=b, psb=psb: e.activation(
                    E2[:, 512 * b: 512 * b + 512], psb[:, :], AF.Exp, scale=0.125),
                    deps=[m_scmm, m_e2_used])
                m_sc_used[b % 2] = S.marker("scalar")
                S.emit("vector", lambda e, b=b: e.tensor_tensor(
                    E2[:, 512 * b: 512 * b + 512], E2[:, 512 * b: 512 * b + 512],
                    bass.AP(sb_mask, 0, [[128, 128], [0, 4], [1, 128]]), ALU.mult),
                    deps=[m_sc_used[b % 2], m_consts])
            m_mask = S.marker("vector")

            for b in range(4):
                pav = p_avs[b % 2]
                for q4 in range(4):
                    g = 4 * b + q4
                    S.emit("tensor", lambda e, g=g, q4=q4, pav=pav:
                           e.matmul(pav[:, 65 * q4: 65 * q4 + 65],
                                    E2[:, 128 * g: 128 * g + 128],
                                    Vs[:, 65 * g: 65 * g + 65],
                                    start=True, stop=True),
                           deps=([m_mask, m_vs_ready, m_av_used[b % 2]] if q4 == 0 else []),
                           inc=(q4 == 3))
                m_avmm = S.marker("tensor")
                S.emit("vector", lambda e, pav=pav: e.reciprocal(
                    zrec[:, 0:4], bass.AP(pav, 64, [[260, 128], [65, 4]])),
                    deps=[m_avmm, m_xs])
                for q4 in range(4):
                    g = 4 * b + q4
                    S.emit("vector", lambda e, g=g, q4=q4, tile=tile, pav=pav:
                           e.scalar_tensor_tensor(
                               u[:, 1024 * tile + 64 * g: 1024 * tile + 64 * g + 64],
                               pav[:, 65 * q4: 65 * q4 + 64],
                               zrec[:, q4: q4 + 1],
                               u[:, 1024 * tile + 64 * g: 1024 * tile + 64 * g + 64],
                               ALU.mult, ALU.add))
                m_av_used[b % 2] = S.marker("vector")
            m_e2_used = m_avmm


        m_stacks_used = S.marker("tensor")
        m_xt_used[buf] = S.marker("tensor")


    for tile in range(NT):
        S.emit("vector", lambda e, tile=tile: e.reduce_sum(
            lnS[:, 0:16],
            bass.AP(u, 1024 * tile, [[NT * 1024, 128], [64, 16], [1, 64]]),
            mybir.AxisListType.X))
        m_s1 = S.marker("vector")
        S.emit("scalar", lambda e, tile=tile: e.activation(
            usq[:, :], u[:, 1024 * tile: 1024 * tile + 1024], AF.Square),
            deps=[m_av_used[0], m_av_used[1]])
        m_usq = S.marker("scalar")
        S.emit("vector", lambda e: e.reduce_sum(
            lnS[:, 16:32], bass.AP(usq, 0, [[1024, 128], [64, 16], [1, 64]]),
            mybir.AxisListType.X), deps=[m_usq])
        m_s2 = S.marker("vector")
        S.emit("tensor", lambda e: e.matmul(
            p_stbc[0:8, 0:32], sb_sel8[:, :], lnS[:, :], start=True, stop=True),
            deps=[m_s2, m_consts])
        m_stmm = S.marker("tensor")
        S.emit("vector", lambda e: e.tensor_scalar_mul(lnT[:, :], p_stbc[0:8, 0:32], 1.0 / 1024.0),
               deps=[m_stmm])
        S.emit("vector", lambda e: e.tensor_mul(lnM[:, :], lnT[:, 0:16], lnT[:, 0:16]))
        S.emit("vector", lambda e: e.tensor_sub(lnM[:, :], lnT[:, 16:32], lnM[:, :]))
        m_var = S.marker("vector")
        S.emit("scalar", lambda e: e.activation(
            lnM[:, :], lnM[:, :], AF.Sqrt, bias=sb_eps[0:8, 0:1]), deps=[m_var, m_consts])
        m_std = S.marker("scalar")
        S.emit("vector", lambda e: e.reciprocal(lnT[:, 16:32], lnM[:, :]),
               deps=[m_std])
        S.emit("vector", lambda e: e.tensor_mul(lnT[:, 0:16], lnT[:, 0:16], lnT[:, 16:32]))
        m_mrs = S.marker("vector")
        S.emit("tensor", lambda e: e.matmul(
            p_stbc[:, 32:64], sb_sel8t[:, :], lnT[:, :], start=True, stop=True),
            deps=[m_mrs])
        m_bcmm = S.marker("tensor")
        S.emit("vector", lambda e: e.tensor_copy(bcS[:, :], p_stbc[:, 32:64]), deps=[m_bcmm])
        S.emit("vector", lambda e, tile=tile: e.tensor_tensor(
            bass.AP(u, 1024 * tile, [[NT * 1024, 128], [64, 16], [1, 64]]),
            bass.AP(u, 1024 * tile, [[NT * 1024, 128], [64, 16], [1, 64]]),
            bass.AP(bcS, 16, [[32, 128], [1, 16], [0, 64]]), ALU.mult))
        S.emit("vector", lambda e, tile=tile: e.tensor_tensor(
            bass.AP(u, 1024 * tile, [[NT * 1024, 128], [64, 16], [1, 64]]),
            bass.AP(u, 1024 * tile, [[NT * 1024, 128], [64, 16], [1, 64]]),
            bass.AP(bcS, 0, [[32, 128], [1, 16], [0, 64]]), ALU.subtract))

    m_attn_pe = S.marker("tensor")
    m_attn_v = S.marker("vector")
    m_attn_s = S.marker("scalar")

    esA.close()

    if DEBUG_A:
        total_sync = S.cnt["sync"]
        if total_sync:
            S.ops["sync"].append(lambda e, v=16 * total_sync: e.wait_ge(S.sems["sync"], v))
        with nc.Block() as block:
            @block.sync
            def _(eng):
                for fn in S.ops["sync"]:
                    fn(eng)

            @block.gpsimd
            def _(eng):
                for fn in S.ops["gpsimd"]:
                    fn(eng)

            @block.tensor
            def _(eng):
                for fn in S.ops["tensor"]:
                    fn(eng)

            @block.vector
            def _(eng):
                for fn in S.ops["vector"]:
                    fn(eng)

            @block.scalar
            def _(eng):
                for fn in S.ops["scalar"]:
                    fn(eng)
        esA.close()
        es_top.close()
        return nc
    esB = ExitStack()
    u1p16 = esB.enter_context(nc.sbuf_tensor("u1p16", [64, 16 * RANK], BF16))
    v116 = esB.enter_context(nc.sbuf_tensor("v116", [128, 2 * DFF], BF16))
    u216 = esB.enter_context(nc.sbuf_tensor("u216", [128, 32 * RANK], BF16))
    v2p16 = esB.enter_context(nc.sbuf_tensor("v2p16", [128, 2 * 1024], BF16))
    y16 = esB.enter_context(nc.sbuf_tensor("y16", [128, 2048], BF16))
    YTl = esB.enter_context(nc.sbuf_tensor("YTl", [64, 2048], BF16))
    YTh = esB.enter_context(nc.sbuf_tensor("YTh", [64, 2048], BF16))
    P16 = esB.enter_context(nc.sbuf_tensor("P16", [128, 512], BF16))
    H16 = esB.enter_context(nc.sbuf_tensor("H16", [128, 32 * 256], BF16))
    Q2T = esB.enter_context(nc.sbuf_tensor("Q2T", [128, 512], BF16))
    FTs = esB.enter_context(nc.sbuf_tensor("FTs", [128, 2048], F32))

    p_a = esB.enter_context(nc.psum_tensor("p_a", [128, 256], F32))
    p_b = esB.enter_context(nc.psum_tensor("p_b", [128, 256], F32))
    p_c = esB.enter_context(nc.psum_tensor("p_c", [128, 256], F32))
    p_d = esB.enter_context(nc.psum_tensor("p_d", [128, 256], F32))
    p_fs = esB.enter_context(nc.psum_tensor("p_fs", [128, 256], F32))
    p_yt = esB.enter_context(nc.psum_tensor("p_yt", [64, 256], F32))
    p_st2 = esB.enter_context(nc.psum_tensor("p_st2", [8, 32], F32))
    p_bc2 = esB.enter_context(nc.psum_tensor("p_bc2", [128, 32], F32))

    gate = [m_attn_pe, m_attn_v, m_attn_s]
    S.emit("gpsimd", lambda e: e.dma_start(u1p16[:, :], d_u1.ap()), deps=gate)
    S.emit("gpsimd", lambda e: e.dma_start(
        bass.AP(v116, 0, [[2 * DFF, 128], [DFF, 2], [1, DFF]]),
        bass.AP(d_v1, 0, [[DFF, 128], [128 * DFF, 2], [1, DFF]])))
    S.emit("gpsimd", lambda e: e.dma_start(
        bass.AP(u216, 0, [[32 * RANK, 128], [RANK, 32], [1, RANK]]),
        bass.AP(d_u2, 0, [[RANK, 128], [128 * RANK, 32], [1, RANK]])))
    S.emit("gpsimd", lambda e: e.dma_start(
        bass.AP(v2p16, 0, [[2 * 1024, 128], [1024, 2], [1, 1024]]),
        bass.AP(d_v2, 0, [[1024, 128], [128 * 1024, 2], [1, 1024]])))
    m_wffn = S.marker("gpsimd")

    m_y16_used = ("tensor", 0)
    m_yt_used = ("vector", 0)
    m_yts_used = ("tensor", 0)
    m_p16_used = ("tensor", 0)
    m_h16_used = ("tensor", 0)
    m_q2t_used = ("tensor", 0)
    m_ft_used = ("tensor", 0)
    m_pab_used = ("vector", 0)
    m_pcd_used = ("scalar", 0)
    m_fs_used = ("vector", 0)
    m_outS_used = [("sync", 0), ("sync", 0)]
    n_store = 0

    for f in range(FSLAB):
        c0 = 2048 * f
        S.emit("vector", lambda e, c0=c0: e.tensor_copy(
            y16[:, :], u[:, c0: c0 + 2048]), deps=[m_y16_used])
        m_y16 = S.marker("vector")

        for hi in range(2):
            ytd = YTl if hi == 0 else YTh
            for blk in range(8):
                for gg in range(4):
                    gidx = 4 * blk + gg
                    S.emit("tensor", lambda e, hi=hi, gidx=gidx, gg=gg:
                           e.matmul(p_yt[:, 64 * gg: 64 * gg + 64],
                                    y16[64 * hi: 64 * hi + 64, 64 * gidx: 64 * gidx + 64],
                                    sb_id16[64 * hi: 64 * hi + 64, :],
                                    start=True, stop=True),
                           deps=([m_y16, m_yt_used, m_consts] if gg == 0 else []), inc=(gg == 3))
                m_ytp = S.marker("tensor")
                S.emit("vector", lambda e, ytd=ytd, blk=blk: e.tensor_copy(
                    ytd[:, 256 * blk: 256 * blk + 256], p_yt[:, :]),
                    deps=[m_ytp, m_yts_used])
                m_yt_used = S.marker("vector")
        m_yt_ready = S.marker("vector")

        for rh in range(2):
            ps = p_a if rh == 0 else p_b
            for hc in range(16):
                hi, ch = hc // 8, hc % 8
                yts = YTl if hi == 0 else YTh
                S.emit("tensor", lambda e, rh=rh, hi=hi, ch=ch, ps=ps, yts=yts, hc=hc:
                       e.matmul(ps[:, :],
                                u1p16[:, RANK * (ch + 8 * hi) + 128 * rh:
                                      RANK * (ch + 8 * hi) + 128 * rh + 128],
                                bass.AP(yts, ch, [[2048, 64], [64, 32], [8, 8]]),
                                start=(hc == 0), stop=(hc == 15)),
                       deps=([m_yt_ready, m_wffn, m_pab_used, m_consts] if hc == 0 else []),
                       inc=(hc == 15))
            m_pmm = S.marker("tensor")
            S.emit("vector", lambda e, rh=rh, ps=ps: e.tensor_copy(
                P16[:, 256 * rh: 256 * rh + 256], ps[:, :]), deps=[m_pmm, m_p16_used])
        m_pab_used = S.marker("vector")
        m_p16 = S.marker("vector")

        for ffc in range(32):
            ps = p_c if ffc % 2 == 0 else p_d
            for rc in range(2):
                S.emit("tensor", lambda e, ffc=ffc, rc=rc, ps=ps:
                       e.matmul(ps[:, :],
                                v116[:, DFF * rc + 128 * ffc: DFF * rc + 128 * ffc + 128],
                                P16[:, 256 * rc: 256 * rc + 256],
                                start=(rc == 0), stop=(rc == 1)),
                       deps=([m_p16, m_pcd_used] if rc == 0 else []), inc=(rc == 1))
            m_hmm = S.marker("tensor")
            S.emit("scalar", lambda e, ffc=ffc, ps=ps: e.activation(
                H16[:, 256 * ffc: 256 * ffc + 256], ps[:, :], AF.Gelu_apprx_tanh,
                bias=sb_b1[:, ffc: ffc + 1]), deps=[m_hmm, m_h16_used, m_consts])
            if ffc % 2 == 1:
                m_pcd_used = S.marker("scalar")
        m_h16 = S.marker("scalar")

        for rh in range(2):
            ps = p_a if rh == 0 else p_b
            for ffc in range(32):
                S.emit("tensor", lambda e, rh=rh, ffc=ffc, ps=ps:
                       e.matmul(ps[:, :],
                                u216[:, RANK * ffc + 128 * rh: RANK * ffc + 128 * rh + 128],
                                H16[:, 256 * ffc: 256 * ffc + 256],
                                start=(ffc == 0), stop=(ffc == 31)),
                       deps=([m_h16, m_pab_used] if ffc == 0 else []), inc=(ffc == 31))
            m_qmm = S.marker("tensor")
            S.emit("vector", lambda e, rh=rh, ps=ps: e.tensor_copy(
                Q2T[:, 256 * rh: 256 * rh + 256], ps[:, :]), deps=[m_qmm, m_q2t_used])
        m_pab_used = S.marker("vector")
        m_q2t = S.marker("vector")

        for ch in range(8):
            ps = p_c if ch % 2 == 0 else p_d
            for rc in range(2):
                S.emit("tensor", lambda e, ch=ch, rc=rc, ps=ps:
                       e.matmul(ps[:, :],
                                v2p16[:, 1024 * rc + 128 * ch: 1024 * rc + 128 * ch + 128],
                                Q2T[:, 256 * rc: 256 * rc + 256],
                                start=(rc == 0), stop=(rc == 1)),
                       deps=([m_q2t, m_pcd_used] if rc == 0 else []), inc=(rc == 1))
            m_fmm = S.marker("tensor")
            for half in range(2):
                rows = slice(64 * half, 64 * half + 64)
                S.emit("scalar", lambda e, ch=ch, half=half, rows=rows, ps=ps:
                       e.activation(
                           bass.AP(FTs, half * (64 * 2048) + ch,
                                   [[2048, 64], [1024, 2], [64, 16], [8, 8]]),
                           ps[rows, :], AF.Identity, bias=sb_b2[rows, ch: ch + 1]),
                       deps=[m_fmm, m_ft_used, m_consts])
            if ch % 2 == 1:
                m_pcd_used = S.marker("scalar")
        m_ft = S.marker("scalar")

        for tp in range(2):
            tile = 2 * f + tp
            for gb in range(4):
                for hi in range(2):
                    for gg in range(4):
                        g = 4 * gb + gg
                        S.emit("tensor", lambda e, hi=hi, g=g, gg=gg, tp=tp:
                               e.matmul(p_fs[64 * hi: 64 * hi + 64, 64 * gg: 64 * gg + 64],
                                        FTs[64 * hi: 64 * hi + 64,
                                            1024 * tp + 64 * g: 1024 * tp + 64 * g + 64],
                                        sb_idf[64 * hi: 64 * hi + 64, :],
                                        start=True, stop=True),
                               deps=([m_ft, m_fs_used, m_consts] if (hi == 0 and gg == 0) else []),
                               inc=(hi == 1 and gg == 3))
                m_fsmm = S.marker("tensor")
                S.emit("vector", lambda e, tile=tile, gb=gb: e.scalar_tensor_tensor(
                    u[:, 1024 * tile + 256 * gb: 1024 * tile + 256 * gb + 256],
                    p_fs[:, :], 1.0,
                    u[:, 1024 * tile + 256 * gb: 1024 * tile + 256 * gb + 256],
                    ALU.mult, ALU.add), deps=[m_fsmm])
                m_fs_used = S.marker("vector")


    for tile in range(NT):
        S.emit("vector", lambda e, tile=tile: e.reduce_sum(
            lnS[:, 0:16],
            bass.AP(u, 1024 * tile, [[NT * 1024, 128], [64, 16], [1, 64]]),
            mybir.AxisListType.X), deps=[m_fs_used])
        S.emit("scalar", lambda e, tile=tile: e.activation(
            usq[:, :], u[:, 1024 * tile: 1024 * tile + 1024], AF.Square),
            deps=[m_fs_used])
        m_usq = S.marker("scalar")
        S.emit("vector", lambda e: e.reduce_sum(
            lnS[:, 16:32], bass.AP(usq, 0, [[1024, 128], [64, 16], [1, 64]]),
            mybir.AxisListType.X), deps=[m_usq])
        m_s2 = S.marker("vector")
        S.emit("tensor", lambda e: e.matmul(
            p_st2[:, 0:32], sb_sel8[:, :], lnS[:, :], start=True, stop=True),
            deps=[m_s2, m_consts])
        m_stmm = S.marker("tensor")
        S.emit("vector", lambda e: e.tensor_scalar_mul(lnT[:, :], p_st2[:, 0:32], 1.0 / 1024.0),
               deps=[m_stmm])
        S.emit("vector", lambda e: e.tensor_mul(lnM[:, :], lnT[:, 0:16], lnT[:, 0:16]))
        S.emit("vector", lambda e: e.tensor_sub(lnM[:, :], lnT[:, 16:32], lnM[:, :]))
        m_var = S.marker("vector")
        S.emit("scalar", lambda e: e.activation(
            lnM[:, :], lnM[:, :], AF.Sqrt, bias=sb_eps[0:8, 0:1]), deps=[m_var, m_consts])
        m_std = S.marker("scalar")
        S.emit("vector", lambda e: e.reciprocal(lnT[:, 16:32], lnM[:, :]),
               deps=[m_std])
        S.emit("vector", lambda e: e.tensor_mul(lnT[:, 0:16], lnT[:, 0:16], lnT[:, 16:32]))
        m_mrs = S.marker("vector")
        S.emit("tensor", lambda e: e.matmul(
            p_bc2[:, 0:32], sb_sel8t[:, :], lnT[:, :], start=True, stop=True),
            deps=[m_mrs])
        m_bcmm = S.marker("tensor")
        S.emit("vector", lambda e: e.tensor_copy(bcS[:, :], p_bc2[:, 0:32]), deps=[m_bcmm])
        ob = tile % 2
        S.emit("vector", lambda e, tile=tile, ob=ob: e.tensor_tensor(
            bass.AP(outS, 1024 * ob, [[2048, 128], [64, 16], [1, 64]]),
            bass.AP(u, 1024 * tile, [[NT * 1024, 128], [64, 16], [1, 64]]),
            bass.AP(bcS, 16, [[32, 128], [1, 16], [0, 64]]), ALU.mult),
            deps=[m_outS_used[ob]])
        S.emit("vector", lambda e, ob=ob: e.tensor_tensor(
            bass.AP(outS, 1024 * ob, [[2048, 128], [64, 16], [1, 64]]),
            bass.AP(outS, 1024 * ob, [[2048, 128], [64, 16], [1, 64]]),
            bass.AP(bcS, 0, [[32, 128], [1, 16], [0, 64]]), ALU.subtract))
        m_norm = S.marker("vector")
        S.emit("sync", lambda e, tile=tile, ob=ob: e.dma_start(
            bass.AP(d_out, 128 * 1024 * tile, [[1024, 128], [1, 1024]]),
            outS[:, 1024 * ob: 1024 * ob + 1024]), deps=[m_norm])
        m_outS_used[ob] = S.marker("sync")

    total_sync = S.cnt["sync"]
    S.ops["sync"].append(lambda e, v=16 * total_sync: e.wait_ge(S.sems["sync"], v))

    with nc.Block() as block:
        @block.sync
        def _(eng):
            for fn in S.ops["sync"]:
                fn(eng)

        @block.gpsimd
        def _(eng):
            for fn in S.ops["gpsimd"]:
                fn(eng)

        @block.tensor
        def _(eng):
            for fn in S.ops["tensor"]:
                fn(eng)

        @block.vector
        def _(eng):
            for fn in S.ops["vector"]:
                fn(eng)

        @block.scalar
        def _(eng):
            for fn in S.ops["scalar"]:
                fn(eng)

    esB.close()
    es_top.close()
    return nc



def _rope_tables():
    freqs = (10000.0 ** (-(np.arange(0, DK, 2, dtype=np.float64) / D))).astype(np.float32)
    ang = (np.arange(NCORES * T, dtype=np.float32)[:, None] * freqs[None, :]).astype(np.float32)
    libm = ctypes.CDLL(ctypes.util.find_library("m"))
    libm.cosf.restype = ctypes.c_float
    libm.cosf.argtypes = [ctypes.c_float]
    libm.sinf.restype = ctypes.c_float
    libm.sinf.argtypes = [ctypes.c_float]
    flat = ang.ravel()
    cos = np.fromiter((libm.cosf(float(v)) for v in flat), np.float32, flat.size).reshape(ang.shape)
    sin = np.fromiter((libm.sinf(float(v)) for v in flat), np.float32, flat.size).reshape(ang.shape)
    return cos, sin


_NC_CACHE = {}


def kernel(L0_attn_x, L0_attn_Wq, L0_attn_bq, L0_attn_Wk, L0_attn_bk,
           L0_attn_Wv, L0_attn_bv, L0_ffn_U1, L0_ffn_V1, L0_ffn_b1,
           L0_ffn_U2, L0_ffn_V2, L0_ffn_b2):
    x = np.ascontiguousarray(np.asarray(L0_attn_x, dtype=np.float32).reshape(NCORES * T, D))

    perm = np.concatenate([
        np.concatenate([np.arange(64 * c, 64 * c + 64),
                        np.arange(64 * (c + 8), 64 * (c + 8) + 64)])
        for c in range(8)])
    wq = np.ascontiguousarray(np.asarray(L0_attn_Wq, np.float32)[:, perm])
    wk = np.ascontiguousarray(np.asarray(L0_attn_Wk, np.float32)[:, perm])
    wv = np.ascontiguousarray(np.asarray(L0_attn_Wv, np.float32)[:, perm])
    v2p = np.ascontiguousarray(np.asarray(L0_ffn_V2, np.float32)[:, perm])

    def bias_chunks(b):
        return np.ascontiguousarray(np.asarray(b, np.float32)[perm].reshape(8, 128).T)

    bqp, bkp, bvp = (bias_chunks(b) for b in (L0_attn_bq, L0_attn_bk, L0_attn_bv))
    b2p = bias_chunks(L0_ffn_b2)
    u1p = np.ascontiguousarray(
        np.asarray(L0_ffn_U1, np.float32).reshape(16, 64, RANK).transpose(1, 0, 2).reshape(64, 16 * RANK))
    v1r = np.ascontiguousarray(np.asarray(L0_ffn_V1, np.float32))
    u2r = np.ascontiguousarray(np.asarray(L0_ffn_U2, np.float32))
    b1p = np.ascontiguousarray(np.asarray(L0_ffn_b1, np.float32).reshape(32, 128).T)

    cos, sin = _rope_tables()

    mask01 = np.zeros((128, 128), np.float32)
    for r in range(128):
        for n in range(128):
            if ((r % 64) >> 3) == ((n % 64) >> 3):
                mask01[r, n] = 1.0
    mask01 = mask01.astype(ml_dtypes.bfloat16)
    sel8 = np.zeros((128, 8), np.float32)
    for row in range(128):
        sel8[row, (row % 64) >> 3] = 1.0
    sel8t = np.ascontiguousarray(sel8.T)
    id16 = np.concatenate([np.eye(64, dtype=ml_dtypes.bfloat16)] * 2, 0)
    idf = np.concatenate([np.eye(64, dtype=np.float32)] * 2, 0)
    epsc = np.full((128, 1), 1e-5, np.float32)

    if "nc" not in _NC_CACHE:
        _NC_CACHE["nc"] = build_nc()
    nc = _NC_CACHE["nc"]

    in_maps = []
    for c in range(NCORES):
        xc = x[T * c: T * (c + 1)]
        xt = np.ascontiguousarray(xc.T)
        v = xc.reshape(NT, 16, 8, 2, 8, 64)
        xs = np.ascontiguousarray(v.transpose(0, 3, 2, 4, 1, 5).reshape(T, D))
        cos128 = np.ascontiguousarray(
            np.tile(np.repeat(cos[T * c: T * (c + 1)].T, 2, axis=0), (2, 1)))
        sin128 = np.ascontiguousarray(
            np.tile(np.repeat(sin[T * c: T * (c + 1)].T, 2, axis=0), (2, 1)))
        in_maps.append({
            "xt": xt, "xs": xs, "wq": wq, "wk": wk, "wv": wv,
            "bqp": bqp, "bkp": bkp, "bvp": bvp,
            "u1p": u1p, "v1r": v1r, "b1p": b1p, "u2r": u2r, "v2p": v2p, "b2p": b2p,
            "cos128": cos128, "sin128": sin128,
            "mask01": mask01, "sel8": sel8, "sel8t": sel8t,
            "id16": id16, "idf": idf, "epsc": epsc,
        })

    res = run_bass_kernel_spmd(nc, in_maps, core_ids=list(range(NCORES)))

    out = np.empty((NCORES * T, D), np.float32)
    for c in range(NCORES):
        o = np.asarray(res.results[c]["outs"], np.float32)
        v = o.reshape(NT, 2, 8, 8, 16, 64)
        out[T * c: T * (c + 1)] = v.transpose(0, 4, 2, 1, 3, 5).reshape(T, D)
    return out.reshape(1, NCORES * T, D)


# version 06
# speedup vs baseline: 1.2074x; 1.0683x over previous
import ctypes
import ctypes.util
from contextlib import ExitStack

import numpy as np
import ml_dtypes

import concourse.bass as bass
import concourse.mybir as mybir
from concourse.bass_utils import run_bass_kernel_spmd

F32 = mybir.dt.float32
BF16 = mybir.dt.bfloat16
AF = mybir.ActivationFunctionType
ALU = mybir.AluOpType

NCORES = 8
T = 2048
NT = 16
D = 1024
NH = 16
DK = 64
RANK = 256
DFF = 4096
NSLAB = 4
FSLAB = 8

ENGS = ("sync", "gpsimd", "tensor", "vector", "scalar")
DEBUG_A = False


class Sched:

    def __init__(self, nc, es):
        self.nc = nc
        self.ops = {e: [] for e in ENGS}
        self.cnt = {e: 0 for e in ENGS}
        self.hw = {e: {s: 0 for s in ENGS} for e in ENGS}
        self.sems = {e: es.enter_context(nc.semaphore(f"sem_{e}")) for e in ENGS}

    def mult(self, e):
        return 16 if e in ("sync", "gpsimd") else 1

    def throttle(self, eng):
        v = self.cnt[eng] * self.mult(eng)
        if v > 0:
            sem = self.sems[eng]
            self.ops[eng].append(lambda e, sem=sem, v=v: e.wait_ge(sem, v))

    def wait(self, eng, dep):
        src, val = dep
        if src == eng or val <= 0:
            return
        if self.hw[eng][src] >= val:
            return
        self.hw[eng][src] = val
        sem = self.sems[src]
        v = val * self.mult(src)
        self.ops[eng].append(lambda e, sem=sem, v=v: e.wait_ge(sem, v))

    def emit(self, eng, fn, deps=(), inc=True):
        for d in deps:
            self.wait(eng, d)
        if eng in ("sync", "gpsimd"):
            self.throttle(eng)
        elif inc and eng != "tensor" and self.cnt[eng] > 0:
            sem, v = self.sems[eng], self.cnt[eng]
            self.ops[eng].append(lambda e, sem=sem, v=v: e.wait_ge(sem, v))
        if inc:
            self.cnt[eng] += 1
            sem, m = self.sems[eng], self.mult(eng)
            self.ops[eng].append(lambda e, fn=fn, sem=sem, m=m: fn(e).then_inc(sem, m))
        else:
            self.ops[eng].append(fn)
        return (eng, self.cnt[eng])

    def marker(self, eng):
        return (eng, self.cnt[eng])


def build_nc():
    nc = bass.Bass("TRN2", target_bir_lowering=False)
    es_top = ExitStack()

    d_xt = nc.dram_tensor("xt", [D, T], F32, kind="ExternalInput")
    d_xs = nc.dram_tensor("xs", [T, D], F32, kind="ExternalInput")
    d_wq = nc.dram_tensor("wq", [D, D], F32, kind="ExternalInput")
    d_wk = nc.dram_tensor("wk", [D, D], F32, kind="ExternalInput")
    d_wv = nc.dram_tensor("wv", [D, D], F32, kind="ExternalInput")
    d_bq = nc.dram_tensor("bqp", [128, 8], F32, kind="ExternalInput")
    d_bk = nc.dram_tensor("bkp", [128, 8], F32, kind="ExternalInput")
    d_bv = nc.dram_tensor("bvp", [128, 8], F32, kind="ExternalInput")
    d_u1 = nc.dram_tensor("u1p", [64, 16 * RANK], F32, kind="ExternalInput")
    d_v1 = nc.dram_tensor("v1r", [RANK, DFF], F32, kind="ExternalInput")
    d_b1 = nc.dram_tensor("b1p", [128, 32], F32, kind="ExternalInput")
    d_u2 = nc.dram_tensor("u2r", [DFF, RANK], F32, kind="ExternalInput")
    d_v2 = nc.dram_tensor("v2p", [RANK, D], F32, kind="ExternalInput")
    d_b2 = nc.dram_tensor("b2p", [128, 8], F32, kind="ExternalInput")
    d_cos = nc.dram_tensor("cos128", [128, T], F32, kind="ExternalInput")
    d_sin = nc.dram_tensor("sin128", [128, T], F32, kind="ExternalInput")
    d_mask = nc.dram_tensor("mask01", [128, 128], BF16, kind="ExternalInput")
    d_sel8 = nc.dram_tensor("sel8", [128, 8], F32, kind="ExternalInput")
    d_sel8t = nc.dram_tensor("sel8t", [8, 128], F32, kind="ExternalInput")
    d_id16 = nc.dram_tensor("id16", [128, 64], BF16, kind="ExternalInput")
    d_idf = nc.dram_tensor("idf", [128, 64], F32, kind="ExternalInput")
    d_eps = nc.dram_tensor("epsc", [128, 1], F32, kind="ExternalInput")
    d_out = nc.dram_tensor("outs", [T, D], F32, kind="ExternalOutput")

    u = es_top.enter_context(nc.sbuf_tensor("u", [128, NT * 1024], F32))
    usq = es_top.enter_context(nc.sbuf_tensor("usq", [128, 1024], F32))
    w128 = es_top.enter_context(nc.sbuf_tensor("w128", [128, T], F32))
    sb_bq = es_top.enter_context(nc.sbuf_tensor("sb_bq", [128, 8], F32))
    sb_bk = es_top.enter_context(nc.sbuf_tensor("sb_bk", [128, 8], F32))
    sb_bv = es_top.enter_context(nc.sbuf_tensor("sb_bv", [128, 8], F32))
    sb_b1 = es_top.enter_context(nc.sbuf_tensor("sb_b1", [128, 32], F32))
    sb_b2 = es_top.enter_context(nc.sbuf_tensor("sb_b2", [128, 8], F32))
    sb_mask = es_top.enter_context(nc.sbuf_tensor("sb_mask", [128, 128], BF16))
    sb_sel8 = es_top.enter_context(nc.sbuf_tensor("sb_sel8", [128, 8], F32))
    sb_sel8t = es_top.enter_context(nc.sbuf_tensor("sb_sel8t", [8, 128], F32))
    sb_id16 = es_top.enter_context(nc.sbuf_tensor("sb_id16", [128, 64], BF16))
    sb_idf = es_top.enter_context(nc.sbuf_tensor("sb_idf", [128, 64], F32))
    sb_eps = es_top.enter_context(nc.sbuf_tensor("sb_eps", [128, 1], F32))
    lnS = es_top.enter_context(nc.sbuf_tensor("lnS", [128, 32], F32))
    lnT = es_top.enter_context(nc.sbuf_tensor("lnT", [8, 32], F32))
    lnM = es_top.enter_context(nc.sbuf_tensor("lnM", [8, 16], F32))
    bcS = es_top.enter_context(nc.sbuf_tensor("bcS", [128, 32], F32))
    zrec = es_top.enter_context(nc.sbuf_tensor("zrec", [128, 4], F32))
    outS = es_top.enter_context(nc.sbuf_tensor("outS", [128, 2 * 1024], F32))

    S = Sched(nc, es_top)

    esA = ExitStack()
    w16 = {t: esA.enter_context(nc.sbuf_tensor(f"w16_{t}", [128, 8 * D], BF16))
           for t in "qkv"}
    xt16 = [esA.enter_context(nc.sbuf_tensor(f"xt16_{i}", [128, 8 * 512], BF16))
            for i in range(2)]
    Qs = esA.enter_context(nc.sbuf_tensor("Qs", [128, 4096], BF16))
    Ks = esA.enter_context(nc.sbuf_tensor("Ks", [128, 4096], BF16))
    Q2s = esA.enter_context(nc.sbuf_tensor("Q2s", [128, 4096], BF16))
    VTs = esA.enter_context(nc.sbuf_tensor("VTs", [128, 4096], BF16))
    E2 = esA.enter_context(nc.sbuf_tensor("E2", [128, 2048], BF16))
    Vs = esA.enter_context(nc.sbuf_tensor("Vs", [128, 16 * 65], BF16))

    p_qkv = [esA.enter_context(nc.psum_tensor(f"p_qkv{i}", [128, 512], F32)) for i in range(2)]
    p_sc = [esA.enter_context(nc.psum_tensor(f"p_sc{i}", [128, 512], F32)) for i in range(2)]
    p_avs = [esA.enter_context(nc.psum_tensor(f"p_av{i}", [128, 260], F32)) for i in range(2)]
    p_tr = esA.enter_context(nc.psum_tensor("p_tr", [128, 256], F32))
    p_stbc = esA.enter_context(nc.psum_tensor("p_stbc", [128, 64], F32))

    for dst, src in [(sb_bq, d_bq), (sb_bk, d_bk), (sb_bv, d_bv), (sb_b1, d_b1),
                     (sb_b2, d_b2), (sb_mask, d_mask), (sb_sel8, d_sel8),
                     (sb_sel8t, d_sel8t), (sb_id16, d_id16), (sb_idf, d_idf),
                     (sb_eps, d_eps), (w128, d_cos)]:
        S.emit("sync", lambda e, dst=dst, src=src: e.dma_start(dst[:, :], src.ap()))
    m_consts = S.marker("sync")
    S.emit("sync", lambda e: e.dma_start(
        bass.AP(u, 0, [[NT * 1024, 128], [1024, NT], [1, 1024]]),
        bass.AP(d_xs, 0, [[1024, 128], [128 * 1024, NT], [1, 1024]])))
    m_xs = S.marker("sync")

    for t, d_w in [("q", d_wq), ("k", d_wk), ("v", d_wv)]:
        S.emit("gpsimd", lambda e, t=t, d_w=d_w: e.dma_start(
            bass.AP(w16[t], 0, [[8 * D, 128], [D, 8], [1, D]]),
            bass.AP(d_w, 0, [[D, 128], [128 * D, 8], [1, D]]),
        ))
    m_wqkv = S.marker("gpsimd")

    S.emit("vector", lambda e: e.tensor_mul(w128[:, :], w128[:, :], w128[:, :]),
           deps=[m_consts])
    m_usq_free = ("vector", 0)
    for h in range(2):
        S.emit("sync", lambda e, h=h: e.dma_start(
            usq[:, :], bass.AP(d_sin, 1024 * h, [[T, 128], [1, 1024]])),
            deps=[m_usq_free])
        m_sinh = S.marker("sync")
        S.emit("vector", lambda e: e.tensor_mul(usq[:, :], usq[:, :], usq[:, :]),
               deps=[m_sinh])
        S.emit("vector", lambda e, h=h: e.tensor_add(
            w128[:, 1024 * h: 1024 * h + 1024],
            w128[:, 1024 * h: 1024 * h + 1024], usq[:, :]))
        m_usq_free = S.marker("vector")
    m_w128 = S.marker("vector")

    S.emit("vector", lambda e: e.memset(bass.AP(Vs, 64, [[16 * 65, 128], [65, 16]]), 1.0))
    m_vsones = S.marker("vector")

    m_xt_used = [("tensor", 0), ("tensor", 0)]
    m_stacks_used = ("tensor", 0)
    m_evac_done_v = ("vector", 0)
    m_evac_done_s = ("scalar", 0)
    m_psum_used = [("vector", 0), ("vector", 0)]
    m_q2_dma = ("sync", 0)
    m_e2_used = ("tensor", 0)
    m_vs_used = ("tensor", 0)
    m_av_used = [("vector", 0), ("vector", 0)]
    m_sc_used = [("scalar", 0), ("scalar", 0)]
    m_tr_used = {"vector": ("vector", 0), "scalar": ("scalar", 0)}
    sync_q_prev = ("sync", 0)

    for s in range(NSLAB):
        buf = s % 2
        S.emit("gpsimd", lambda e, s=s, buf=buf: e.dma_start(
            bass.AP(xt16[buf], 0, [[8 * 512, 128], [512, 8], [1, 512]]),
            bass.AP(d_xt, 512 * s, [[T, 128], [128 * T, 8], [1, 512]]),
        ), deps=[m_xt_used[buf]])
        m_xt = S.marker("gpsimd")

        pp = 0
        for tname in "qkv":
            for ch in range(8):
                ps = p_qkv[pp % 2]
                dep_evac = m_psum_used[pp % 2]
                for kc in range(8):
                    S.emit("tensor", lambda e, tname=tname, ch=ch, kc=kc, ps=ps, buf=buf:
                           e.matmul(ps[:, :],
                                    w16[tname][:, D * kc + 128 * ch: D * kc + 128 * ch + 128],
                                    xt16[buf][:, 512 * kc: 512 * kc + 512],
                                    start=(kc == 0), stop=(kc == 7)),
                           deps=([m_wqkv, m_xt, dep_evac] if kc == 0 else []),
                           inc=(kc == 7))
                m_mm = S.marker("tensor")
                for half in range(2):
                    rows = slice(64 * half, 64 * half + 64)
                    ps_ap = lambda ps=ps, half=half: bass.AP(
                        ps, half * 64 * 512, [[512, 64], [128, 4], [1, 128]])
                    if tname == "k":
                        S.emit("vector", lambda e, ps_ap=ps_ap, rows=rows, ch=ch, half=half, s=s:
                               e.scalar_tensor_tensor(
                                   bass.AP(Ks, half * (64 * 4096) + ch,
                                           [[4096, 64], [1024, 4], [8, 128]]),
                                   ps_ap(),
                                   sb_bk[rows, ch: ch + 1],
                                   bass.AP(w128, half * 64 * T + 512 * s,
                                           [[T, 64], [128, 4], [1, 128]]),
                                   ALU.add, ALU.mult),
                               deps=[m_mm, m_w128, m_stacks_used, m_q2_dma, m_consts])
                    else:
                        dst = Qs if tname == "q" else VTs
                        bt = sb_bq if tname == "q" else sb_bv
                        S.emit("scalar", lambda e, ps_ap=ps_ap, rows=rows, dst=dst, ch=ch, half=half, bt=bt:
                               e.activation(
                                   bass.AP(dst, half * (64 * 4096) + ch,
                                           [[4096, 64], [1024, 4], [8, 128]]),
                                   ps_ap(), AF.Identity, bias=bt[rows, ch: ch + 1]),
                               deps=[m_mm, m_stacks_used, m_q2_dma, m_consts])
                if tname == "k":
                    m_evac_done_v = S.marker("vector")
                    m_psum_used[pp % 2] = m_evac_done_v
                else:
                    m_evac_done_s = S.marker("scalar")
                    m_psum_used[pp % 2] = m_evac_done_s
                pp += 1
        m_qevac = m_evac_done_s
        m_kevac = m_evac_done_v

        S.wait("sync", sync_q_prev)
        S.emit("sync", lambda e: e.dma_start(Q2s[0:64, :], Qs[64:128, :]),
               deps=[m_qevac, m_stacks_used])
        S.emit("sync", lambda e: e.dma_start(Q2s[64:128, :], Qs[0:64, :]))
        m_q2_dma = S.marker("sync")
        sync_q_prev = m_q2_dma

        for ti in range(4):
            tile = 4 * s + ti
            w0 = 1024 * ti

            for gb in range(4):
                for hi in range(2):
                    for gg in range(4):
                        g = 4 * gb + gg
                        S.emit("tensor", lambda e, hi=hi, g=g, gg=gg, w0=w0:
                               e.matmul(p_tr[64 * hi: 64 * hi + 64, 64 * gg: 64 * gg + 64],
                                        VTs[64 * hi: 64 * hi + 64, w0 + 64 * g: w0 + 64 * g + 64],
                                        sb_id16[64 * hi: 64 * hi + 64, :],
                                        start=True, stop=True),
                               deps=([m_kevac, m_qevac, m_consts,
                                      m_tr_used["vector"]] if (gg == 0 and hi == 0) else []),
                               inc=(gg == 3 and hi == 1))
                m_trp = S.marker("tensor")
                S.emit("vector", lambda e, gb=gb:
                       e.tensor_copy(
                           bass.AP(Vs, 65 * 4 * gb, [[16 * 65, 128], [65, 4], [1, 64]]),
                           p_tr[:, 0:256]),
                       deps=[m_trp, m_vsones, m_av_used[0], m_av_used[1]])
                m_tr_used["vector"] = S.marker("vector")

            m_vs_ready = S.marker("vector")

            def emit_scores(b):
                psb = p_sc[b % 2]
                for q4 in range(4):
                    g = 4 * b + q4
                    wg = w0 + 64 * g
                    S.emit("tensor", lambda e, wg=wg, q4=q4, psb=psb:
                           e.matmul(psb[0:64, 128 * q4: 128 * q4 + 64],
                                    Ks[0:64, wg: wg + 64], Qs[0:64, wg: wg + 64],
                                    start=True, stop=True),
                           deps=([m_q2_dma, m_sc_used[b % 2]] if q4 == 0 else []), inc=False)
                    S.emit("tensor", lambda e, wg=wg, q4=q4, psb=psb:
                           e.matmul(psb[0:64, 128 * q4 + 64: 128 * q4 + 128],
                                    Ks[0:64, wg: wg + 64], Q2s[0:64, wg: wg + 64],
                                    start=True, stop=True), inc=False)
                    S.emit("tensor", lambda e, wg=wg, q4=q4, psb=psb:
                           e.matmul(psb[64:128, 128 * q4: 128 * q4 + 64],
                                    Ks[64:128, wg: wg + 64], Q2s[64:128, wg: wg + 64],
                                    start=True, stop=True), inc=False)
                    S.emit("tensor", lambda e, wg=wg, q4=q4, psb=psb:
                           e.matmul(psb[64:128, 128 * q4 + 64: 128 * q4 + 128],
                                    Ks[64:128, wg: wg + 64], Qs[64:128, wg: wg + 64],
                                    start=True, stop=True), inc=(q4 == 3))
                m = S.marker("tensor")
                S.emit("scalar", lambda e, b=b, psb=psb: e.activation(
                    E2[:, 512 * b: 512 * b + 512], psb[:, :], AF.Exp, scale=0.125),
                    deps=[m, m_e2_used])
                m_sc_used[b % 2] = S.marker("scalar")
                S.emit("vector", lambda e, b=b: e.tensor_tensor(
                    E2[:, 512 * b: 512 * b + 512], E2[:, 512 * b: 512 * b + 512],
                    bass.AP(sb_mask, 0, [[128, 128], [0, 4], [1, 128]]), ALU.mult),
                    deps=[m_sc_used[b % 2], m_consts])
                return S.marker("vector")

            def emit_av(b, m_mask_b):
                pav = p_avs[b % 2]
                for q4 in range(4):
                    g = 4 * b + q4
                    S.emit("tensor", lambda e, g=g, q4=q4, pav=pav:
                           e.matmul(pav[:, 65 * q4: 65 * q4 + 65],
                                    E2[:, 128 * g: 128 * g + 128],
                                    Vs[:, 65 * g: 65 * g + 65],
                                    start=True, stop=True),
                           deps=([m_mask_b, m_vs_ready, m_av_used[b % 2]] if q4 == 0 else []),
                           inc=(q4 == 3))
                m_avmm = S.marker("tensor")
                S.emit("vector", lambda e, pav=pav: e.reciprocal(
                    zrec[:, 0:4], bass.AP(pav, 64, [[260, 128], [65, 4]])),
                    deps=[m_avmm, m_xs])
                for q4 in range(4):
                    g = 4 * b + q4
                    S.emit("vector", lambda e, g=g, q4=q4, tile=tile, pav=pav:
                           e.scalar_tensor_tensor(
                               u[:, 1024 * tile + 64 * g: 1024 * tile + 64 * g + 64],
                               pav[:, 65 * q4: 65 * q4 + 64],
                               zrec[:, q4: q4 + 1],
                               u[:, 1024 * tile + 64 * g: 1024 * tile + 64 * g + 64],
                               ALU.mult, ALU.add))
                m_av_used[b % 2] = S.marker("vector")
                return m_avmm

            m_masks = {}
            m_masks[0] = emit_scores(0)
            m_masks[1] = emit_scores(1)
            last_avmm = None
            for b in range(4):
                last_avmm = emit_av(b, m_masks[b])
                if b + 2 < 4:
                    m_masks[b + 2] = emit_scores(b + 2)
            m_e2_used = last_avmm


        m_stacks_used = S.marker("tensor")
        m_xt_used[buf] = S.marker("tensor")


    for tile in range(NT):
        S.emit("vector", lambda e, tile=tile: e.reduce_sum(
            lnS[:, 0:16],
            bass.AP(u, 1024 * tile, [[NT * 1024, 128], [64, 16], [1, 64]]),
            mybir.AxisListType.X))
        m_s1 = S.marker("vector")
        S.emit("scalar", lambda e, tile=tile: e.activation(
            usq[:, :], u[:, 1024 * tile: 1024 * tile + 1024], AF.Square),
            deps=[m_av_used[0], m_av_used[1]])
        m_usq = S.marker("scalar")
        S.emit("vector", lambda e: e.reduce_sum(
            lnS[:, 16:32], bass.AP(usq, 0, [[1024, 128], [64, 16], [1, 64]]),
            mybir.AxisListType.X), deps=[m_usq])
        m_s2 = S.marker("vector")
        S.emit("tensor", lambda e: e.matmul(
            p_stbc[0:8, 0:32], sb_sel8[:, :], lnS[:, :], start=True, stop=True),
            deps=[m_s2, m_consts])
        m_stmm = S.marker("tensor")
        S.emit("vector", lambda e: e.tensor_scalar_mul(lnT[:, :], p_stbc[0:8, 0:32], 1.0 / 1024.0),
               deps=[m_stmm])
        S.emit("vector", lambda e: e.tensor_mul(lnM[:, :], lnT[:, 0:16], lnT[:, 0:16]))
        S.emit("vector", lambda e: e.tensor_sub(lnM[:, :], lnT[:, 16:32], lnM[:, :]))
        m_var = S.marker("vector")
        S.emit("scalar", lambda e: e.activation(
            lnM[:, :], lnM[:, :], AF.Sqrt, bias=sb_eps[0:8, 0:1]), deps=[m_var, m_consts])
        m_std = S.marker("scalar")
        S.emit("vector", lambda e: e.reciprocal(lnT[:, 16:32], lnM[:, :]),
               deps=[m_std])
        S.emit("vector", lambda e: e.tensor_mul(lnT[:, 0:16], lnT[:, 0:16], lnT[:, 16:32]))
        m_mrs = S.marker("vector")
        S.emit("tensor", lambda e: e.matmul(
            p_stbc[:, 32:64], sb_sel8t[:, :], lnT[:, :], start=True, stop=True),
            deps=[m_mrs])
        m_bcmm = S.marker("tensor")
        S.emit("vector", lambda e: e.tensor_copy(bcS[:, :], p_stbc[:, 32:64]), deps=[m_bcmm])
        S.emit("vector", lambda e, tile=tile: e.tensor_tensor(
            bass.AP(u, 1024 * tile, [[NT * 1024, 128], [64, 16], [1, 64]]),
            bass.AP(u, 1024 * tile, [[NT * 1024, 128], [64, 16], [1, 64]]),
            bass.AP(bcS, 16, [[32, 128], [1, 16], [0, 64]]), ALU.mult))
        S.emit("vector", lambda e, tile=tile: e.tensor_tensor(
            bass.AP(u, 1024 * tile, [[NT * 1024, 128], [64, 16], [1, 64]]),
            bass.AP(u, 1024 * tile, [[NT * 1024, 128], [64, 16], [1, 64]]),
            bass.AP(bcS, 0, [[32, 128], [1, 16], [0, 64]]), ALU.subtract))

    m_attn_pe = S.marker("tensor")
    m_attn_v = S.marker("vector")
    m_attn_s = S.marker("scalar")

    esA.close()

    if DEBUG_A:
        total_sync = S.cnt["sync"]
        if total_sync:
            S.ops["sync"].append(lambda e, v=16 * total_sync: e.wait_ge(S.sems["sync"], v))
        with nc.Block() as block:
            @block.sync
            def _(eng):
                for fn in S.ops["sync"]:
                    fn(eng)

            @block.gpsimd
            def _(eng):
                for fn in S.ops["gpsimd"]:
                    fn(eng)

            @block.tensor
            def _(eng):
                for fn in S.ops["tensor"]:
                    fn(eng)

            @block.vector
            def _(eng):
                for fn in S.ops["vector"]:
                    fn(eng)

            @block.scalar
            def _(eng):
                for fn in S.ops["scalar"]:
                    fn(eng)
        esA.close()
        es_top.close()
        return nc
    esB = ExitStack()
    u1p16 = esB.enter_context(nc.sbuf_tensor("u1p16", [64, 16 * RANK], BF16))
    v116 = esB.enter_context(nc.sbuf_tensor("v116", [128, 2 * DFF], BF16))
    u216 = esB.enter_context(nc.sbuf_tensor("u216", [128, 32 * RANK], BF16))
    v2p16 = esB.enter_context(nc.sbuf_tensor("v2p16", [128, 2 * 1024], BF16))
    y16 = esB.enter_context(nc.sbuf_tensor("y16", [128, 2048], BF16))
    YTl = esB.enter_context(nc.sbuf_tensor("YTl", [64, 2048], BF16))
    YTh = esB.enter_context(nc.sbuf_tensor("YTh", [64, 2048], BF16))
    P16 = esB.enter_context(nc.sbuf_tensor("P16", [128, 512], BF16))
    H16 = esB.enter_context(nc.sbuf_tensor("H16", [128, 32 * 256], BF16))
    Q2T = esB.enter_context(nc.sbuf_tensor("Q2T", [128, 512], BF16))
    FTs = esB.enter_context(nc.sbuf_tensor("FTs", [128, 2048], F32))

    p_a = esB.enter_context(nc.psum_tensor("p_a", [128, 256], F32))
    p_b = esB.enter_context(nc.psum_tensor("p_b", [128, 256], F32))
    p_c = esB.enter_context(nc.psum_tensor("p_c", [128, 256], F32))
    p_d = esB.enter_context(nc.psum_tensor("p_d", [128, 256], F32))
    p_fs = esB.enter_context(nc.psum_tensor("p_fs", [128, 256], F32))
    p_yt = esB.enter_context(nc.psum_tensor("p_yt", [64, 256], F32))
    p_st2 = esB.enter_context(nc.psum_tensor("p_st2", [8, 32], F32))
    p_bc2 = esB.enter_context(nc.psum_tensor("p_bc2", [128, 32], F32))

    gate = [m_attn_pe, m_attn_v, m_attn_s]
    S.emit("gpsimd", lambda e: e.dma_start(u1p16[:, :], d_u1.ap()), deps=gate)
    S.emit("gpsimd", lambda e: e.dma_start(
        bass.AP(v116, 0, [[2 * DFF, 128], [DFF, 2], [1, DFF]]),
        bass.AP(d_v1, 0, [[DFF, 128], [128 * DFF, 2], [1, DFF]])))
    S.emit("gpsimd", lambda e: e.dma_start(
        bass.AP(u216, 0, [[32 * RANK, 128], [RANK, 32], [1, RANK]]),
        bass.AP(d_u2, 0, [[RANK, 128], [128 * RANK, 32], [1, RANK]])))
    S.emit("gpsimd", lambda e: e.dma_start(
        bass.AP(v2p16, 0, [[2 * 1024, 128], [1024, 2], [1, 1024]]),
        bass.AP(d_v2, 0, [[1024, 128], [128 * 1024, 2], [1, 1024]])))
    m_wffn = S.marker("gpsimd")

    m_y16_used = ("tensor", 0)
    m_yt_used = ("vector", 0)
    m_yts_used = ("tensor", 0)
    m_p16_used = ("tensor", 0)
    m_h16_used = ("tensor", 0)
    m_q2t_used = ("tensor", 0)
    m_ft_used = ("tensor", 0)
    m_pab_used = ("vector", 0)
    m_pcd_used = ("scalar", 0)
    m_fs_used = ("vector", 0)
    m_outS_used = [("sync", 0), ("sync", 0)]
    n_store = 0

    for f in range(FSLAB):
        c0 = 2048 * f
        S.emit("vector", lambda e, c0=c0: e.tensor_copy(
            y16[:, :], u[:, c0: c0 + 2048]), deps=[m_y16_used])
        m_y16 = S.marker("vector")

        for hi in range(2):
            ytd = YTl if hi == 0 else YTh
            for blk in range(8):
                for gg in range(4):
                    gidx = 4 * blk + gg
                    S.emit("tensor", lambda e, hi=hi, gidx=gidx, gg=gg:
                           e.matmul(p_yt[:, 64 * gg: 64 * gg + 64],
                                    y16[64 * hi: 64 * hi + 64, 64 * gidx: 64 * gidx + 64],
                                    sb_id16[64 * hi: 64 * hi + 64, :],
                                    start=True, stop=True),
                           deps=([m_y16, m_yt_used, m_consts] if gg == 0 else []), inc=(gg == 3))
                m_ytp = S.marker("tensor")
                S.emit("vector", lambda e, ytd=ytd, blk=blk: e.tensor_copy(
                    ytd[:, 256 * blk: 256 * blk + 256], p_yt[:, :]),
                    deps=[m_ytp, m_yts_used])
                m_yt_used = S.marker("vector")
        m_yt_ready = S.marker("vector")

        for rh in range(2):
            ps = p_a if rh == 0 else p_b
            for hc in range(16):
                hi, ch = hc // 8, hc % 8
                yts = YTl if hi == 0 else YTh
                S.emit("tensor", lambda e, rh=rh, hi=hi, ch=ch, ps=ps, yts=yts, hc=hc:
                       e.matmul(ps[:, :],
                                u1p16[:, RANK * (ch + 8 * hi) + 128 * rh:
                                      RANK * (ch + 8 * hi) + 128 * rh + 128],
                                bass.AP(yts, ch, [[2048, 64], [64, 32], [8, 8]]),
                                start=(hc == 0), stop=(hc == 15)),
                       deps=([m_yt_ready, m_wffn, m_pab_used, m_consts] if hc == 0 else []),
                       inc=(hc == 15))
            m_pmm = S.marker("tensor")
            S.emit("vector", lambda e, rh=rh, ps=ps: e.tensor_copy(
                P16[:, 256 * rh: 256 * rh + 256], ps[:, :]), deps=[m_pmm, m_p16_used])
        m_pab_used = S.marker("vector")
        m_p16 = S.marker("vector")

        h_rot = [p_a, p_b, p_c, p_d]
        h_used = [m_pab_used, m_pab_used, m_pcd_used, m_pcd_used]
        for ffc in range(32):
            ps = h_rot[ffc % 4]
            for rc in range(2):
                S.emit("tensor", lambda e, ffc=ffc, rc=rc, ps=ps:
                       e.matmul(ps[:, :],
                                v116[:, DFF * rc + 128 * ffc: DFF * rc + 128 * ffc + 128],
                                P16[:, 256 * rc: 256 * rc + 256],
                                start=(rc == 0), stop=(rc == 1)),
                       deps=([m_p16, h_used[ffc % 4]] if rc == 0 else []), inc=(rc == 1))
            m_hmm = S.marker("tensor")
            S.emit("scalar", lambda e, ffc=ffc, ps=ps: e.activation(
                H16[:, 256 * ffc: 256 * ffc + 256], ps[:, :], AF.Gelu_apprx_tanh,
                bias=sb_b1[:, ffc: ffc + 1]), deps=[m_hmm, m_h16_used, m_consts])
            h_used[ffc % 4] = S.marker("scalar")
            if ffc % 2 == 1:
                m_pcd_used = S.marker("scalar")
        m_h16 = S.marker("scalar")

        for rh in range(2):
            ps = p_a if rh == 0 else p_b
            for ffc in range(32):
                S.emit("tensor", lambda e, rh=rh, ffc=ffc, ps=ps:
                       e.matmul(ps[:, :],
                                u216[:, RANK * ffc + 128 * rh: RANK * ffc + 128 * rh + 128],
                                H16[:, 256 * ffc: 256 * ffc + 256],
                                start=(ffc == 0), stop=(ffc == 31)),
                       deps=([m_h16, m_pab_used] if ffc == 0 else []), inc=(ffc == 31))
            m_qmm = S.marker("tensor")
            S.emit("vector", lambda e, rh=rh, ps=ps: e.tensor_copy(
                Q2T[:, 256 * rh: 256 * rh + 256], ps[:, :]), deps=[m_qmm, m_q2t_used])
        m_pab_used = S.marker("vector")
        m_q2t = S.marker("vector")

        for ch in range(8):
            ps = p_c if ch % 2 == 0 else p_d
            for rc in range(2):
                S.emit("tensor", lambda e, ch=ch, rc=rc, ps=ps:
                       e.matmul(ps[:, :],
                                v2p16[:, 1024 * rc + 128 * ch: 1024 * rc + 128 * ch + 128],
                                Q2T[:, 256 * rc: 256 * rc + 256],
                                start=(rc == 0), stop=(rc == 1)),
                       deps=([m_q2t, m_pcd_used] if rc == 0 else []), inc=(rc == 1))
            m_fmm = S.marker("tensor")
            for half in range(2):
                rows = slice(64 * half, 64 * half + 64)
                S.emit("scalar", lambda e, ch=ch, half=half, rows=rows, ps=ps:
                       e.activation(
                           bass.AP(FTs, half * (64 * 2048) + ch,
                                   [[2048, 64], [1024, 2], [8, 128]]),
                           ps[rows, :], AF.Identity, bias=sb_b2[rows, ch: ch + 1]),
                       deps=[m_fmm, m_ft_used, m_consts])
            if ch % 2 == 1:
                m_pcd_used = S.marker("scalar")
        m_ft = S.marker("scalar")

        for tp in range(2):
            tile = 2 * f + tp
            for gb in range(4):
                for hi in range(2):
                    for gg in range(4):
                        g = 4 * gb + gg
                        S.emit("tensor", lambda e, hi=hi, g=g, gg=gg, tp=tp:
                               e.matmul(p_fs[64 * hi: 64 * hi + 64, 64 * gg: 64 * gg + 64],
                                        FTs[64 * hi: 64 * hi + 64,
                                            1024 * tp + 64 * g: 1024 * tp + 64 * g + 64],
                                        sb_idf[64 * hi: 64 * hi + 64, :],
                                        start=True, stop=True),
                               deps=([m_ft, m_fs_used, m_consts] if (hi == 0 and gg == 0) else []),
                               inc=(hi == 1 and gg == 3))
                m_fsmm = S.marker("tensor")
                S.emit("vector", lambda e, tile=tile, gb=gb: e.scalar_tensor_tensor(
                    u[:, 1024 * tile + 256 * gb: 1024 * tile + 256 * gb + 256],
                    p_fs[:, :], 1.0,
                    u[:, 1024 * tile + 256 * gb: 1024 * tile + 256 * gb + 256],
                    ALU.mult, ALU.add), deps=[m_fsmm])
                m_fs_used = S.marker("vector")


    for tile in range(NT):
        S.emit("vector", lambda e, tile=tile: e.reduce_sum(
            lnS[:, 0:16],
            bass.AP(u, 1024 * tile, [[NT * 1024, 128], [64, 16], [1, 64]]),
            mybir.AxisListType.X), deps=[m_fs_used])
        S.emit("scalar", lambda e, tile=tile: e.activation(
            usq[:, :], u[:, 1024 * tile: 1024 * tile + 1024], AF.Square),
            deps=[m_fs_used])
        m_usq = S.marker("scalar")
        S.emit("vector", lambda e: e.reduce_sum(
            lnS[:, 16:32], bass.AP(usq, 0, [[1024, 128], [64, 16], [1, 64]]),
            mybir.AxisListType.X), deps=[m_usq])
        m_s2 = S.marker("vector")
        S.emit("tensor", lambda e: e.matmul(
            p_st2[:, 0:32], sb_sel8[:, :], lnS[:, :], start=True, stop=True),
            deps=[m_s2, m_consts])
        m_stmm = S.marker("tensor")
        S.emit("vector", lambda e: e.tensor_scalar_mul(lnT[:, :], p_st2[:, 0:32], 1.0 / 1024.0),
               deps=[m_stmm])
        S.emit("vector", lambda e: e.tensor_mul(lnM[:, :], lnT[:, 0:16], lnT[:, 0:16]))
        S.emit("vector", lambda e: e.tensor_sub(lnM[:, :], lnT[:, 16:32], lnM[:, :]))
        m_var = S.marker("vector")
        S.emit("scalar", lambda e: e.activation(
            lnM[:, :], lnM[:, :], AF.Sqrt, bias=sb_eps[0:8, 0:1]), deps=[m_var, m_consts])
        m_std = S.marker("scalar")
        S.emit("vector", lambda e: e.reciprocal(lnT[:, 16:32], lnM[:, :]),
               deps=[m_std])
        S.emit("vector", lambda e: e.tensor_mul(lnT[:, 0:16], lnT[:, 0:16], lnT[:, 16:32]))
        m_mrs = S.marker("vector")
        S.emit("tensor", lambda e: e.matmul(
            p_bc2[:, 0:32], sb_sel8t[:, :], lnT[:, :], start=True, stop=True),
            deps=[m_mrs])
        m_bcmm = S.marker("tensor")
        S.emit("vector", lambda e: e.tensor_copy(bcS[:, :], p_bc2[:, 0:32]), deps=[m_bcmm])
        ob = tile % 2
        S.emit("vector", lambda e, tile=tile, ob=ob: e.tensor_tensor(
            bass.AP(outS, 1024 * ob, [[2048, 128], [64, 16], [1, 64]]),
            bass.AP(u, 1024 * tile, [[NT * 1024, 128], [64, 16], [1, 64]]),
            bass.AP(bcS, 16, [[32, 128], [1, 16], [0, 64]]), ALU.mult),
            deps=[m_outS_used[ob]])
        S.emit("vector", lambda e, ob=ob: e.tensor_tensor(
            bass.AP(outS, 1024 * ob, [[2048, 128], [64, 16], [1, 64]]),
            bass.AP(outS, 1024 * ob, [[2048, 128], [64, 16], [1, 64]]),
            bass.AP(bcS, 0, [[32, 128], [1, 16], [0, 64]]), ALU.subtract))
        m_norm = S.marker("vector")
        S.emit("sync", lambda e, tile=tile, ob=ob: e.dma_start(
            bass.AP(d_out, 128 * 1024 * tile, [[1024, 128], [1, 1024]]),
            outS[:, 1024 * ob: 1024 * ob + 1024]), deps=[m_norm])
        m_outS_used[ob] = S.marker("sync")

    total_sync = S.cnt["sync"]
    S.ops["sync"].append(lambda e, v=16 * total_sync: e.wait_ge(S.sems["sync"], v))

    with nc.Block() as block:
        @block.sync
        def _(eng):
            for fn in S.ops["sync"]:
                fn(eng)

        @block.gpsimd
        def _(eng):
            for fn in S.ops["gpsimd"]:
                fn(eng)

        @block.tensor
        def _(eng):
            for fn in S.ops["tensor"]:
                fn(eng)

        @block.vector
        def _(eng):
            for fn in S.ops["vector"]:
                fn(eng)

        @block.scalar
        def _(eng):
            for fn in S.ops["scalar"]:
                fn(eng)

    esB.close()
    es_top.close()
    return nc



def _rope_tables():
    freqs = (10000.0 ** (-(np.arange(0, DK, 2, dtype=np.float64) / D))).astype(np.float32)
    ang = (np.arange(NCORES * T, dtype=np.float32)[:, None] * freqs[None, :]).astype(np.float32)
    libm = ctypes.CDLL(ctypes.util.find_library("m"))
    libm.cosf.restype = ctypes.c_float
    libm.cosf.argtypes = [ctypes.c_float]
    libm.sinf.restype = ctypes.c_float
    libm.sinf.argtypes = [ctypes.c_float]
    flat = ang.ravel()
    cos = np.fromiter((libm.cosf(float(v)) for v in flat), np.float32, flat.size).reshape(ang.shape)
    sin = np.fromiter((libm.sinf(float(v)) for v in flat), np.float32, flat.size).reshape(ang.shape)
    return cos, sin


_NC_CACHE = {}


def kernel(L0_attn_x, L0_attn_Wq, L0_attn_bq, L0_attn_Wk, L0_attn_bk,
           L0_attn_Wv, L0_attn_bv, L0_ffn_U1, L0_ffn_V1, L0_ffn_b1,
           L0_ffn_U2, L0_ffn_V2, L0_ffn_b2):
    x = np.ascontiguousarray(np.asarray(L0_attn_x, dtype=np.float32).reshape(NCORES * T, D))

    perm = np.concatenate([
        np.concatenate([np.arange(64 * c, 64 * c + 64),
                        np.arange(64 * (c + 8), 64 * (c + 8) + 64)])
        for c in range(8)])
    wq = np.ascontiguousarray(np.asarray(L0_attn_Wq, np.float32)[:, perm])
    wk = np.ascontiguousarray(np.asarray(L0_attn_Wk, np.float32)[:, perm])
    wv = np.ascontiguousarray(np.asarray(L0_attn_Wv, np.float32)[:, perm])
    v2p = np.ascontiguousarray(np.asarray(L0_ffn_V2, np.float32)[:, perm])

    def bias_chunks(b):
        return np.ascontiguousarray(np.asarray(b, np.float32)[perm].reshape(8, 128).T)

    bqp, bkp, bvp = (bias_chunks(b) for b in (L0_attn_bq, L0_attn_bk, L0_attn_bv))
    b2p = bias_chunks(L0_ffn_b2)
    u1p = np.ascontiguousarray(
        np.asarray(L0_ffn_U1, np.float32).reshape(16, 64, RANK).transpose(1, 0, 2).reshape(64, 16 * RANK))
    v1r = np.ascontiguousarray(np.asarray(L0_ffn_V1, np.float32))
    u2r = np.ascontiguousarray(np.asarray(L0_ffn_U2, np.float32))
    b1p = np.ascontiguousarray(np.asarray(L0_ffn_b1, np.float32).reshape(32, 128).T)

    cos, sin = _rope_tables()

    mask01 = np.zeros((128, 128), np.float32)
    for r in range(128):
        for n in range(128):
            if ((r % 64) >> 3) == ((n % 64) >> 3):
                mask01[r, n] = 1.0
    mask01 = mask01.astype(ml_dtypes.bfloat16)
    sel8 = np.zeros((128, 8), np.float32)
    for row in range(128):
        sel8[row, (row % 64) >> 3] = 1.0
    sel8t = np.ascontiguousarray(sel8.T)
    id16 = np.concatenate([np.eye(64, dtype=ml_dtypes.bfloat16)] * 2, 0)
    idf = np.concatenate([np.eye(64, dtype=np.float32)] * 2, 0)
    epsc = np.full((128, 1), 1e-5, np.float32)

    if "nc" not in _NC_CACHE:
        _NC_CACHE["nc"] = build_nc()
    nc = _NC_CACHE["nc"]

    in_maps = []
    for c in range(NCORES):
        xc = x[T * c: T * (c + 1)]
        xt = np.ascontiguousarray(xc.T)
        v = xc.reshape(NT, 16, 8, 2, 8, 64)
        xs = np.ascontiguousarray(v.transpose(0, 3, 2, 4, 1, 5).reshape(T, D))
        cos128 = np.ascontiguousarray(
            np.tile(np.repeat(cos[T * c: T * (c + 1)].T, 2, axis=0), (2, 1)))
        sin128 = np.ascontiguousarray(
            np.tile(np.repeat(sin[T * c: T * (c + 1)].T, 2, axis=0), (2, 1)))
        in_maps.append({
            "xt": xt, "xs": xs, "wq": wq, "wk": wk, "wv": wv,
            "bqp": bqp, "bkp": bkp, "bvp": bvp,
            "u1p": u1p, "v1r": v1r, "b1p": b1p, "u2r": u2r, "v2p": v2p, "b2p": b2p,
            "cos128": cos128, "sin128": sin128,
            "mask01": mask01, "sel8": sel8, "sel8t": sel8t,
            "id16": id16, "idf": idf, "epsc": epsc,
        })

    res = run_bass_kernel_spmd(nc, in_maps, core_ids=list(range(NCORES)))

    out = np.empty((NCORES * T, D), np.float32)
    for c in range(NCORES):
        o = np.asarray(res.results[c]["outs"], np.float32)
        v = o.reshape(NT, 2, 8, 8, 16, 64)
        out[T * c: T * (c + 1)] = v.transpose(0, 4, 2, 1, 3, 5).reshape(T, D)
    return out.reshape(1, NCORES * T, D)


# version 07
# speedup vs baseline: 1.3367x; 1.1071x over previous
import ctypes
import ctypes.util
from contextlib import ExitStack

import numpy as np
import ml_dtypes

import concourse.bass as bass
import concourse.mybir as mybir
from concourse.bass_utils import run_bass_kernel_spmd

F32 = mybir.dt.float32
BF16 = mybir.dt.bfloat16
AF = mybir.ActivationFunctionType
ALU = mybir.AluOpType

NCORES = 8
T = 2048
NT = 16
D = 1024
NH = 16
DK = 64
RANK = 256
DFF = 4096
NSLAB = 4
FSLAB = 8

ENGS = ("sync", "gpsimd", "tensor", "vector", "scalar")
DEBUG_A = False


class Sched:

    def __init__(self, nc, es):
        self.nc = nc
        self.ops = {e: [] for e in ENGS}
        self.cnt = {e: 0 for e in ENGS}
        self.hw = {e: {s: 0 for s in ENGS} for e in ENGS}
        self.sems = {e: es.enter_context(nc.semaphore(f"sem_{e}")) for e in ENGS}

    def mult(self, e):
        return 16 if e in ("sync", "gpsimd") else 1

    def throttle(self, eng):
        v = self.cnt[eng] * self.mult(eng)
        if v > 0:
            sem = self.sems[eng]
            self.ops[eng].append(lambda e, sem=sem, v=v: e.wait_ge(sem, v))

    def wait(self, eng, dep):
        src, val = dep
        if src == eng or val <= 0:
            return
        if self.hw[eng][src] >= val:
            return
        self.hw[eng][src] = val
        sem = self.sems[src]
        v = val * self.mult(src)
        self.ops[eng].append(lambda e, sem=sem, v=v: e.wait_ge(sem, v))

    def emit(self, eng, fn, deps=(), inc=True):
        for d in deps:
            self.wait(eng, d)
        if eng in ("sync", "gpsimd"):
            self.throttle(eng)
        elif inc and eng != "tensor" and self.cnt[eng] > 0:
            sem, v = self.sems[eng], self.cnt[eng]
            self.ops[eng].append(lambda e, sem=sem, v=v: e.wait_ge(sem, v))
        if inc:
            self.cnt[eng] += 1
            sem, m = self.sems[eng], self.mult(eng)
            self.ops[eng].append(lambda e, fn=fn, sem=sem, m=m: fn(e).then_inc(sem, m))
        else:
            self.ops[eng].append(fn)
        return (eng, self.cnt[eng])

    def marker(self, eng):
        return (eng, self.cnt[eng])


def build_nc():
    nc = bass.Bass("TRN2", target_bir_lowering=False)
    es_top = ExitStack()

    d_xt = nc.dram_tensor("xt", [D, T], F32, kind="ExternalInput")
    d_xs = nc.dram_tensor("xs", [T, D], F32, kind="ExternalInput")
    d_wq = nc.dram_tensor("wq", [D, D], F32, kind="ExternalInput")
    d_wk = nc.dram_tensor("wk", [D, D], F32, kind="ExternalInput")
    d_wv = nc.dram_tensor("wv", [D, D], F32, kind="ExternalInput")
    d_bq = nc.dram_tensor("bqp", [128, 8], F32, kind="ExternalInput")
    d_bk = nc.dram_tensor("bkp", [128, 8], F32, kind="ExternalInput")
    d_bv = nc.dram_tensor("bvp", [128, 8], F32, kind="ExternalInput")
    d_u1 = nc.dram_tensor("u1p", [64, 16 * RANK], F32, kind="ExternalInput")
    d_v1 = nc.dram_tensor("v1r", [RANK, DFF], F32, kind="ExternalInput")
    d_b1 = nc.dram_tensor("b1p", [128, 32], F32, kind="ExternalInput")
    d_u2 = nc.dram_tensor("u2r", [DFF, RANK], F32, kind="ExternalInput")
    d_v2 = nc.dram_tensor("v2p", [RANK, D], F32, kind="ExternalInput")
    d_b2 = nc.dram_tensor("b2p", [128, 8], F32, kind="ExternalInput")
    d_cos = nc.dram_tensor("cos128", [128, T], F32, kind="ExternalInput")
    d_sin = nc.dram_tensor("sin128", [128, T], F32, kind="ExternalInput")
    d_mask = nc.dram_tensor("mask01", [128, 128], BF16, kind="ExternalInput")
    d_sel8 = nc.dram_tensor("sel8", [128, 8], F32, kind="ExternalInput")
    d_sel8t = nc.dram_tensor("sel8t", [8, 128], F32, kind="ExternalInput")
    d_id16 = nc.dram_tensor("id16", [128, 64], BF16, kind="ExternalInput")
    d_idf = nc.dram_tensor("idf", [128, 64], F32, kind="ExternalInput")
    d_eps = nc.dram_tensor("epsc", [128, 1], F32, kind="ExternalInput")
    d_out = nc.dram_tensor("outs", [T, D], F32, kind="ExternalOutput")

    u = es_top.enter_context(nc.sbuf_tensor("u", [128, NT * 1024], F32))
    usq = es_top.enter_context(nc.sbuf_tensor("usq", [128, 2048], F32))
    w128 = es_top.enter_context(nc.sbuf_tensor("w128", [128, T], F32))
    sb_bq = es_top.enter_context(nc.sbuf_tensor("sb_bq", [128, 8], F32))
    sb_bk = es_top.enter_context(nc.sbuf_tensor("sb_bk", [128, 8], F32))
    sb_bv = es_top.enter_context(nc.sbuf_tensor("sb_bv", [128, 8], F32))
    sb_b1 = es_top.enter_context(nc.sbuf_tensor("sb_b1", [128, 32], F32))
    sb_b2 = es_top.enter_context(nc.sbuf_tensor("sb_b2", [128, 8], F32))
    sb_mask = es_top.enter_context(nc.sbuf_tensor("sb_mask", [128, 128], BF16))
    sb_sel8 = es_top.enter_context(nc.sbuf_tensor("sb_sel8", [128, 8], F32))
    sb_sel8t = es_top.enter_context(nc.sbuf_tensor("sb_sel8t", [8, 128], F32))
    sb_id16 = es_top.enter_context(nc.sbuf_tensor("sb_id16", [128, 64], BF16))
    sb_idf = es_top.enter_context(nc.sbuf_tensor("sb_idf", [128, 64], F32))
    sb_eps = es_top.enter_context(nc.sbuf_tensor("sb_eps", [128, 1], F32))
    lnS = es_top.enter_context(nc.sbuf_tensor("lnS", [128, 32], F32))
    lnT = es_top.enter_context(nc.sbuf_tensor("lnT", [8, 32], F32))
    lnM = es_top.enter_context(nc.sbuf_tensor("lnM", [8, 16], F32))
    bcS = es_top.enter_context(nc.sbuf_tensor("bcS", [128, 512], F32))
    zrec = es_top.enter_context(nc.sbuf_tensor("zrec", [128, 4], F32))
    outS = es_top.enter_context(nc.sbuf_tensor("outS", [128, 2 * 1024], F32))

    S = Sched(nc, es_top)

    esA = ExitStack()
    w16 = {t: esA.enter_context(nc.sbuf_tensor(f"w16_{t}", [128, 8 * D], BF16))
           for t in "qkv"}
    xt16 = [esA.enter_context(nc.sbuf_tensor(f"xt16_{i}", [128, 8 * 512], BF16))
            for i in range(2)]
    Qs = esA.enter_context(nc.sbuf_tensor("Qs", [128, 4096], BF16))
    Ks = esA.enter_context(nc.sbuf_tensor("Ks", [128, 4096], BF16))
    Q2s = esA.enter_context(nc.sbuf_tensor("Q2s", [128, 4096], BF16))
    VTs = esA.enter_context(nc.sbuf_tensor("VTs", [128, 4096], BF16))
    E2 = esA.enter_context(nc.sbuf_tensor("E2", [128, 2048], BF16))
    Vs = esA.enter_context(nc.sbuf_tensor("Vs", [128, 16 * 65], BF16))

    p_qkv = [esA.enter_context(nc.psum_tensor(f"p_qkv{i}", [128, 512], F32)) for i in range(2)]
    p_sc = [esA.enter_context(nc.psum_tensor(f"p_sc{i}", [128, 512], F32)) for i in range(2)]
    p_avs = [esA.enter_context(nc.psum_tensor(f"p_av{i}", [128, 260], F32)) for i in range(2)]
    p_tr = esA.enter_context(nc.psum_tensor("p_tr", [128, 256], F32))
    p_stbc = esA.enter_context(nc.psum_tensor("p_stbc", [128, 64], F32))

    for dst, src in [(sb_bq, d_bq), (sb_bk, d_bk), (sb_bv, d_bv), (sb_b1, d_b1),
                     (sb_b2, d_b2), (sb_mask, d_mask), (sb_sel8, d_sel8),
                     (sb_sel8t, d_sel8t), (sb_id16, d_id16), (sb_idf, d_idf),
                     (sb_eps, d_eps), (w128, d_cos)]:
        S.emit("sync", lambda e, dst=dst, src=src: e.dma_start(dst[:, :], src.ap()))
    m_consts = S.marker("sync")
    S.emit("sync", lambda e: e.dma_start(
        bass.AP(u, 0, [[NT * 1024, 128], [1024, NT], [1, 1024]]),
        bass.AP(d_xs, 0, [[1024, 128], [128 * 1024, NT], [1, 1024]])))
    m_xs = S.marker("sync")

    for t, d_w in [("q", d_wq), ("k", d_wk), ("v", d_wv)]:
        S.emit("gpsimd", lambda e, t=t, d_w=d_w: e.dma_start(
            bass.AP(w16[t], 0, [[8 * D, 128], [D, 8], [1, D]]),
            bass.AP(d_w, 0, [[D, 128], [128 * D, 8], [1, D]]),
        ))
    m_wqkv = S.marker("gpsimd")

    S.emit("vector", lambda e: e.tensor_mul(w128[:, :], w128[:, :], w128[:, :]),
           deps=[m_consts])
    m_usq_free = ("vector", 0)
    for h in range(2):
        S.emit("sync", lambda e, h=h: e.dma_start(
            usq[:, 0:1024], bass.AP(d_sin, 1024 * h, [[T, 128], [1, 1024]])),
            deps=[m_usq_free])
        m_sinh = S.marker("sync")
        S.emit("vector", lambda e: e.tensor_mul(usq[:, 0:1024], usq[:, 0:1024], usq[:, 0:1024]),
               deps=[m_sinh])
        S.emit("vector", lambda e, h=h: e.tensor_add(
            w128[:, 1024 * h: 1024 * h + 1024],
            w128[:, 1024 * h: 1024 * h + 1024], usq[:, 0:1024]))
        m_usq_free = S.marker("vector")
    m_w128 = S.marker("vector")

    S.emit("vector", lambda e: e.memset(bass.AP(Vs, 64, [[16 * 65, 128], [65, 16]]), 1.0))
    m_vsones = S.marker("vector")

    m_xt_used = [("tensor", 0), ("tensor", 0)]
    m_stacks_used = ("tensor", 0)
    m_evac_done_v = ("vector", 0)
    m_evac_done_s = ("scalar", 0)
    m_psum_used = [("vector", 0), ("vector", 0)]
    m_q2_dma = ("sync", 0)
    m_e2_used = ("tensor", 0)
    m_vs_used = ("tensor", 0)
    m_av_used = [("vector", 0), ("vector", 0)]
    m_sc_used = [("scalar", 0), ("scalar", 0)]
    m_tr_used = {"vector": ("vector", 0), "scalar": ("scalar", 0)}
    sync_q_prev = ("sync", 0)

    for s in range(NSLAB):
        buf = s % 2
        S.emit("gpsimd", lambda e, s=s, buf=buf: e.dma_start(
            bass.AP(xt16[buf], 0, [[8 * 512, 128], [512, 8], [1, 512]]),
            bass.AP(d_xt, 512 * s, [[T, 128], [128 * T, 8], [1, 512]]),
        ), deps=[m_xt_used[buf]])
        m_xt = S.marker("gpsimd")

        pp = 0
        for tname in "qkv":
            for ch in range(8):
                ps = p_qkv[pp % 2]
                dep_evac = m_psum_used[pp % 2]
                for kc in range(8):
                    S.emit("tensor", lambda e, tname=tname, ch=ch, kc=kc, ps=ps, buf=buf:
                           e.matmul(ps[:, :],
                                    w16[tname][:, D * kc + 128 * ch: D * kc + 128 * ch + 128],
                                    xt16[buf][:, 512 * kc: 512 * kc + 512],
                                    start=(kc == 0), stop=(kc == 7)),
                           deps=([m_wqkv, m_xt, dep_evac] if kc == 0 else []),
                           inc=(kc == 7))
                m_mm = S.marker("tensor")
                for half in range(2):
                    rows = slice(64 * half, 64 * half + 64)
                    ps_ap = lambda ps=ps, half=half: bass.AP(
                        ps, half * 64 * 512, [[512, 64], [128, 4], [1, 128]])
                    if tname == "k":
                        S.emit("vector", lambda e, ps=ps, rows=rows, ch=ch, half=half, s=s:
                               e.scalar_tensor_tensor(
                                   bass.AP(Ks, half * (64 * 4096) + 8 * ch,
                                           [[4096, 64], [1024, 4], [64, 16], [1, 8]]),
                                   bass.AP(ps, half * 64 * 512,
                                           [[512, 64], [128, 4], [8, 16], [1, 8]]),
                                   sb_bk[rows, ch: ch + 1],
                                   bass.AP(w128, half * 64 * T + 512 * s,
                                           [[T, 64], [128, 4], [8, 16], [1, 8]]),
                                   ALU.add, ALU.mult),
                               deps=[m_mm, m_w128, m_stacks_used, m_q2_dma, m_consts])
                    elif tname == "v":
                        S.emit("scalar", lambda e, ps=ps, rows=rows, ch=ch, half=half:
                               e.activation(
                                   bass.AP(VTs, half * (64 * 4096) + 8 * ch,
                                           [[4096, 64], [1024, 4], [64, 16], [1, 8]]),
                                   bass.AP(ps, half * 64 * 512,
                                           [[512, 64], [128, 4], [8, 16], [1, 8]]),
                                   AF.Identity, bias=sb_bv[rows, ch: ch + 1]),
                               deps=[m_mm, m_stacks_used, m_q2_dma, m_consts])
                    else:
                        S.emit("scalar", lambda e, ps_ap=ps_ap, rows=rows, ch=ch, half=half:
                               e.activation(
                                   bass.AP(Qs, half * (64 * 4096) + ch,
                                           [[4096, 64], [1024, 4], [8, 128]]),
                                   ps_ap(), AF.Identity, bias=sb_bq[rows, ch: ch + 1]),
                               deps=[m_mm, m_stacks_used, m_q2_dma, m_consts])
                if tname == "k":
                    m_evac_done_v = S.marker("vector")
                    m_psum_used[pp % 2] = m_evac_done_v
                else:
                    m_evac_done_s = S.marker("scalar")
                    m_psum_used[pp % 2] = m_evac_done_s
                pp += 1
        m_qevac = m_evac_done_s
        m_kevac = m_evac_done_v

        S.wait("sync", sync_q_prev)
        S.emit("sync", lambda e: e.dma_start(Q2s[0:64, :], Qs[64:128, :]),
               deps=[m_qevac, m_stacks_used])
        S.emit("sync", lambda e: e.dma_start(Q2s[64:128, :], Qs[0:64, :]))
        m_q2_dma = S.marker("sync")
        sync_q_prev = m_q2_dma

        for ti in range(4):
            tile = 4 * s + ti
            w0 = 1024 * ti

            for gb in range(4):
                for hi in range(2):
                    for gg in range(4):
                        g = 4 * gb + gg
                        S.emit("tensor", lambda e, hi=hi, g=g, gg=gg, w0=w0:
                               e.matmul(p_tr[64 * hi: 64 * hi + 64, 64 * gg: 64 * gg + 64],
                                        VTs[64 * hi: 64 * hi + 64, w0 + 64 * g: w0 + 64 * g + 64],
                                        sb_id16[64 * hi: 64 * hi + 64, :],
                                        start=True, stop=True),
                               deps=([m_kevac, m_qevac, m_consts,
                                      m_tr_used["vector"]] if (gg == 0 and hi == 0) else []),
                               inc=(gg == 3 and hi == 1))
                m_trp = S.marker("tensor")
                S.emit("vector", lambda e, gb=gb:
                       e.tensor_copy(
                           bass.AP(Vs, 65 * 4 * gb, [[16 * 65, 128], [65, 4], [1, 64]]),
                           p_tr[:, 0:256]),
                       deps=[m_trp, m_vsones, m_av_used[0], m_av_used[1]])
                m_tr_used["vector"] = S.marker("vector")

            m_vs_ready = S.marker("vector")

            def emit_scores(b):
                psb = p_sc[b % 2]
                for q4 in range(4):
                    g = 4 * b + q4
                    wg = w0 + 64 * g
                    S.emit("tensor", lambda e, wg=wg, q4=q4, psb=psb:
                           e.matmul(psb[0:64, 128 * q4: 128 * q4 + 64],
                                    Ks[0:64, wg: wg + 64], Qs[0:64, wg: wg + 64],
                                    start=True, stop=True),
                           deps=([m_q2_dma, m_sc_used[b % 2]] if q4 == 0 else []), inc=False)
                    S.emit("tensor", lambda e, wg=wg, q4=q4, psb=psb:
                           e.matmul(psb[0:64, 128 * q4 + 64: 128 * q4 + 128],
                                    Ks[0:64, wg: wg + 64], Q2s[0:64, wg: wg + 64],
                                    start=True, stop=True), inc=False)
                    S.emit("tensor", lambda e, wg=wg, q4=q4, psb=psb:
                           e.matmul(psb[64:128, 128 * q4: 128 * q4 + 64],
                                    Ks[64:128, wg: wg + 64], Q2s[64:128, wg: wg + 64],
                                    start=True, stop=True), inc=False)
                    S.emit("tensor", lambda e, wg=wg, q4=q4, psb=psb:
                           e.matmul(psb[64:128, 128 * q4 + 64: 128 * q4 + 128],
                                    Ks[64:128, wg: wg + 64], Qs[64:128, wg: wg + 64],
                                    start=True, stop=True), inc=(q4 == 3))
                m = S.marker("tensor")
                S.emit("scalar", lambda e, b=b, psb=psb: e.activation(
                    E2[:, 512 * b: 512 * b + 512], psb[:, :], AF.Exp, scale=0.125),
                    deps=[m, m_e2_used])
                m_sc_used[b % 2] = S.marker("scalar")
                S.emit("vector", lambda e, b=b: e.tensor_tensor(
                    E2[:, 512 * b: 512 * b + 512], E2[:, 512 * b: 512 * b + 512],
                    bass.AP(sb_mask, 0, [[128, 128], [0, 4], [1, 128]]), ALU.mult),
                    deps=[m_sc_used[b % 2], m_consts])
                return S.marker("vector")

            def emit_av(b, m_mask_b):
                pav = p_avs[b % 2]
                for q4 in range(4):
                    g = 4 * b + q4
                    S.emit("tensor", lambda e, g=g, q4=q4, pav=pav:
                           e.matmul(pav[:, 65 * q4: 65 * q4 + 65],
                                    E2[:, 128 * g: 128 * g + 128],
                                    Vs[:, 65 * g: 65 * g + 65],
                                    start=True, stop=True),
                           deps=([m_mask_b, m_vs_ready, m_av_used[b % 2]] if q4 == 0 else []),
                           inc=(q4 == 3))
                m_avmm = S.marker("tensor")
                S.emit("vector", lambda e, pav=pav: e.reciprocal(
                    zrec[:, 0:4], bass.AP(pav, 64, [[260, 128], [65, 4]])),
                    deps=[m_avmm, m_xs])
                for q4 in range(4):
                    g = 4 * b + q4
                    S.emit("vector", lambda e, g=g, q4=q4, tile=tile, pav=pav:
                           e.scalar_tensor_tensor(
                               u[:, 1024 * tile + 64 * g: 1024 * tile + 64 * g + 64],
                               pav[:, 65 * q4: 65 * q4 + 64],
                               zrec[:, q4: q4 + 1],
                               u[:, 1024 * tile + 64 * g: 1024 * tile + 64 * g + 64],
                               ALU.mult, ALU.add))
                m_av_used[b % 2] = S.marker("vector")
                return m_avmm

            m_masks = {}
            m_masks[0] = emit_scores(0)
            m_masks[1] = emit_scores(1)
            last_avmm = None
            for b in range(4):
                last_avmm = emit_av(b, m_masks[b])
                if b + 2 < 4:
                    m_masks[b + 2] = emit_scores(b + 2)
            m_e2_used = last_avmm


        m_stacks_used = S.marker("tensor")
        m_xt_used[buf] = S.marker("tensor")


    m_usq_rd = [("vector", 0), ("vector", 0)]

    for tile in range(NT):
        ub = tile % 2
        S.emit("scalar", lambda e, tile=tile, ub=ub: e.activation(
            usq[:, 1024 * ub: 1024 * ub + 1024],
            u[:, 1024 * tile: 1024 * tile + 1024], AF.Square),
            deps=[m_av_used[0], m_av_used[1]] + [m_usq_rd[ub]])
        m_usq_w = S.marker("scalar")
        S.emit("vector", lambda e, tile=tile: e.reduce_sum(
            lnS[:, 0:16],
            bass.AP(u, 1024 * tile, [[NT * 1024, 128], [64, 16], [1, 64]]),
            mybir.AxisListType.X))
        S.emit("vector", lambda e, ub=ub: e.reduce_sum(
            lnS[:, 16:32], bass.AP(usq, 1024 * ub, [[2048, 128], [64, 16], [1, 64]]),
            mybir.AxisListType.X), deps=[m_usq_w])
        m_usq_rd[ub] = S.marker("vector")
        m_s2 = S.marker("vector")
        S.emit("tensor", lambda e: e.matmul(
            p_stbc[0:8, 0:32], sb_sel8[:, :], lnS[:, :], start=True, stop=True),
            deps=[m_s2, m_consts])
        m_stmm = S.marker("tensor")
        S.emit("vector", lambda e: e.tensor_scalar_mul(lnT[:, :], p_stbc[0:8, 0:32], 1.0 / 1024.0),
               deps=[m_stmm])
        S.emit("vector", lambda e: e.tensor_mul(lnM[:, :], lnT[:, 0:16], lnT[:, 0:16]))
        S.emit("vector", lambda e: e.tensor_sub(lnM[:, :], lnT[:, 16:32], lnM[:, :]))
        m_var = S.marker("vector")
        S.emit("scalar", lambda e: e.activation(
            lnM[:, :], lnM[:, :], AF.Sqrt, bias=sb_eps[0:8, 0:1]), deps=[m_var, m_consts])
        m_std = S.marker("scalar")
        S.emit("vector", lambda e: e.reciprocal(lnT[:, 16:32], lnM[:, :]),
               deps=[m_std])
        S.emit("vector", lambda e: e.tensor_mul(lnT[:, 0:16], lnT[:, 0:16], lnT[:, 16:32]))
        m_mrs = S.marker("vector")
        S.emit("tensor", lambda e, tile=tile: e.matmul(
            p_stbc[:, 32:64], sb_sel8t[:, :], lnT[:, :], start=True, stop=True),
            deps=[m_mrs])
        m_bcmm = S.marker("tensor")
        S.emit("vector", lambda e, tile=tile: e.tensor_copy(
            bcS[:, 32 * tile: 32 * tile + 32], p_stbc[:, 32:64]), deps=[m_bcmm])
    for tile in range(NT):
        S.emit("vector", lambda e, tile=tile: e.tensor_tensor(
            bass.AP(u, 1024 * tile, [[NT * 1024, 128], [64, 16], [1, 64]]),
            bass.AP(u, 1024 * tile, [[NT * 1024, 128], [64, 16], [1, 64]]),
            bass.AP(bcS, 32 * tile + 16, [[512, 128], [1, 16], [0, 64]]), ALU.mult))
        S.emit("vector", lambda e, tile=tile: e.tensor_tensor(
            bass.AP(u, 1024 * tile, [[NT * 1024, 128], [64, 16], [1, 64]]),
            bass.AP(u, 1024 * tile, [[NT * 1024, 128], [64, 16], [1, 64]]),
            bass.AP(bcS, 32 * tile, [[512, 128], [1, 16], [0, 64]]), ALU.subtract))

    m_attn_pe = S.marker("tensor")
    m_attn_v = S.marker("vector")
    m_attn_s = S.marker("scalar")

    esA.close()

    if DEBUG_A:
        total_sync = S.cnt["sync"]
        if total_sync:
            S.ops["sync"].append(lambda e, v=16 * total_sync: e.wait_ge(S.sems["sync"], v))
        with nc.Block() as block:
            @block.sync
            def _(eng):
                for fn in S.ops["sync"]:
                    fn(eng)

            @block.gpsimd
            def _(eng):
                for fn in S.ops["gpsimd"]:
                    fn(eng)

            @block.tensor
            def _(eng):
                for fn in S.ops["tensor"]:
                    fn(eng)

            @block.vector
            def _(eng):
                for fn in S.ops["vector"]:
                    fn(eng)

            @block.scalar
            def _(eng):
                for fn in S.ops["scalar"]:
                    fn(eng)
        esA.close()
        es_top.close()
        return nc
    esB = ExitStack()
    u1p16 = esB.enter_context(nc.sbuf_tensor("u1p16", [64, 16 * RANK], BF16))
    v116 = esB.enter_context(nc.sbuf_tensor("v116", [128, 2 * DFF], BF16))
    u216 = esB.enter_context(nc.sbuf_tensor("u216", [128, 32 * RANK], BF16))
    v2p16 = esB.enter_context(nc.sbuf_tensor("v2p16", [128, 2 * 1024], BF16))
    y16 = esB.enter_context(nc.sbuf_tensor("y16", [128, 2048], BF16))
    YTl = esB.enter_context(nc.sbuf_tensor("YTl", [64, 2048], BF16))
    YTh = esB.enter_context(nc.sbuf_tensor("YTh", [64, 2048], BF16))
    P16 = esB.enter_context(nc.sbuf_tensor("P16", [128, 512], BF16))
    H16 = esB.enter_context(nc.sbuf_tensor("H16", [128, 32 * 256], BF16))
    Q2T = esB.enter_context(nc.sbuf_tensor("Q2T", [128, 512], BF16))
    FTs = esB.enter_context(nc.sbuf_tensor("FTs", [128, 2048], F32))

    p_a = esB.enter_context(nc.psum_tensor("p_a", [128, 256], F32))
    p_b = esB.enter_context(nc.psum_tensor("p_b", [128, 256], F32))
    p_c = esB.enter_context(nc.psum_tensor("p_c", [128, 256], F32))
    p_d = esB.enter_context(nc.psum_tensor("p_d", [128, 256], F32))
    p_fs = esB.enter_context(nc.psum_tensor("p_fs", [128, 256], F32))
    p_yt = esB.enter_context(nc.psum_tensor("p_yt", [64, 256], F32))
    p_st2 = esB.enter_context(nc.psum_tensor("p_st2", [128, 64], F32))

    gate = [m_attn_pe, m_attn_v, m_attn_s]
    S.emit("gpsimd", lambda e: e.dma_start(u1p16[:, :], d_u1.ap()), deps=gate)
    S.emit("gpsimd", lambda e: e.dma_start(
        bass.AP(v116, 0, [[2 * DFF, 128], [DFF, 2], [1, DFF]]),
        bass.AP(d_v1, 0, [[DFF, 128], [128 * DFF, 2], [1, DFF]])))
    S.emit("gpsimd", lambda e: e.dma_start(
        bass.AP(u216, 0, [[32 * RANK, 128], [RANK, 32], [1, RANK]]),
        bass.AP(d_u2, 0, [[RANK, 128], [128 * RANK, 32], [1, RANK]])))
    S.emit("gpsimd", lambda e: e.dma_start(
        bass.AP(v2p16, 0, [[2 * 1024, 128], [1024, 2], [1, 1024]]),
        bass.AP(d_v2, 0, [[1024, 128], [128 * 1024, 2], [1, 1024]])))
    m_wffn = S.marker("gpsimd")

    m_y16_used = ("tensor", 0)
    m_yt_used = ("vector", 0)
    m_yts_used = ("tensor", 0)
    m_p16_used = ("tensor", 0)
    m_h16_used = ("tensor", 0)
    m_q2t_used = ("tensor", 0)
    m_ft_used = ("tensor", 0)
    m_pab_used = ("vector", 0)
    m_pcd_used = ("scalar", 0)
    m_fs_used = ("vector", 0)
    m_outS_used = [("sync", 0), ("sync", 0)]
    n_store = 0

    for f in range(FSLAB):
        c0 = 2048 * f
        S.emit("vector", lambda e, c0=c0: e.tensor_copy(
            y16[:, :], u[:, c0: c0 + 2048]), deps=[m_y16_used])
        m_y16 = S.marker("vector")

        for hi in range(2):
            ytd = YTl if hi == 0 else YTh
            for blk in range(8):
                for gg in range(4):
                    gidx = 4 * blk + gg
                    S.emit("tensor", lambda e, hi=hi, gidx=gidx, gg=gg:
                           e.matmul(p_yt[:, 64 * gg: 64 * gg + 64],
                                    y16[64 * hi: 64 * hi + 64, 64 * gidx: 64 * gidx + 64],
                                    sb_id16[64 * hi: 64 * hi + 64, :],
                                    start=True, stop=True),
                           deps=([m_y16, m_yt_used, m_consts] if gg == 0 else []), inc=(gg == 3))
                m_ytp = S.marker("tensor")
                S.emit("vector", lambda e, ytd=ytd, blk=blk: e.tensor_copy(
                    ytd[:, 256 * blk: 256 * blk + 256], p_yt[:, :]),
                    deps=[m_ytp, m_yts_used])
                m_yt_used = S.marker("vector")
        m_yt_ready = S.marker("vector")

        for rh in range(2):
            ps = p_a if rh == 0 else p_b
            for hc in range(16):
                hi, ch = hc // 8, hc % 8
                yts = YTl if hi == 0 else YTh
                S.emit("tensor", lambda e, rh=rh, hi=hi, ch=ch, ps=ps, yts=yts, hc=hc:
                       e.matmul(ps[:, :],
                                u1p16[:, RANK * (ch + 8 * hi) + 128 * rh:
                                      RANK * (ch + 8 * hi) + 128 * rh + 128],
                                bass.AP(yts, ch, [[2048, 64], [64, 32], [8, 8]]),
                                start=(hc == 0), stop=(hc == 15)),
                       deps=([m_yt_ready, m_wffn, m_pab_used, m_consts] if hc == 0 else []),
                       inc=(hc == 15))
            m_pmm = S.marker("tensor")
            S.emit("vector", lambda e, rh=rh, ps=ps: e.tensor_copy(
                P16[:, 256 * rh: 256 * rh + 256], ps[:, :]), deps=[m_pmm, m_p16_used])
        m_pab_used = S.marker("vector")
        m_p16 = S.marker("vector")

        h_rot = [p_a, p_b, p_c, p_d]
        h_used = [m_pab_used, m_pab_used, m_pcd_used, m_pcd_used]
        for ffc in range(32):
            ps = h_rot[ffc % 4]
            for rc in range(2):
                S.emit("tensor", lambda e, ffc=ffc, rc=rc, ps=ps:
                       e.matmul(ps[:, :],
                                v116[:, DFF * rc + 128 * ffc: DFF * rc + 128 * ffc + 128],
                                P16[:, 256 * rc: 256 * rc + 256],
                                start=(rc == 0), stop=(rc == 1)),
                       deps=([m_p16, h_used[ffc % 4]] if rc == 0 else []), inc=(rc == 1))
            m_hmm = S.marker("tensor")
            S.emit("scalar", lambda e, ffc=ffc, ps=ps: e.activation(
                H16[:, 256 * ffc: 256 * ffc + 256], ps[:, :], AF.Gelu_apprx_tanh,
                bias=sb_b1[:, ffc: ffc + 1]), deps=[m_hmm, m_h16_used, m_consts])
            h_used[ffc % 4] = S.marker("scalar")
            if ffc % 2 == 1:
                m_pcd_used = S.marker("scalar")
        m_h16 = S.marker("scalar")

        for rh in range(2):
            ps = p_a if rh == 0 else p_b
            for ffc in range(32):
                S.emit("tensor", lambda e, rh=rh, ffc=ffc, ps=ps:
                       e.matmul(ps[:, :],
                                u216[:, RANK * ffc + 128 * rh: RANK * ffc + 128 * rh + 128],
                                H16[:, 256 * ffc: 256 * ffc + 256],
                                start=(ffc == 0), stop=(ffc == 31)),
                       deps=([m_h16, m_pab_used] if ffc == 0 else []), inc=(ffc == 31))
            m_qmm = S.marker("tensor")
            S.emit("vector", lambda e, rh=rh, ps=ps: e.tensor_copy(
                Q2T[:, 256 * rh: 256 * rh + 256], ps[:, :]), deps=[m_qmm, m_q2t_used])
        m_pab_used = S.marker("vector")
        m_q2t = S.marker("vector")

        for ch in range(8):
            ps = p_c if ch % 2 == 0 else p_d
            for rc in range(2):
                S.emit("tensor", lambda e, ch=ch, rc=rc, ps=ps:
                       e.matmul(ps[:, :],
                                v2p16[:, 1024 * rc + 128 * ch: 1024 * rc + 128 * ch + 128],
                                Q2T[:, 256 * rc: 256 * rc + 256],
                                start=(rc == 0), stop=(rc == 1)),
                       deps=([m_q2t, m_pcd_used] if rc == 0 else []), inc=(rc == 1))
            m_fmm = S.marker("tensor")
            for half in range(2):
                rows = slice(64 * half, 64 * half + 64)
                S.emit("scalar", lambda e, ch=ch, half=half, rows=rows, ps=ps:
                       e.activation(
                           bass.AP(FTs, half * (64 * 2048) + ch,
                                   [[2048, 64], [1024, 2], [8, 128]]),
                           ps[rows, :], AF.Identity, bias=sb_b2[rows, ch: ch + 1]),
                       deps=[m_fmm, m_ft_used, m_consts])
            if ch % 2 == 1:
                m_pcd_used = S.marker("scalar")
        m_ft = S.marker("scalar")

        for tp in range(2):
            tile = 2 * f + tp
            for gb in range(4):
                for hi in range(2):
                    for gg in range(4):
                        g = 4 * gb + gg
                        S.emit("tensor", lambda e, hi=hi, g=g, gg=gg, tp=tp:
                               e.matmul(p_fs[64 * hi: 64 * hi + 64, 64 * gg: 64 * gg + 64],
                                        FTs[64 * hi: 64 * hi + 64,
                                            1024 * tp + 64 * g: 1024 * tp + 64 * g + 64],
                                        sb_idf[64 * hi: 64 * hi + 64, :],
                                        start=True, stop=True),
                               deps=([m_ft, m_fs_used, m_consts] if (hi == 0 and gg == 0) else []),
                               inc=(hi == 1 and gg == 3))
                m_fsmm = S.marker("tensor")
                S.emit("vector", lambda e, tile=tile, gb=gb: e.scalar_tensor_tensor(
                    u[:, 1024 * tile + 256 * gb: 1024 * tile + 256 * gb + 256],
                    p_fs[:, :], 1.0,
                    u[:, 1024 * tile + 256 * gb: 1024 * tile + 256 * gb + 256],
                    ALU.mult, ALU.add), deps=[m_fsmm])
                m_fs_used = S.marker("vector")


    m_usq_rd = [("vector", 0), ("vector", 0)]

    for tile in range(NT):
        ub = tile % 2
        S.emit("scalar", lambda e, tile=tile, ub=ub: e.activation(
            usq[:, 1024 * ub: 1024 * ub + 1024],
            u[:, 1024 * tile: 1024 * tile + 1024], AF.Square),
            deps=[m_fs_used] + [m_usq_rd[ub]])
        m_usq_w = S.marker("scalar")
        S.emit("vector", lambda e, tile=tile: e.reduce_sum(
            lnS[:, 0:16],
            bass.AP(u, 1024 * tile, [[NT * 1024, 128], [64, 16], [1, 64]]),
            mybir.AxisListType.X), deps=[m_fs_used])
        S.emit("vector", lambda e, ub=ub: e.reduce_sum(
            lnS[:, 16:32], bass.AP(usq, 1024 * ub, [[2048, 128], [64, 16], [1, 64]]),
            mybir.AxisListType.X), deps=[m_usq_w])
        m_usq_rd[ub] = S.marker("vector")
        m_s2 = S.marker("vector")
        S.emit("tensor", lambda e: e.matmul(
            p_st2[0:8, 0:32], sb_sel8[:, :], lnS[:, :], start=True, stop=True),
            deps=[m_s2, m_consts])
        m_stmm = S.marker("tensor")
        S.emit("vector", lambda e: e.tensor_scalar_mul(lnT[:, :], p_st2[0:8, 0:32], 1.0 / 1024.0),
               deps=[m_stmm])
        S.emit("vector", lambda e: e.tensor_mul(lnM[:, :], lnT[:, 0:16], lnT[:, 0:16]))
        S.emit("vector", lambda e: e.tensor_sub(lnM[:, :], lnT[:, 16:32], lnM[:, :]))
        m_var = S.marker("vector")
        S.emit("scalar", lambda e: e.activation(
            lnM[:, :], lnM[:, :], AF.Sqrt, bias=sb_eps[0:8, 0:1]), deps=[m_var, m_consts])
        m_std = S.marker("scalar")
        S.emit("vector", lambda e: e.reciprocal(lnT[:, 16:32], lnM[:, :]),
               deps=[m_std])
        S.emit("vector", lambda e: e.tensor_mul(lnT[:, 0:16], lnT[:, 0:16], lnT[:, 16:32]))
        m_mrs = S.marker("vector")
        S.emit("tensor", lambda e, tile=tile: e.matmul(
            p_st2[:, 32:64], sb_sel8t[:, :], lnT[:, :], start=True, stop=True),
            deps=[m_mrs])
        m_bcmm = S.marker("tensor")
        S.emit("vector", lambda e, tile=tile: e.tensor_copy(
            bcS[:, 32 * tile: 32 * tile + 32], p_st2[:, 32:64]), deps=[m_bcmm])
    for tile in range(NT):
        ob = tile % 2
        S.emit("vector", lambda e, tile=tile, ob=ob: e.tensor_tensor(
            bass.AP(outS, 1024 * ob, [[2048, 128], [64, 16], [1, 64]]),
            bass.AP(u, 1024 * tile, [[NT * 1024, 128], [64, 16], [1, 64]]),
            bass.AP(bcS, 32 * tile + 16, [[512, 128], [1, 16], [0, 64]]), ALU.mult),
            deps=[m_outS_used[ob]])
        S.emit("vector", lambda e, tile=tile, ob=ob: e.tensor_tensor(
            bass.AP(outS, 1024 * ob, [[2048, 128], [64, 16], [1, 64]]),
            bass.AP(outS, 1024 * ob, [[2048, 128], [64, 16], [1, 64]]),
            bass.AP(bcS, 32 * tile, [[512, 128], [1, 16], [0, 64]]), ALU.subtract))
        m_norm = S.marker("vector")
        S.emit("sync", lambda e, tile=tile, ob=ob: e.dma_start(
            bass.AP(d_out, 128 * 1024 * tile, [[1024, 128], [1, 1024]]),
            outS[:, 1024 * ob: 1024 * ob + 1024]), deps=[m_norm])
        m_outS_used[ob] = S.marker("sync")

    total_sync = S.cnt["sync"]
    S.ops["sync"].append(lambda e, v=16 * total_sync: e.wait_ge(S.sems["sync"], v))

    with nc.Block() as block:
        @block.sync
        def _(eng):
            for fn in S.ops["sync"]:
                fn(eng)

        @block.gpsimd
        def _(eng):
            for fn in S.ops["gpsimd"]:
                fn(eng)

        @block.tensor
        def _(eng):
            for fn in S.ops["tensor"]:
                fn(eng)

        @block.vector
        def _(eng):
            for fn in S.ops["vector"]:
                fn(eng)

        @block.scalar
        def _(eng):
            for fn in S.ops["scalar"]:
                fn(eng)

    esB.close()
    es_top.close()
    return nc



def _rope_tables():
    freqs = (10000.0 ** (-(np.arange(0, DK, 2, dtype=np.float64) / D))).astype(np.float32)
    ang = (np.arange(NCORES * T, dtype=np.float32)[:, None] * freqs[None, :]).astype(np.float32)
    libm = ctypes.CDLL(ctypes.util.find_library("m"))
    libm.cosf.restype = ctypes.c_float
    libm.cosf.argtypes = [ctypes.c_float]
    libm.sinf.restype = ctypes.c_float
    libm.sinf.argtypes = [ctypes.c_float]
    flat = ang.ravel()
    cos = np.fromiter((libm.cosf(float(v)) for v in flat), np.float32, flat.size).reshape(ang.shape)
    sin = np.fromiter((libm.sinf(float(v)) for v in flat), np.float32, flat.size).reshape(ang.shape)
    return cos, sin


_NC_CACHE = {}


def kernel(L0_attn_x, L0_attn_Wq, L0_attn_bq, L0_attn_Wk, L0_attn_bk,
           L0_attn_Wv, L0_attn_bv, L0_ffn_U1, L0_ffn_V1, L0_ffn_b1,
           L0_ffn_U2, L0_ffn_V2, L0_ffn_b2):
    x = np.ascontiguousarray(np.asarray(L0_attn_x, dtype=np.float32).reshape(NCORES * T, D))

    perm = np.concatenate([
        np.concatenate([np.arange(64 * c, 64 * c + 64),
                        np.arange(64 * (c + 8), 64 * (c + 8) + 64)])
        for c in range(8)])
    wq = np.ascontiguousarray(np.asarray(L0_attn_Wq, np.float32)[:, perm])
    wk = np.ascontiguousarray(np.asarray(L0_attn_Wk, np.float32)[:, perm])
    wv = np.ascontiguousarray(np.asarray(L0_attn_Wv, np.float32)[:, perm])
    v2p = np.ascontiguousarray(np.asarray(L0_ffn_V2, np.float32)[:, perm])

    def bias_chunks(b):
        return np.ascontiguousarray(np.asarray(b, np.float32)[perm].reshape(8, 128).T)

    bqp, bkp, bvp = (bias_chunks(b) for b in (L0_attn_bq, L0_attn_bk, L0_attn_bv))
    b2p = bias_chunks(L0_ffn_b2)
    u1p = np.ascontiguousarray(
        np.asarray(L0_ffn_U1, np.float32).reshape(16, 64, RANK).transpose(1, 0, 2).reshape(64, 16 * RANK))
    v1r = np.ascontiguousarray(np.asarray(L0_ffn_V1, np.float32))
    u2r = np.ascontiguousarray(np.asarray(L0_ffn_U2, np.float32))
    b1p = np.ascontiguousarray(np.asarray(L0_ffn_b1, np.float32).reshape(32, 128).T)

    cos, sin = _rope_tables()

    mask01 = np.zeros((128, 128), np.float32)
    for r in range(128):
        for n in range(128):
            if (r % 8) == ((n % 64) >> 3):
                mask01[r, n] = 1.0
    mask01 = mask01.astype(ml_dtypes.bfloat16)
    sel8 = np.zeros((128, 8), np.float32)
    for row in range(128):
        sel8[row, (row % 64) >> 3] = 1.0
    sel8t = np.ascontiguousarray(sel8.T)
    id16 = np.concatenate([np.eye(64, dtype=ml_dtypes.bfloat16)] * 2, 0)
    idf = np.concatenate([np.eye(64, dtype=np.float32)] * 2, 0)
    epsc = np.full((128, 1), 1e-5, np.float32)

    if "nc" not in _NC_CACHE:
        _NC_CACHE["nc"] = build_nc()
    nc = _NC_CACHE["nc"]

    in_maps = []
    for c in range(NCORES):
        xc = x[T * c: T * (c + 1)]
        xt = np.ascontiguousarray(xc.T)
        v = xc.reshape(NT, 16, 8, 2, 8, 64)
        xs = np.ascontiguousarray(v.transpose(0, 3, 2, 4, 1, 5).reshape(T, D))
        cos128 = np.ascontiguousarray(
            np.tile(np.repeat(cos[T * c: T * (c + 1)].T, 2, axis=0), (2, 1)))
        sin128 = np.ascontiguousarray(
            np.tile(np.repeat(sin[T * c: T * (c + 1)].T, 2, axis=0), (2, 1)))
        in_maps.append({
            "xt": xt, "xs": xs, "wq": wq, "wk": wk, "wv": wv,
            "bqp": bqp, "bkp": bkp, "bvp": bvp,
            "u1p": u1p, "v1r": v1r, "b1p": b1p, "u2r": u2r, "v2p": v2p, "b2p": b2p,
            "cos128": cos128, "sin128": sin128,
            "mask01": mask01, "sel8": sel8, "sel8t": sel8t,
            "id16": id16, "idf": idf, "epsc": epsc,
        })

    res = run_bass_kernel_spmd(nc, in_maps, core_ids=list(range(NCORES)))

    out = np.empty((NCORES * T, D), np.float32)
    for c in range(NCORES):
        o = np.asarray(res.results[c]["outs"], np.float32)
        v = o.reshape(NT, 2, 8, 8, 16, 64)
        out[T * c: T * (c + 1)] = v.transpose(0, 4, 2, 1, 3, 5).reshape(T, D)
    return out.reshape(1, NCORES * T, D)


# version 08
# speedup vs baseline: 1.3998x; 1.0471x over previous
import ctypes
import ctypes.util
from contextlib import ExitStack

import numpy as np
import ml_dtypes

import concourse.bass as bass
import concourse.mybir as mybir
from concourse.bass_utils import run_bass_kernel_spmd

F32 = mybir.dt.float32
BF16 = mybir.dt.bfloat16
AF = mybir.ActivationFunctionType
ALU = mybir.AluOpType

NCORES = 8
T = 2048
NT = 16
D = 1024
NH = 16
DK = 64
RANK = 256
DFF = 4096
NSLAB = 4
FSLAB = 8

ENGS = ("sync", "gpsimd", "tensor", "vector", "scalar")
DEBUG_A = False


class Sched:

    def __init__(self, nc, es):
        self.nc = nc
        self.ops = {e: [] for e in ENGS}
        self.cnt = {e: 0 for e in ENGS}
        self.hw = {e: {s: 0 for s in ENGS} for e in ENGS}
        self.sems = {e: es.enter_context(nc.semaphore(f"sem_{e}")) for e in ENGS}

    def mult(self, e):
        return 16 if e in ("sync", "gpsimd") else 1

    def throttle(self, eng):
        v = self.cnt[eng] * self.mult(eng)
        if v > 0:
            sem = self.sems[eng]
            self.ops[eng].append(lambda e, sem=sem, v=v: e.wait_ge(sem, v))

    def wait(self, eng, dep):
        src, val = dep
        if src == eng or val <= 0:
            return
        if self.hw[eng][src] >= val:
            return
        self.hw[eng][src] = val
        sem = self.sems[src]
        v = val * self.mult(src)
        self.ops[eng].append(lambda e, sem=sem, v=v: e.wait_ge(sem, v))

    def emit(self, eng, fn, deps=(), inc=True):
        for d in deps:
            self.wait(eng, d)
        if eng in ("sync", "gpsimd"):
            self.throttle(eng)
        elif inc and eng != "tensor" and self.cnt[eng] > 0:
            sem, v = self.sems[eng], self.cnt[eng]
            self.ops[eng].append(lambda e, sem=sem, v=v: e.wait_ge(sem, v))
        if inc:
            self.cnt[eng] += 1
            sem, m = self.sems[eng], self.mult(eng)
            self.ops[eng].append(lambda e, fn=fn, sem=sem, m=m: fn(e).then_inc(sem, m))
        else:
            self.ops[eng].append(fn)
        return (eng, self.cnt[eng])

    def marker(self, eng):
        return (eng, self.cnt[eng])


def build_nc():
    nc = bass.Bass("TRN2", target_bir_lowering=False)
    es_top = ExitStack()

    d_xt = nc.dram_tensor("xt", [D, T], F32, kind="ExternalInput")
    d_xs = nc.dram_tensor("xs", [T, D], F32, kind="ExternalInput")
    d_wq = nc.dram_tensor("wq", [D, D], F32, kind="ExternalInput")
    d_wk = nc.dram_tensor("wk", [D, D], F32, kind="ExternalInput")
    d_wv = nc.dram_tensor("wv", [D, D], F32, kind="ExternalInput")
    d_bq = nc.dram_tensor("bqp", [128, 8], F32, kind="ExternalInput")
    d_bk = nc.dram_tensor("bkp", [128, 8], F32, kind="ExternalInput")
    d_bv = nc.dram_tensor("bvp", [128, 8], F32, kind="ExternalInput")
    d_u1 = nc.dram_tensor("u1p", [64, 16 * RANK], F32, kind="ExternalInput")
    d_v1 = nc.dram_tensor("v1r", [RANK, DFF], F32, kind="ExternalInput")
    d_b1 = nc.dram_tensor("b1p", [128, 32], F32, kind="ExternalInput")
    d_u2 = nc.dram_tensor("u2r", [DFF, RANK], F32, kind="ExternalInput")
    d_v2 = nc.dram_tensor("v2p", [RANK, D], F32, kind="ExternalInput")
    d_b2 = nc.dram_tensor("b2p", [128, 8], F32, kind="ExternalInput")
    d_cos = nc.dram_tensor("cos128", [128, T], F32, kind="ExternalInput")
    d_sin = nc.dram_tensor("sin128", [128, T], F32, kind="ExternalInput")
    d_mask = nc.dram_tensor("mask01", [128, 128], BF16, kind="ExternalInput")
    d_sel8 = nc.dram_tensor("sel8", [128, 8], F32, kind="ExternalInput")
    d_sel8t = nc.dram_tensor("sel8t", [8, 128], F32, kind="ExternalInput")
    d_id16 = nc.dram_tensor("id16", [128, 64], BF16, kind="ExternalInput")
    d_idf = nc.dram_tensor("idf", [128, 64], F32, kind="ExternalInput")
    d_eps = nc.dram_tensor("epsc", [128, 1], F32, kind="ExternalInput")
    d_out = nc.dram_tensor("outs", [T, D], F32, kind="ExternalOutput")

    u = es_top.enter_context(nc.sbuf_tensor("u", [128, NT * 1024], F32))
    usq = es_top.enter_context(nc.sbuf_tensor("usq", [128, 2048], F32))
    w128 = es_top.enter_context(nc.sbuf_tensor("w128", [128, T], F32))
    sb_bq = es_top.enter_context(nc.sbuf_tensor("sb_bq", [128, 8], F32))
    sb_bk = es_top.enter_context(nc.sbuf_tensor("sb_bk", [128, 8], F32))
    sb_bv = es_top.enter_context(nc.sbuf_tensor("sb_bv", [128, 8], F32))
    sb_b1 = es_top.enter_context(nc.sbuf_tensor("sb_b1", [128, 32], F32))
    sb_b2 = es_top.enter_context(nc.sbuf_tensor("sb_b2", [128, 8], F32))
    sb_mask = es_top.enter_context(nc.sbuf_tensor("sb_mask", [128, 128], BF16))
    sb_sel8 = es_top.enter_context(nc.sbuf_tensor("sb_sel8", [128, 8], F32))
    sb_sel8t = es_top.enter_context(nc.sbuf_tensor("sb_sel8t", [8, 128], F32))
    sb_id16 = es_top.enter_context(nc.sbuf_tensor("sb_id16", [128, 64], BF16))
    sb_idf = es_top.enter_context(nc.sbuf_tensor("sb_idf", [128, 64], F32))
    sb_eps = es_top.enter_context(nc.sbuf_tensor("sb_eps", [128, 1], F32))
    lnS = es_top.enter_context(nc.sbuf_tensor("lnS", [128, 32], F32))
    lnT = es_top.enter_context(nc.sbuf_tensor("lnT", [8, 32], F32))
    lnM = es_top.enter_context(nc.sbuf_tensor("lnM", [8, 16], F32))
    bcS = es_top.enter_context(nc.sbuf_tensor("bcS", [128, 512], F32))
    zrec = es_top.enter_context(nc.sbuf_tensor("zrec", [128, 4], F32))
    outS = es_top.enter_context(nc.sbuf_tensor("outS", [128, 2 * 1024], F32))

    S = Sched(nc, es_top)

    esA = ExitStack()
    w16 = {t: esA.enter_context(nc.sbuf_tensor(f"w16_{t}", [128, 8 * D], BF16))
           for t in "qkv"}
    xt16 = [esA.enter_context(nc.sbuf_tensor(f"xt16_{i}", [128, 8 * 512], BF16))
            for i in range(2)]
    Qs = esA.enter_context(nc.sbuf_tensor("Qs", [128, 4096], BF16))
    Ks = esA.enter_context(nc.sbuf_tensor("Ks", [128, 4096], BF16))
    Q2s = esA.enter_context(nc.sbuf_tensor("Q2s", [128, 4096], BF16))
    VTs = esA.enter_context(nc.sbuf_tensor("VTs", [128, 4096], BF16))
    E2 = esA.enter_context(nc.sbuf_tensor("E2", [128, 2048], BF16))
    Vs = esA.enter_context(nc.sbuf_tensor("Vs", [128, 16 * 65], BF16))

    p_qkv = [esA.enter_context(nc.psum_tensor(f"p_qkv{i}", [128, 512], F32)) for i in range(2)]
    p_sc = [esA.enter_context(nc.psum_tensor(f"p_sc{i}", [128, 512], F32)) for i in range(2)]
    p_avs = [esA.enter_context(nc.psum_tensor(f"p_av{i}", [128, 260], F32)) for i in range(2)]
    p_tr = esA.enter_context(nc.psum_tensor("p_tr", [128, 256], F32))
    p_stbc = esA.enter_context(nc.psum_tensor("p_stbc", [128, 64], F32))

    for dst, src in [(sb_bq, d_bq), (sb_bk, d_bk), (sb_bv, d_bv), (sb_b1, d_b1),
                     (sb_b2, d_b2), (sb_mask, d_mask), (sb_sel8, d_sel8),
                     (sb_sel8t, d_sel8t), (sb_id16, d_id16), (sb_idf, d_idf),
                     (sb_eps, d_eps), (w128, d_cos)]:
        S.emit("sync", lambda e, dst=dst, src=src: e.dma_start(dst[:, :], src.ap()))
    m_consts = S.marker("sync")
    S.emit("sync", lambda e: e.dma_start(
        bass.AP(u, 0, [[NT * 1024, 128], [1024, NT], [1, 1024]]),
        bass.AP(d_xs, 0, [[1024, 128], [128 * 1024, NT], [1, 1024]])))
    m_xs = S.marker("sync")

    for t, d_w in [("q", d_wq), ("k", d_wk), ("v", d_wv)]:
        S.emit("gpsimd", lambda e, t=t, d_w=d_w: e.dma_start(
            bass.AP(w16[t], 0, [[8 * D, 128], [D, 8], [1, D]]),
            bass.AP(d_w, 0, [[D, 128], [128 * D, 8], [1, D]]),
        ))
    m_wqkv = S.marker("gpsimd")

    S.emit("vector", lambda e: e.tensor_mul(w128[:, :], w128[:, :], w128[:, :]),
           deps=[m_consts])
    m_usq_free = ("vector", 0)
    for h in range(2):
        S.emit("sync", lambda e, h=h: e.dma_start(
            usq[:, 0:1024], bass.AP(d_sin, 1024 * h, [[T, 128], [1, 1024]])),
            deps=[m_usq_free])
        m_sinh = S.marker("sync")
        S.emit("vector", lambda e: e.tensor_mul(usq[:, 0:1024], usq[:, 0:1024], usq[:, 0:1024]),
               deps=[m_sinh])
        S.emit("vector", lambda e, h=h: e.tensor_add(
            w128[:, 1024 * h: 1024 * h + 1024],
            w128[:, 1024 * h: 1024 * h + 1024], usq[:, 0:1024]))
        m_usq_free = S.marker("vector")
    m_w128 = S.marker("vector")

    S.emit("vector", lambda e: e.memset(bass.AP(Vs, 64, [[16 * 65, 128], [65, 16]]), 1.0))
    m_vsones = S.marker("vector")

    m_xt_used = [("tensor", 0), ("tensor", 0)]
    m_stacks_used = ("tensor", 0)
    m_evac_done_v = ("vector", 0)
    m_evac_done_s = ("scalar", 0)
    m_psum_used = [[("vector", 0)], [("vector", 0)]]
    m_q2_dma = ("sync", 0)
    m_e2_used = ("tensor", 0)
    m_vs_used = ("tensor", 0)
    m_av_used = [("vector", 0), ("vector", 0)]
    m_sc_used = [("scalar", 0), ("scalar", 0)]
    m_tr_used = {"vector": ("vector", 0), "scalar": ("scalar", 0)}
    sync_q_prev = ("sync", 0)

    for s in range(NSLAB):
        buf = s % 2
        S.emit("gpsimd", lambda e, s=s, buf=buf: e.dma_start(
            bass.AP(xt16[buf], 0, [[8 * 512, 128], [512, 8], [1, 512]]),
            bass.AP(d_xt, 512 * s, [[T, 128], [128 * T, 8], [1, 512]]),
        ), deps=[m_xt_used[buf]])
        m_xt = S.marker("gpsimd")

        pp = 0
        for tname in "qkv":
            for ch in range(8):
                ps = p_qkv[pp % 2]
                dep_evac = m_psum_used[pp % 2]
                if not isinstance(dep_evac, list):
                    dep_evac = [dep_evac]
                for kc in range(8):
                    S.emit("tensor", lambda e, tname=tname, ch=ch, kc=kc, ps=ps, buf=buf:
                           e.matmul(ps[:, :],
                                    w16[tname][:, D * kc + 128 * ch: D * kc + 128 * ch + 128],
                                    xt16[buf][:, 512 * kc: 512 * kc + 512],
                                    start=(kc == 0), stop=(kc == 7)),
                           deps=([m_wqkv, m_xt] + dep_evac if kc == 0 else []),
                           inc=(kc == 7))
                m_mm = S.marker("tensor")
                for half in range(2):
                    rows = slice(64 * half, 64 * half + 64)
                    ps_ap = lambda ps=ps, half=half: bass.AP(
                        ps, half * 64 * 512, [[512, 64], [128, 4], [1, 128]])
                    if tname == "k":
                        S.emit("vector", lambda e, ps=ps, rows=rows, ch=ch, half=half, s=s:
                               e.scalar_tensor_tensor(
                                   bass.AP(Ks, half * (64 * 4096) + 8 * ch,
                                           [[4096, 64], [1024, 4], [64, 16], [1, 8]]),
                                   bass.AP(ps, half * 64 * 512,
                                           [[512, 64], [128, 4], [8, 16], [1, 8]]),
                                   sb_bk[rows, ch: ch + 1],
                                   bass.AP(w128, half * 64 * T + 512 * s,
                                           [[T, 64], [128, 4], [8, 16], [1, 8]]),
                                   ALU.add, ALU.mult),
                               deps=[m_mm, m_w128, m_stacks_used, m_q2_dma, m_consts])
                    elif tname == "v":
                        S.emit("scalar", lambda e, ps=ps, rows=rows, ch=ch, half=half:
                               e.activation(
                                   bass.AP(VTs, half * (64 * 4096) + 8 * ch,
                                           [[4096, 64], [1024, 4], [64, 16], [1, 8]]),
                                   bass.AP(ps, half * 64 * 512,
                                           [[512, 64], [128, 4], [8, 16], [1, 8]]),
                                   AF.Identity, bias=sb_bv[rows, ch: ch + 1]),
                               deps=[m_mm, m_stacks_used, m_q2_dma, m_consts])
                    else:
                        if half == 0:
                            S.emit("scalar", lambda e, ps_ap=ps_ap, rows=rows, ch=ch, half=half:
                                   e.activation(
                                       bass.AP(Qs, half * (64 * 4096) + ch,
                                               [[4096, 64], [1024, 4], [8, 128]]),
                                       ps_ap(), AF.Identity, bias=sb_bq[rows, ch: ch + 1]),
                                   deps=[m_mm, m_stacks_used, m_q2_dma, m_consts])
                        else:
                            S.emit("vector", lambda e, ps_ap=ps_ap, rows=rows, ch=ch, half=half:
                                   e.tensor_scalar_add(
                                       bass.AP(Qs, half * (64 * 4096) + ch,
                                               [[4096, 64], [1024, 4], [8, 128]]),
                                       ps_ap(), sb_bq[rows, ch: ch + 1]),
                                   deps=[m_mm, m_stacks_used, m_q2_dma, m_consts])
                if tname == "k":
                    m_evac_done_v = S.marker("vector")
                    m_psum_used[pp % 2] = [m_evac_done_v]
                elif tname == "q":
                    m_evac_done_s = S.marker("scalar")
                    m_psum_used[pp % 2] = [m_evac_done_s, S.marker("vector")]
                else:
                    m_evac_done_s = S.marker("scalar")
                    m_psum_used[pp % 2] = [m_evac_done_s]
                pp += 1
        m_qevac = m_evac_done_s
        m_kevac = m_evac_done_v

        S.wait("sync", sync_q_prev)
        S.emit("sync", lambda e: e.dma_start(Q2s[0:64, :], Qs[64:128, :]),
               deps=[m_qevac, m_stacks_used])
        S.emit("sync", lambda e: e.dma_start(Q2s[64:128, :], Qs[0:64, :]))
        m_q2_dma = S.marker("sync")
        sync_q_prev = m_q2_dma

        for ti in range(4):
            tile = 4 * s + ti
            w0 = 1024 * ti

            for gb in range(4):
                for hi in range(2):
                    for gg in range(4):
                        g = 4 * gb + gg
                        S.emit("tensor", lambda e, hi=hi, g=g, gg=gg, w0=w0:
                               e.matmul(p_tr[64 * hi: 64 * hi + 64, 64 * gg: 64 * gg + 64],
                                        VTs[64 * hi: 64 * hi + 64, w0 + 64 * g: w0 + 64 * g + 64],
                                        sb_id16[64 * hi: 64 * hi + 64, :],
                                        start=True, stop=True),
                               deps=([m_kevac, m_qevac, m_consts,
                                      m_tr_used["vector"]] if (gg == 0 and hi == 0) else []),
                               inc=(gg == 3 and hi == 1))
                m_trp = S.marker("tensor")
                S.emit("vector", lambda e, gb=gb:
                       e.tensor_copy(
                           bass.AP(Vs, 65 * 4 * gb, [[16 * 65, 128], [65, 4], [1, 64]]),
                           p_tr[:, 0:256]),
                       deps=[m_trp, m_vsones, m_av_used[0], m_av_used[1]])
                m_tr_used["vector"] = S.marker("vector")

            m_vs_ready = S.marker("vector")

            def emit_scores(b):
                psb = p_sc[b % 2]
                for q4 in range(4):
                    g = 4 * b + q4
                    wg = w0 + 64 * g
                    S.emit("tensor", lambda e, wg=wg, q4=q4, psb=psb:
                           e.matmul(psb[0:64, 128 * q4: 128 * q4 + 64],
                                    Ks[0:64, wg: wg + 64], Qs[0:64, wg: wg + 64],
                                    start=True, stop=True),
                           deps=([m_q2_dma, m_sc_used[b % 2]] if q4 == 0 else []), inc=False)
                    S.emit("tensor", lambda e, wg=wg, q4=q4, psb=psb:
                           e.matmul(psb[0:64, 128 * q4 + 64: 128 * q4 + 128],
                                    Ks[0:64, wg: wg + 64], Q2s[0:64, wg: wg + 64],
                                    start=True, stop=True), inc=False)
                    S.emit("tensor", lambda e, wg=wg, q4=q4, psb=psb:
                           e.matmul(psb[64:128, 128 * q4: 128 * q4 + 64],
                                    Ks[64:128, wg: wg + 64], Q2s[64:128, wg: wg + 64],
                                    start=True, stop=True), inc=False)
                    S.emit("tensor", lambda e, wg=wg, q4=q4, psb=psb:
                           e.matmul(psb[64:128, 128 * q4 + 64: 128 * q4 + 128],
                                    Ks[64:128, wg: wg + 64], Qs[64:128, wg: wg + 64],
                                    start=True, stop=True), inc=(q4 == 3))
                m = S.marker("tensor")
                S.emit("scalar", lambda e, b=b, psb=psb: e.activation(
                    E2[:, 512 * b: 512 * b + 512], psb[:, :], AF.Exp, scale=0.125),
                    deps=[m, m_e2_used])
                m_sc_used[b % 2] = S.marker("scalar")
                S.emit("vector", lambda e, b=b: e.tensor_tensor(
                    E2[:, 512 * b: 512 * b + 512], E2[:, 512 * b: 512 * b + 512],
                    bass.AP(sb_mask, 0, [[128, 128], [0, 4], [1, 128]]), ALU.mult),
                    deps=[m_sc_used[b % 2], m_consts])
                return S.marker("vector")

            def emit_av(b, m_mask_b):
                pav = p_avs[b % 2]
                for q4 in range(4):
                    g = 4 * b + q4
                    S.emit("tensor", lambda e, g=g, q4=q4, pav=pav:
                           e.matmul(pav[:, 65 * q4: 65 * q4 + 65],
                                    E2[:, 128 * g: 128 * g + 128],
                                    Vs[:, 65 * g: 65 * g + 65],
                                    start=True, stop=True),
                           deps=([m_mask_b, m_vs_ready, m_av_used[b % 2]] if q4 == 0 else []),
                           inc=(q4 == 3))
                m_avmm = S.marker("tensor")
                S.emit("vector", lambda e, pav=pav: e.reciprocal(
                    zrec[:, 0:4], bass.AP(pav, 64, [[260, 128], [65, 4]])),
                    deps=[m_avmm, m_xs])
                for q4 in range(4):
                    g = 4 * b + q4
                    S.emit("vector", lambda e, g=g, q4=q4, tile=tile, pav=pav:
                           e.scalar_tensor_tensor(
                               u[:, 1024 * tile + 64 * g: 1024 * tile + 64 * g + 64],
                               pav[:, 65 * q4: 65 * q4 + 64],
                               zrec[:, q4: q4 + 1],
                               u[:, 1024 * tile + 64 * g: 1024 * tile + 64 * g + 64],
                               ALU.mult, ALU.add))
                m_av_used[b % 2] = S.marker("vector")
                return m_avmm

            m_masks = {}
            m_masks[0] = emit_scores(0)
            m_masks[1] = emit_scores(1)
            last_avmm = None
            for b in range(4):
                last_avmm = emit_av(b, m_masks[b])
                if b + 2 < 4:
                    m_masks[b + 2] = emit_scores(b + 2)
            m_e2_used = last_avmm


        m_stacks_used = S.marker("tensor")
        m_xt_used[buf] = S.marker("tensor")


    m_usq_rd = [("vector", 0), ("vector", 0)]

    for tile in range(NT):
        ub = tile % 2
        S.emit("scalar", lambda e, tile=tile, ub=ub: e.activation(
            usq[:, 1024 * ub: 1024 * ub + 1024],
            u[:, 1024 * tile: 1024 * tile + 1024], AF.Square),
            deps=[m_av_used[0], m_av_used[1]] + [m_usq_rd[ub]])
        m_usq_w = S.marker("scalar")
        S.emit("vector", lambda e, tile=tile: e.reduce_sum(
            lnS[:, 0:16],
            bass.AP(u, 1024 * tile, [[NT * 1024, 128], [64, 16], [1, 64]]),
            mybir.AxisListType.X))
        S.emit("vector", lambda e, ub=ub: e.reduce_sum(
            lnS[:, 16:32], bass.AP(usq, 1024 * ub, [[2048, 128], [64, 16], [1, 64]]),
            mybir.AxisListType.X), deps=[m_usq_w])
        m_usq_rd[ub] = S.marker("vector")
        m_s2 = S.marker("vector")
        S.emit("tensor", lambda e: e.matmul(
            p_stbc[0:8, 0:32], sb_sel8[:, :], lnS[:, :], start=True, stop=True),
            deps=[m_s2, m_consts])
        m_stmm = S.marker("tensor")
        S.emit("vector", lambda e: e.tensor_scalar_mul(lnT[:, :], p_stbc[0:8, 0:32], 1.0 / 1024.0),
               deps=[m_stmm])
        S.emit("vector", lambda e: e.tensor_mul(lnM[:, :], lnT[:, 0:16], lnT[:, 0:16]))
        S.emit("vector", lambda e: e.tensor_sub(lnM[:, :], lnT[:, 16:32], lnM[:, :]))
        m_var = S.marker("vector")
        S.emit("scalar", lambda e: e.activation(
            lnM[:, :], lnM[:, :], AF.Sqrt, bias=sb_eps[0:8, 0:1]), deps=[m_var, m_consts])
        m_std = S.marker("scalar")
        S.emit("vector", lambda e: e.reciprocal(lnT[:, 16:32], lnM[:, :]),
               deps=[m_std])
        S.emit("vector", lambda e: e.tensor_mul(lnT[:, 0:16], lnT[:, 0:16], lnT[:, 16:32]))
        m_mrs = S.marker("vector")
        S.emit("tensor", lambda e, tile=tile: e.matmul(
            p_stbc[:, 32:64], sb_sel8t[:, :], lnT[:, :], start=True, stop=True),
            deps=[m_mrs])
        m_bcmm = S.marker("tensor")
        S.emit("vector", lambda e, tile=tile: e.tensor_copy(
            bcS[:, 32 * tile: 32 * tile + 32], p_stbc[:, 32:64]), deps=[m_bcmm])
    for tile in range(NT):
        S.emit("vector", lambda e, tile=tile: e.tensor_tensor(
            bass.AP(u, 1024 * tile, [[NT * 1024, 128], [64, 16], [1, 64]]),
            bass.AP(u, 1024 * tile, [[NT * 1024, 128], [64, 16], [1, 64]]),
            bass.AP(bcS, 32 * tile + 16, [[512, 128], [1, 16], [0, 64]]), ALU.mult))
        S.emit("vector", lambda e, tile=tile: e.tensor_tensor(
            bass.AP(u, 1024 * tile, [[NT * 1024, 128], [64, 16], [1, 64]]),
            bass.AP(u, 1024 * tile, [[NT * 1024, 128], [64, 16], [1, 64]]),
            bass.AP(bcS, 32 * tile, [[512, 128], [1, 16], [0, 64]]), ALU.subtract))

    m_attn_pe = S.marker("tensor")
    m_attn_v = S.marker("vector")
    m_attn_s = S.marker("scalar")

    esA.close()

    if DEBUG_A:
        total_sync = S.cnt["sync"]
        if total_sync:
            S.ops["sync"].append(lambda e, v=16 * total_sync: e.wait_ge(S.sems["sync"], v))
        with nc.Block() as block:
            @block.sync
            def _(eng):
                for fn in S.ops["sync"]:
                    fn(eng)

            @block.gpsimd
            def _(eng):
                for fn in S.ops["gpsimd"]:
                    fn(eng)

            @block.tensor
            def _(eng):
                for fn in S.ops["tensor"]:
                    fn(eng)

            @block.vector
            def _(eng):
                for fn in S.ops["vector"]:
                    fn(eng)

            @block.scalar
            def _(eng):
                for fn in S.ops["scalar"]:
                    fn(eng)
        esA.close()
        es_top.close()
        return nc
    esB = ExitStack()
    u1p16 = esB.enter_context(nc.sbuf_tensor("u1p16", [64, 16 * RANK], BF16))
    v116 = esB.enter_context(nc.sbuf_tensor("v116", [128, 2 * DFF], BF16))
    u216 = esB.enter_context(nc.sbuf_tensor("u216", [128, 32 * RANK], BF16))
    v2p16 = esB.enter_context(nc.sbuf_tensor("v2p16", [128, 2 * 1024], BF16))
    y16 = esB.enter_context(nc.sbuf_tensor("y16", [128, 2048], BF16))
    YTl = esB.enter_context(nc.sbuf_tensor("YTl", [64, 2048], BF16))
    YTh = esB.enter_context(nc.sbuf_tensor("YTh", [64, 2048], BF16))
    P16 = esB.enter_context(nc.sbuf_tensor("P16", [128, 512], BF16))
    H16 = esB.enter_context(nc.sbuf_tensor("H16", [128, 32 * 256], BF16))
    Q2T = esB.enter_context(nc.sbuf_tensor("Q2T", [128, 512], BF16))
    FTs = esB.enter_context(nc.sbuf_tensor("FTs", [128, 2048], F32))

    p_a = esB.enter_context(nc.psum_tensor("p_a", [128, 256], F32))
    p_b = esB.enter_context(nc.psum_tensor("p_b", [128, 256], F32))
    p_c = esB.enter_context(nc.psum_tensor("p_c", [128, 256], F32))
    p_d = esB.enter_context(nc.psum_tensor("p_d", [128, 256], F32))
    p_fs = esB.enter_context(nc.psum_tensor("p_fs", [128, 256], F32))
    p_yt = esB.enter_context(nc.psum_tensor("p_yt", [64, 256], F32))
    p_st2 = esB.enter_context(nc.psum_tensor("p_st2", [128, 64], F32))

    gate = [m_attn_pe, m_attn_v, m_attn_s]
    S.emit("gpsimd", lambda e: e.dma_start(u1p16[:, :], d_u1.ap()), deps=gate)
    S.emit("gpsimd", lambda e: e.dma_start(
        bass.AP(v116, 0, [[2 * DFF, 128], [DFF, 2], [1, DFF]]),
        bass.AP(d_v1, 0, [[DFF, 128], [128 * DFF, 2], [1, DFF]])))
    S.emit("gpsimd", lambda e: e.dma_start(
        bass.AP(u216, 0, [[32 * RANK, 128], [RANK, 32], [1, RANK]]),
        bass.AP(d_u2, 0, [[RANK, 128], [128 * RANK, 32], [1, RANK]])))
    S.emit("gpsimd", lambda e: e.dma_start(
        bass.AP(v2p16, 0, [[2 * 1024, 128], [1024, 2], [1, 1024]]),
        bass.AP(d_v2, 0, [[1024, 128], [128 * 1024, 2], [1, 1024]])))
    m_wffn = S.marker("gpsimd")

    m_y16_used = ("tensor", 0)
    m_yt_used = ("vector", 0)
    m_yts_used = ("tensor", 0)
    m_p16_used = ("tensor", 0)
    m_h16_used = ("tensor", 0)
    m_q2t_used = ("tensor", 0)
    m_ft_used = ("tensor", 0)
    m_ftv_used = ("vector", 0)
    m_pab_used = ("vector", 0)
    m_pcd_used = ("scalar", 0)
    m_fs_used = ("vector", 0)
    m_outS_used = [("sync", 0), ("sync", 0)]
    n_store = 0

    for f in range(FSLAB):
        c0 = 2048 * f
        S.emit("vector", lambda e, c0=c0: e.tensor_copy(
            y16[:, :], u[:, c0: c0 + 2048]), deps=[m_y16_used])
        m_y16 = S.marker("vector")

        for hi in range(2):
            ytd = YTl if hi == 0 else YTh
            for blk in range(8):
                for gg in range(4):
                    gidx = 4 * blk + gg
                    S.emit("tensor", lambda e, hi=hi, gidx=gidx, gg=gg:
                           e.matmul(p_yt[:, 64 * gg: 64 * gg + 64],
                                    y16[64 * hi: 64 * hi + 64, 64 * gidx: 64 * gidx + 64],
                                    sb_id16[64 * hi: 64 * hi + 64, :],
                                    start=True, stop=True),
                           deps=([m_y16, m_yt_used, m_consts] if gg == 0 else []), inc=(gg == 3))
                m_ytp = S.marker("tensor")
                S.emit("vector", lambda e, ytd=ytd, blk=blk: e.tensor_copy(
                    ytd[:, 256 * blk: 256 * blk + 256], p_yt[:, :]),
                    deps=[m_ytp, m_yts_used])
                m_yt_used = S.marker("vector")
        m_yt_ready = S.marker("vector")

        for rh in range(2):
            ps = p_a if rh == 0 else p_b
            for hc in range(16):
                hi, ch = hc // 8, hc % 8
                yts = YTl if hi == 0 else YTh
                S.emit("tensor", lambda e, rh=rh, hi=hi, ch=ch, ps=ps, yts=yts, hc=hc:
                       e.matmul(ps[:, :],
                                u1p16[:, RANK * (ch + 8 * hi) + 128 * rh:
                                      RANK * (ch + 8 * hi) + 128 * rh + 128],
                                bass.AP(yts, ch, [[2048, 64], [64, 32], [8, 8]]),
                                start=(hc == 0), stop=(hc == 15)),
                       deps=([m_yt_ready, m_wffn, m_pab_used, m_consts] if hc == 0 else []),
                       inc=(hc == 15))
            m_pmm = S.marker("tensor")
            S.emit("vector", lambda e, rh=rh, ps=ps: e.tensor_copy(
                P16[:, 256 * rh: 256 * rh + 256], ps[:, :]), deps=[m_pmm, m_p16_used])
        m_pab_used = S.marker("vector")
        m_p16 = S.marker("vector")

        h_rot = [p_a, p_b, p_c, p_d]
        h_used = [m_pab_used, m_pab_used, m_pcd_used, m_pcd_used]
        for ffc in range(32):
            ps = h_rot[ffc % 4]
            for rc in range(2):
                S.emit("tensor", lambda e, ffc=ffc, rc=rc, ps=ps:
                       e.matmul(ps[:, :],
                                v116[:, DFF * rc + 128 * ffc: DFF * rc + 128 * ffc + 128],
                                P16[:, 256 * rc: 256 * rc + 256],
                                start=(rc == 0), stop=(rc == 1)),
                       deps=([m_p16, h_used[ffc % 4]] if rc == 0 else []), inc=(rc == 1))
            m_hmm = S.marker("tensor")
            S.emit("scalar", lambda e, ffc=ffc, ps=ps: e.activation(
                H16[:, 256 * ffc: 256 * ffc + 256], ps[:, :], AF.Gelu_apprx_tanh,
                bias=sb_b1[:, ffc: ffc + 1]), deps=[m_hmm, m_h16_used, m_consts])
            h_used[ffc % 4] = S.marker("scalar")
            if ffc % 2 == 1:
                m_pcd_used = S.marker("scalar")
        m_h16 = S.marker("scalar")

        for rh in range(2):
            ps = p_a if rh == 0 else p_b
            for ffc in range(32):
                S.emit("tensor", lambda e, rh=rh, ffc=ffc, ps=ps:
                       e.matmul(ps[:, :],
                                u216[:, RANK * ffc + 128 * rh: RANK * ffc + 128 * rh + 128],
                                H16[:, 256 * ffc: 256 * ffc + 256],
                                start=(ffc == 0), stop=(ffc == 31)),
                       deps=([m_h16, m_pab_used] if ffc == 0 else []), inc=(ffc == 31))
            m_qmm = S.marker("tensor")
            S.emit("vector", lambda e, rh=rh, ps=ps: e.tensor_copy(
                Q2T[:, 256 * rh: 256 * rh + 256], ps[:, :]), deps=[m_qmm, m_q2t_used])
        m_pab_used = S.marker("vector")
        m_q2t = S.marker("vector")

        for ch in range(8):
            ps = p_c if ch % 2 == 0 else p_d
            for rc in range(2):
                S.emit("tensor", lambda e, ch=ch, rc=rc, ps=ps:
                       e.matmul(ps[:, :],
                                v2p16[:, 1024 * rc + 128 * ch: 1024 * rc + 128 * ch + 128],
                                Q2T[:, 256 * rc: 256 * rc + 256],
                                start=(rc == 0), stop=(rc == 1)),
                       deps=([m_q2t, m_pcd_used, m_ftv_used] if rc == 0 else []), inc=(rc == 1))
            m_fmm = S.marker("tensor")
            S.emit("scalar", lambda e, ch=ch, ps=ps:
                   e.activation(
                       bass.AP(FTs, ch, [[2048, 64], [1024, 2], [8, 128]]),
                       ps[0:64, :], AF.Identity, bias=sb_b2[0:64, ch: ch + 1]),
                   deps=[m_fmm, m_ft_used, m_consts])
            S.emit("vector", lambda e, ch=ch, ps=ps:
                   e.tensor_scalar_add(
                       bass.AP(FTs, 64 * 2048 + ch, [[2048, 64], [1024, 2], [8, 128]]),
                       ps[64:128, :], sb_b2[64:128, ch: ch + 1]),
                   deps=[m_fmm, m_ft_used, m_consts])
            m_ftv_used = S.marker("vector")
            if ch % 2 == 1:
                m_pcd_used = S.marker("scalar")
        m_ft = S.marker("scalar")
        m_ft_v = S.marker("vector")

        for tp in range(2):
            tile = 2 * f + tp
            for gb in range(4):
                for hi in range(2):
                    for gg in range(4):
                        g = 4 * gb + gg
                        S.emit("tensor", lambda e, hi=hi, g=g, gg=gg, tp=tp:
                               e.matmul(p_fs[64 * hi: 64 * hi + 64, 64 * gg: 64 * gg + 64],
                                        FTs[64 * hi: 64 * hi + 64,
                                            1024 * tp + 64 * g: 1024 * tp + 64 * g + 64],
                                        sb_idf[64 * hi: 64 * hi + 64, :],
                                        start=True, stop=True),
                               deps=([m_ft, m_ft_v, m_fs_used, m_consts] if (hi == 0 and gg == 0) else []),
                               inc=(hi == 1 and gg == 3))
                m_fsmm = S.marker("tensor")
                S.emit("vector", lambda e, tile=tile, gb=gb: e.scalar_tensor_tensor(
                    u[:, 1024 * tile + 256 * gb: 1024 * tile + 256 * gb + 256],
                    p_fs[:, :], 1.0,
                    u[:, 1024 * tile + 256 * gb: 1024 * tile + 256 * gb + 256],
                    ALU.mult, ALU.add), deps=[m_fsmm])
                m_fs_used = S.marker("vector")


    m_usq_rd = [("vector", 0), ("vector", 0)]

    for tile in range(NT):
        ub = tile % 2
        S.emit("scalar", lambda e, tile=tile, ub=ub: e.activation(
            usq[:, 1024 * ub: 1024 * ub + 1024],
            u[:, 1024 * tile: 1024 * tile + 1024], AF.Square),
            deps=[m_fs_used] + [m_usq_rd[ub]])
        m_usq_w = S.marker("scalar")
        S.emit("vector", lambda e, tile=tile: e.reduce_sum(
            lnS[:, 0:16],
            bass.AP(u, 1024 * tile, [[NT * 1024, 128], [64, 16], [1, 64]]),
            mybir.AxisListType.X), deps=[m_fs_used])
        S.emit("vector", lambda e, ub=ub: e.reduce_sum(
            lnS[:, 16:32], bass.AP(usq, 1024 * ub, [[2048, 128], [64, 16], [1, 64]]),
            mybir.AxisListType.X), deps=[m_usq_w])
        m_usq_rd[ub] = S.marker("vector")
        m_s2 = S.marker("vector")
        S.emit("tensor", lambda e: e.matmul(
            p_st2[0:8, 0:32], sb_sel8[:, :], lnS[:, :], start=True, stop=True),
            deps=[m_s2, m_consts])
        m_stmm = S.marker("tensor")
        S.emit("vector", lambda e: e.tensor_scalar_mul(lnT[:, :], p_st2[0:8, 0:32], 1.0 / 1024.0),
               deps=[m_stmm])
        S.emit("vector", lambda e: e.tensor_mul(lnM[:, :], lnT[:, 0:16], lnT[:, 0:16]))
        S.emit("vector", lambda e: e.tensor_sub(lnM[:, :], lnT[:, 16:32], lnM[:, :]))
        m_var = S.marker("vector")
        S.emit("scalar", lambda e: e.activation(
            lnM[:, :], lnM[:, :], AF.Sqrt, bias=sb_eps[0:8, 0:1]), deps=[m_var, m_consts])
        m_std = S.marker("scalar")
        S.emit("vector", lambda e: e.reciprocal(lnT[:, 16:32], lnM[:, :]),
               deps=[m_std])
        S.emit("vector", lambda e: e.tensor_mul(lnT[:, 0:16], lnT[:, 0:16], lnT[:, 16:32]))
        m_mrs = S.marker("vector")
        S.emit("tensor", lambda e, tile=tile: e.matmul(
            p_st2[:, 32:64], sb_sel8t[:, :], lnT[:, :], start=True, stop=True),
            deps=[m_mrs])
        m_bcmm = S.marker("tensor")
        S.emit("vector", lambda e, tile=tile: e.tensor_copy(
            bcS[:, 32 * tile: 32 * tile + 32], p_st2[:, 32:64]), deps=[m_bcmm])
    for tile in range(NT):
        ob = tile % 2
        S.emit("vector", lambda e, tile=tile, ob=ob: e.tensor_tensor(
            bass.AP(outS, 1024 * ob, [[2048, 128], [64, 16], [1, 64]]),
            bass.AP(u, 1024 * tile, [[NT * 1024, 128], [64, 16], [1, 64]]),
            bass.AP(bcS, 32 * tile + 16, [[512, 128], [1, 16], [0, 64]]), ALU.mult),
            deps=[m_outS_used[ob]])
        S.emit("vector", lambda e, tile=tile, ob=ob: e.tensor_tensor(
            bass.AP(outS, 1024 * ob, [[2048, 128], [64, 16], [1, 64]]),
            bass.AP(outS, 1024 * ob, [[2048, 128], [64, 16], [1, 64]]),
            bass.AP(bcS, 32 * tile, [[512, 128], [1, 16], [0, 64]]), ALU.subtract))
        m_norm = S.marker("vector")
        S.emit("sync", lambda e, tile=tile, ob=ob: e.dma_start(
            bass.AP(d_out, 128 * 1024 * tile, [[1024, 128], [1, 1024]]),
            outS[:, 1024 * ob: 1024 * ob + 1024]), deps=[m_norm])
        m_outS_used[ob] = S.marker("sync")

    total_sync = S.cnt["sync"]
    S.ops["sync"].append(lambda e, v=16 * total_sync: e.wait_ge(S.sems["sync"], v))

    with nc.Block() as block:
        @block.sync
        def _(eng):
            for fn in S.ops["sync"]:
                fn(eng)

        @block.gpsimd
        def _(eng):
            for fn in S.ops["gpsimd"]:
                fn(eng)

        @block.tensor
        def _(eng):
            for fn in S.ops["tensor"]:
                fn(eng)

        @block.vector
        def _(eng):
            for fn in S.ops["vector"]:
                fn(eng)

        @block.scalar
        def _(eng):
            for fn in S.ops["scalar"]:
                fn(eng)

    esB.close()
    es_top.close()
    return nc



def _rope_tables():
    freqs = (10000.0 ** (-(np.arange(0, DK, 2, dtype=np.float64) / D))).astype(np.float32)
    ang = (np.arange(NCORES * T, dtype=np.float32)[:, None] * freqs[None, :]).astype(np.float32)
    libm = ctypes.CDLL(ctypes.util.find_library("m"))
    libm.cosf.restype = ctypes.c_float
    libm.cosf.argtypes = [ctypes.c_float]
    libm.sinf.restype = ctypes.c_float
    libm.sinf.argtypes = [ctypes.c_float]
    flat = ang.ravel()
    cos = np.fromiter((libm.cosf(float(v)) for v in flat), np.float32, flat.size).reshape(ang.shape)
    sin = np.fromiter((libm.sinf(float(v)) for v in flat), np.float32, flat.size).reshape(ang.shape)
    return cos, sin


_NC_CACHE = {}


def kernel(L0_attn_x, L0_attn_Wq, L0_attn_bq, L0_attn_Wk, L0_attn_bk,
           L0_attn_Wv, L0_attn_bv, L0_ffn_U1, L0_ffn_V1, L0_ffn_b1,
           L0_ffn_U2, L0_ffn_V2, L0_ffn_b2):
    x = np.ascontiguousarray(np.asarray(L0_attn_x, dtype=np.float32).reshape(NCORES * T, D))

    perm = np.concatenate([
        np.concatenate([np.arange(64 * c, 64 * c + 64),
                        np.arange(64 * (c + 8), 64 * (c + 8) + 64)])
        for c in range(8)])
    wq = np.ascontiguousarray(np.asarray(L0_attn_Wq, np.float32)[:, perm])
    wk = np.ascontiguousarray(np.asarray(L0_attn_Wk, np.float32)[:, perm])
    wv = np.ascontiguousarray(np.asarray(L0_attn_Wv, np.float32)[:, perm])
    v2p = np.ascontiguousarray(np.asarray(L0_ffn_V2, np.float32)[:, perm])

    def bias_chunks(b):
        return np.ascontiguousarray(np.asarray(b, np.float32)[perm].reshape(8, 128).T)

    bqp, bkp, bvp = (bias_chunks(b) for b in (L0_attn_bq, L0_attn_bk, L0_attn_bv))
    b2p = bias_chunks(L0_ffn_b2)
    u1p = np.ascontiguousarray(
        np.asarray(L0_ffn_U1, np.float32).reshape(16, 64, RANK).transpose(1, 0, 2).reshape(64, 16 * RANK))
    v1r = np.ascontiguousarray(np.asarray(L0_ffn_V1, np.float32))
    u2r = np.ascontiguousarray(np.asarray(L0_ffn_U2, np.float32))
    b1p = np.ascontiguousarray(np.asarray(L0_ffn_b1, np.float32).reshape(32, 128).T)

    cos, sin = _rope_tables()

    mask01 = np.zeros((128, 128), np.float32)
    for r in range(128):
        for n in range(128):
            if (r % 8) == ((n % 64) >> 3):
                mask01[r, n] = 1.0
    mask01 = mask01.astype(ml_dtypes.bfloat16)
    sel8 = np.zeros((128, 8), np.float32)
    for row in range(128):
        sel8[row, (row % 64) >> 3] = 1.0
    sel8t = np.ascontiguousarray(sel8.T)
    id16 = np.concatenate([np.eye(64, dtype=ml_dtypes.bfloat16)] * 2, 0)
    idf = np.concatenate([np.eye(64, dtype=np.float32)] * 2, 0)
    epsc = np.full((128, 1), 1e-5, np.float32)

    if "nc" not in _NC_CACHE:
        _NC_CACHE["nc"] = build_nc()
    nc = _NC_CACHE["nc"]

    in_maps = []
    for c in range(NCORES):
        xc = x[T * c: T * (c + 1)]
        xt = np.ascontiguousarray(xc.T)
        v = xc.reshape(NT, 16, 8, 2, 8, 64)
        xs = np.ascontiguousarray(v.transpose(0, 3, 2, 4, 1, 5).reshape(T, D))
        cos128 = np.ascontiguousarray(
            np.tile(np.repeat(cos[T * c: T * (c + 1)].T, 2, axis=0), (2, 1)))
        sin128 = np.ascontiguousarray(
            np.tile(np.repeat(sin[T * c: T * (c + 1)].T, 2, axis=0), (2, 1)))
        in_maps.append({
            "xt": xt, "xs": xs, "wq": wq, "wk": wk, "wv": wv,
            "bqp": bqp, "bkp": bkp, "bvp": bvp,
            "u1p": u1p, "v1r": v1r, "b1p": b1p, "u2r": u2r, "v2p": v2p, "b2p": b2p,
            "cos128": cos128, "sin128": sin128,
            "mask01": mask01, "sel8": sel8, "sel8t": sel8t,
            "id16": id16, "idf": idf, "epsc": epsc,
        })

    res = run_bass_kernel_spmd(nc, in_maps, core_ids=list(range(NCORES)))

    out = np.empty((NCORES * T, D), np.float32)
    for c in range(NCORES):
        o = np.asarray(res.results[c]["outs"], np.float32)
        v = o.reshape(NT, 2, 8, 8, 16, 64)
        out[T * c: T * (c + 1)] = v.transpose(0, 4, 2, 1, 3, 5).reshape(T, D)
    return out.reshape(1, NCORES * T, D)


# version 10
# speedup vs baseline: 1.4952x; 1.0681x over previous
import ctypes
import ctypes.util
from contextlib import ExitStack

import numpy as np
import ml_dtypes

import concourse.bass as bass
import concourse.mybir as mybir
from concourse.bass_utils import run_bass_kernel_spmd

F32 = mybir.dt.float32
BF16 = mybir.dt.bfloat16
AF = mybir.ActivationFunctionType
ALU = mybir.AluOpType

NCORES = 8
T = 2048
NT = 16
D = 1024
NH = 16
DK = 64
RANK = 256
DFF = 4096
NSLAB = 4
FSLAB = 4

ENGS = ("sync", "gpsimd", "tensor", "vector", "scalar")
DEBUG_A = False


class Sched:

    def __init__(self, nc, es):
        self.nc = nc
        self.ops = {e: [] for e in ENGS}
        self.cnt = {e: 0 for e in ENGS}
        self.hw = {e: {s: 0 for s in ENGS} for e in ENGS}
        self.sems = {e: es.enter_context(nc.semaphore(f"sem_{e}")) for e in ENGS}

    def mult(self, e):
        return 16 if e in ("sync", "gpsimd") else 1

    def throttle(self, eng):
        v = self.cnt[eng] * self.mult(eng)
        if v > 0:
            sem = self.sems[eng]
            self.ops[eng].append(lambda e, sem=sem, v=v: e.wait_ge(sem, v))

    def wait(self, eng, dep):
        src, val = dep
        if src == eng or val <= 0:
            return
        if self.hw[eng][src] >= val:
            return
        self.hw[eng][src] = val
        sem = self.sems[src]
        v = val * self.mult(src)
        self.ops[eng].append(lambda e, sem=sem, v=v: e.wait_ge(sem, v))

    def emit(self, eng, fn, deps=(), inc=True):
        for d in deps:
            self.wait(eng, d)
        if eng in ("sync", "gpsimd"):
            self.throttle(eng)
        elif inc and eng != "tensor" and self.cnt[eng] > 0:
            sem, v = self.sems[eng], self.cnt[eng]
            self.ops[eng].append(lambda e, sem=sem, v=v: e.wait_ge(sem, v))
        if inc:
            self.cnt[eng] += 1
            sem, m = self.sems[eng], self.mult(eng)
            self.ops[eng].append(lambda e, fn=fn, sem=sem, m=m: fn(e).then_inc(sem, m))
        else:
            self.ops[eng].append(fn)
        return (eng, self.cnt[eng])

    def marker(self, eng):
        return (eng, self.cnt[eng])


def build_nc():
    nc = bass.Bass("TRN2", target_bir_lowering=False)
    es_top = ExitStack()

    d_xt = nc.dram_tensor("xt", [D, T], F32, kind="ExternalInput")
    d_xs = nc.dram_tensor("xs", [T, D], F32, kind="ExternalInput")
    d_wq = nc.dram_tensor("wq", [D, D], F32, kind="ExternalInput")
    d_wk = nc.dram_tensor("wk", [D, D], F32, kind="ExternalInput")
    d_wv = nc.dram_tensor("wv", [D, D], F32, kind="ExternalInput")
    d_bq = nc.dram_tensor("bqp", [128, 8], F32, kind="ExternalInput")
    d_bk = nc.dram_tensor("bkp", [128, 8], F32, kind="ExternalInput")
    d_bv = nc.dram_tensor("bvp", [128, 8], F32, kind="ExternalInput")
    d_u1 = nc.dram_tensor("u1p", [64, 16 * RANK], F32, kind="ExternalInput")
    d_v1 = nc.dram_tensor("v1r", [RANK, DFF], F32, kind="ExternalInput")
    d_b1 = nc.dram_tensor("b1p", [128, 32], F32, kind="ExternalInput")
    d_u2 = nc.dram_tensor("u2r", [DFF, RANK], F32, kind="ExternalInput")
    d_v2 = nc.dram_tensor("v2p", [RANK, D], F32, kind="ExternalInput")
    d_b2 = nc.dram_tensor("b2p", [128, 8], F32, kind="ExternalInput")
    d_cos = nc.dram_tensor("cos128", [128, T], F32, kind="ExternalInput")
    d_sin = nc.dram_tensor("sin128", [128, T], F32, kind="ExternalInput")
    d_mask = nc.dram_tensor("mask01", [128, 128], BF16, kind="ExternalInput")
    d_sel8 = nc.dram_tensor("sel8", [128, 8], F32, kind="ExternalInput")
    d_sel8t = nc.dram_tensor("sel8t", [8, 128], F32, kind="ExternalInput")
    d_id16 = nc.dram_tensor("id16", [128, 64], BF16, kind="ExternalInput")
    d_idf = nc.dram_tensor("idf", [128, 64], F32, kind="ExternalInput")
    d_eps = nc.dram_tensor("epsc", [128, 1], F32, kind="ExternalInput")
    d_out = nc.dram_tensor("outs", [T, D], F32, kind="ExternalOutput")

    u = es_top.enter_context(nc.sbuf_tensor("u", [128, NT * 1024], F32))
    usq = es_top.enter_context(nc.sbuf_tensor("usq", [128, 2048], F32))
    sb_bq = es_top.enter_context(nc.sbuf_tensor("sb_bq", [128, 8], F32))
    sb_bk = es_top.enter_context(nc.sbuf_tensor("sb_bk", [128, 8], F32))
    sb_bv = es_top.enter_context(nc.sbuf_tensor("sb_bv", [128, 8], F32))
    sb_b1 = es_top.enter_context(nc.sbuf_tensor("sb_b1", [128, 32], F32))
    sb_b2 = es_top.enter_context(nc.sbuf_tensor("sb_b2", [128, 8], F32))
    sb_mask = es_top.enter_context(nc.sbuf_tensor("sb_mask", [128, 128], BF16))
    sb_sel8 = es_top.enter_context(nc.sbuf_tensor("sb_sel8", [128, 8], F32))
    sb_sel8t = es_top.enter_context(nc.sbuf_tensor("sb_sel8t", [8, 128], F32))
    sb_id16 = es_top.enter_context(nc.sbuf_tensor("sb_id16", [128, 64], BF16))
    sb_idf = es_top.enter_context(nc.sbuf_tensor("sb_idf", [128, 64], F32))
    sb_eps = es_top.enter_context(nc.sbuf_tensor("sb_eps", [128, 1], F32))
    lnS = es_top.enter_context(nc.sbuf_tensor("lnS", [128, 32], F32))
    lnT = es_top.enter_context(nc.sbuf_tensor("lnT", [8, 32], F32))
    lnM = es_top.enter_context(nc.sbuf_tensor("lnM", [8, 16], F32))
    bcS = es_top.enter_context(nc.sbuf_tensor("bcS", [128, 512], F32))
    zrec = es_top.enter_context(nc.sbuf_tensor("zrec", [128, 4], F32))
    outS = es_top.enter_context(nc.sbuf_tensor("outS", [128, 2 * 1024], F32))

    S = Sched(nc, es_top)

    esA = ExitStack()
    w128 = esA.enter_context(nc.sbuf_tensor("w128", [128, T], F32))
    w16 = {t: esA.enter_context(nc.sbuf_tensor(f"w16_{t}", [128, 8 * D], BF16))
           for t in "qkv"}
    xt16 = [esA.enter_context(nc.sbuf_tensor(f"xt16_{i}", [128, 8 * 512], BF16))
            for i in range(2)]
    Qs = esA.enter_context(nc.sbuf_tensor("Qs", [128, 4096], BF16))
    Ks = esA.enter_context(nc.sbuf_tensor("Ks", [128, 4096], BF16))
    Q2s = esA.enter_context(nc.sbuf_tensor("Q2s", [128, 4096], BF16))
    VTs = esA.enter_context(nc.sbuf_tensor("VTs", [128, 4096], BF16))
    E2 = esA.enter_context(nc.sbuf_tensor("E2", [128, 2048], BF16))
    Vs = esA.enter_context(nc.sbuf_tensor("Vs", [128, 16 * 65], BF16))

    p_qkv = [esA.enter_context(nc.psum_tensor(f"p_qkv{i}", [128, 512], F32)) for i in range(2)]
    p_sc = [esA.enter_context(nc.psum_tensor(f"p_sc{i}", [128, 512], F32)) for i in range(2)]
    p_avs = [esA.enter_context(nc.psum_tensor(f"p_av{i}", [128, 260], F32)) for i in range(2)]
    p_tr = esA.enter_context(nc.psum_tensor("p_tr", [128, 256], F32))
    p_stbc = esA.enter_context(nc.psum_tensor("p_stbc", [128, 64], F32))

    for dst, src in [(sb_bq, d_bq), (sb_bk, d_bk), (sb_bv, d_bv), (sb_b1, d_b1),
                     (sb_b2, d_b2), (sb_mask, d_mask), (sb_sel8, d_sel8),
                     (sb_sel8t, d_sel8t), (sb_id16, d_id16), (sb_idf, d_idf),
                     (sb_eps, d_eps), (w128, d_cos)]:
        S.emit("sync", lambda e, dst=dst, src=src: e.dma_start(dst[:, :], src.ap()))
    m_consts = S.marker("sync")
    S.emit("sync", lambda e: e.dma_start(
        bass.AP(u, 0, [[NT * 1024, 128], [1024, NT], [1, 1024]]),
        bass.AP(d_xs, 0, [[1024, 128], [128 * 1024, NT], [1, 1024]])))
    m_xs = S.marker("sync")

    for t, d_w in [("q", d_wq), ("k", d_wk), ("v", d_wv)]:
        S.emit("gpsimd", lambda e, t=t, d_w=d_w: e.dma_start(
            bass.AP(w16[t], 0, [[8 * D, 128], [D, 8], [1, D]]),
            bass.AP(d_w, 0, [[D, 128], [128 * D, 8], [1, D]]),
        ))
    m_wqkv = S.marker("gpsimd")

    S.emit("vector", lambda e: e.tensor_mul(w128[:, :], w128[:, :], w128[:, :]),
           deps=[m_consts])
    m_usq_free = ("vector", 0)
    for h in range(2):
        S.emit("sync", lambda e, h=h: e.dma_start(
            usq[:, 0:1024], bass.AP(d_sin, 1024 * h, [[T, 128], [1, 1024]])),
            deps=[m_usq_free])
        m_sinh = S.marker("sync")
        S.emit("vector", lambda e: e.tensor_mul(usq[:, 0:1024], usq[:, 0:1024], usq[:, 0:1024]),
               deps=[m_sinh])
        S.emit("vector", lambda e, h=h: e.tensor_add(
            w128[:, 1024 * h: 1024 * h + 1024],
            w128[:, 1024 * h: 1024 * h + 1024], usq[:, 0:1024]))
        m_usq_free = S.marker("vector")
    m_w128 = S.marker("vector")

    S.emit("vector", lambda e: e.memset(bass.AP(Vs, 64, [[16 * 65, 128], [65, 16]]), 1.0))
    m_vsones = S.marker("vector")

    m_xt_used = [("tensor", 0), ("tensor", 0)]
    m_stacks_used = ("tensor", 0)
    m_evac_done_v = ("vector", 0)
    m_evac_done_s = ("scalar", 0)
    m_psum_used = [[("vector", 0)], [("vector", 0)]]
    m_q2_dma = ("sync", 0)
    m_e2_used = ("tensor", 0)
    m_vs_used = ("tensor", 0)
    m_av_used = [("vector", 0), ("vector", 0)]
    m_sc_used = [("scalar", 0), ("scalar", 0)]
    m_tr_used = {"vector": ("vector", 0), "scalar": ("scalar", 0)}
    sync_q_prev = ("sync", 0)

    for s in range(NSLAB):
        buf = s % 2
        S.emit("gpsimd", lambda e, s=s, buf=buf: e.dma_start(
            bass.AP(xt16[buf], 0, [[8 * 512, 128], [512, 8], [1, 512]]),
            bass.AP(d_xt, 512 * s, [[T, 128], [128 * T, 8], [1, 512]]),
        ), deps=[m_xt_used[buf]])
        m_xt = S.marker("gpsimd")

        pp = 0
        for tname in "qkv":
            for ch in range(8):
                ps = p_qkv[pp % 2]
                dep_evac = m_psum_used[pp % 2]
                if not isinstance(dep_evac, list):
                    dep_evac = [dep_evac]
                for kc in range(8):
                    S.emit("tensor", lambda e, tname=tname, ch=ch, kc=kc, ps=ps, buf=buf:
                           e.matmul(ps[:, :],
                                    w16[tname][:, D * kc + 128 * ch: D * kc + 128 * ch + 128],
                                    xt16[buf][:, 512 * kc: 512 * kc + 512],
                                    start=(kc == 0), stop=(kc == 7)),
                           deps=([m_wqkv, m_xt] + dep_evac if kc == 0 else []),
                           inc=(kc == 7))
                m_mm = S.marker("tensor")
                for half in range(2):
                    rows = slice(64 * half, 64 * half + 64)
                    ps_ap = lambda ps=ps, half=half: bass.AP(
                        ps, half * 64 * 512, [[512, 64], [128, 4], [1, 128]])
                    if tname == "k":
                        S.emit("vector", lambda e, ps=ps, rows=rows, ch=ch, half=half, s=s:
                               e.scalar_tensor_tensor(
                                   bass.AP(Ks, half * (64 * 4096) + 8 * ch,
                                           [[4096, 64], [1024, 4], [64, 16], [1, 8]]),
                                   bass.AP(ps, half * 64 * 512,
                                           [[512, 64], [128, 4], [8, 16], [1, 8]]),
                                   sb_bk[rows, ch: ch + 1],
                                   bass.AP(w128, half * 64 * T + 512 * s,
                                           [[T, 64], [128, 4], [8, 16], [1, 8]]),
                                   ALU.add, ALU.mult),
                               deps=[m_mm, m_w128, m_stacks_used, m_q2_dma, m_consts])
                    elif tname == "v":
                        S.emit("scalar", lambda e, ps=ps, rows=rows, ch=ch, half=half:
                               e.activation(
                                   bass.AP(VTs, half * (64 * 4096) + 8 * ch,
                                           [[4096, 64], [1024, 4], [64, 16], [1, 8]]),
                                   bass.AP(ps, half * 64 * 512,
                                           [[512, 64], [128, 4], [8, 16], [1, 8]]),
                                   AF.Identity, bias=sb_bv[rows, ch: ch + 1]),
                               deps=[m_mm, m_stacks_used, m_q2_dma, m_consts])
                    else:
                        if half == 0:
                            S.emit("scalar", lambda e, ps_ap=ps_ap, rows=rows, ch=ch, half=half:
                                   e.activation(
                                       bass.AP(Qs, half * (64 * 4096) + ch,
                                               [[4096, 64], [1024, 4], [8, 128]]),
                                       ps_ap(), AF.Identity, bias=sb_bq[rows, ch: ch + 1]),
                                   deps=[m_mm, m_stacks_used, m_q2_dma, m_consts])
                        else:
                            S.emit("vector", lambda e, ps_ap=ps_ap, rows=rows, ch=ch, half=half:
                                   e.tensor_scalar_add(
                                       bass.AP(Qs, half * (64 * 4096) + ch,
                                               [[4096, 64], [1024, 4], [8, 128]]),
                                       ps_ap(), sb_bq[rows, ch: ch + 1]),
                                   deps=[m_mm, m_stacks_used, m_q2_dma, m_consts])
                if tname == "k":
                    m_evac_done_v = S.marker("vector")
                    m_psum_used[pp % 2] = [m_evac_done_v]
                elif tname == "q":
                    m_evac_done_s = S.marker("scalar")
                    m_psum_used[pp % 2] = [m_evac_done_s, S.marker("vector")]
                else:
                    m_evac_done_s = S.marker("scalar")
                    m_psum_used[pp % 2] = [m_evac_done_s]
                pp += 1
        m_qevac = m_evac_done_s
        m_kevac = m_evac_done_v

        S.wait("sync", sync_q_prev)
        S.emit("sync", lambda e: e.dma_start(Q2s[0:64, :], Qs[64:128, :]),
               deps=[m_qevac, m_stacks_used])
        S.emit("sync", lambda e: e.dma_start(Q2s[64:128, :], Qs[0:64, :]))
        m_q2_dma = S.marker("sync")
        sync_q_prev = m_q2_dma

        for ti in range(4):
            tile = 4 * s + ti
            w0 = 1024 * ti

            for gb in range(4):
                for hi in range(2):
                    for gg in range(4):
                        g = 4 * gb + gg
                        S.emit("tensor", lambda e, hi=hi, g=g, gg=gg, w0=w0:
                               e.matmul(p_tr[64 * hi: 64 * hi + 64, 64 * gg: 64 * gg + 64],
                                        VTs[64 * hi: 64 * hi + 64, w0 + 64 * g: w0 + 64 * g + 64],
                                        sb_id16[64 * hi: 64 * hi + 64, :],
                                        start=True, stop=True),
                               deps=([m_kevac, m_qevac, m_consts,
                                      m_tr_used["vector"]] if (gg == 0 and hi == 0) else []),
                               inc=(gg == 3 and hi == 1))
                m_trp = S.marker("tensor")
                S.emit("vector", lambda e, gb=gb:
                       e.tensor_copy(
                           bass.AP(Vs, 65 * 4 * gb, [[16 * 65, 128], [65, 4], [1, 64]]),
                           p_tr[:, 0:256]),
                       deps=[m_trp, m_vsones, m_av_used[0], m_av_used[1]])
                m_tr_used["vector"] = S.marker("vector")

            m_vs_ready = S.marker("vector")

            def emit_scores(b):
                psb = p_sc[b % 2]
                for q4 in range(4):
                    g = 4 * b + q4
                    wg = w0 + 64 * g
                    S.emit("tensor", lambda e, wg=wg, q4=q4, psb=psb:
                           e.matmul(psb[0:64, 128 * q4: 128 * q4 + 64],
                                    Ks[0:64, wg: wg + 64], Qs[0:64, wg: wg + 64],
                                    start=True, stop=True),
                           deps=([m_q2_dma, m_sc_used[b % 2]] if q4 == 0 else []), inc=False)
                    S.emit("tensor", lambda e, wg=wg, q4=q4, psb=psb:
                           e.matmul(psb[0:64, 128 * q4 + 64: 128 * q4 + 128],
                                    Ks[0:64, wg: wg + 64], Q2s[0:64, wg: wg + 64],
                                    start=True, stop=True), inc=False)
                    S.emit("tensor", lambda e, wg=wg, q4=q4, psb=psb:
                           e.matmul(psb[64:128, 128 * q4: 128 * q4 + 64],
                                    Ks[64:128, wg: wg + 64], Q2s[64:128, wg: wg + 64],
                                    start=True, stop=True), inc=False)
                    S.emit("tensor", lambda e, wg=wg, q4=q4, psb=psb:
                           e.matmul(psb[64:128, 128 * q4 + 64: 128 * q4 + 128],
                                    Ks[64:128, wg: wg + 64], Qs[64:128, wg: wg + 64],
                                    start=True, stop=True), inc=(q4 == 3))
                m = S.marker("tensor")
                S.emit("scalar", lambda e, b=b, psb=psb: e.activation(
                    E2[:, 512 * b: 512 * b + 512], psb[:, :], AF.Exp, scale=0.125),
                    deps=[m, m_e2_used])
                m_sc_used[b % 2] = S.marker("scalar")
                S.emit("vector", lambda e, b=b: e.tensor_tensor(
                    E2[:, 512 * b: 512 * b + 512], E2[:, 512 * b: 512 * b + 512],
                    bass.AP(sb_mask, 0, [[128, 128], [0, 4], [1, 128]]), ALU.mult),
                    deps=[m_sc_used[b % 2], m_consts])
                return S.marker("vector")

            def emit_av(b, m_mask_b):
                pav = p_avs[b % 2]
                for q4 in range(4):
                    g = 4 * b + q4
                    S.emit("tensor", lambda e, g=g, q4=q4, pav=pav:
                           e.matmul(pav[:, 65 * q4: 65 * q4 + 65],
                                    E2[:, 128 * g: 128 * g + 128],
                                    Vs[:, 65 * g: 65 * g + 65],
                                    start=True, stop=True),
                           deps=([m_mask_b, m_vs_ready, m_av_used[b % 2]] if q4 == 0 else []),
                           inc=(q4 == 3))
                m_avmm = S.marker("tensor")
                S.emit("vector", lambda e, pav=pav: e.reciprocal(
                    zrec[:, 0:4], bass.AP(pav, 64, [[260, 128], [65, 4]])),
                    deps=[m_avmm, m_xs])
                for q4 in range(4):
                    g = 4 * b + q4
                    S.emit("vector", lambda e, g=g, q4=q4, tile=tile, pav=pav:
                           e.scalar_tensor_tensor(
                               u[:, 1024 * tile + 64 * g: 1024 * tile + 64 * g + 64],
                               pav[:, 65 * q4: 65 * q4 + 64],
                               zrec[:, q4: q4 + 1],
                               u[:, 1024 * tile + 64 * g: 1024 * tile + 64 * g + 64],
                               ALU.mult, ALU.add))
                m_av_used[b % 2] = S.marker("vector")
                return m_avmm

            m_masks = {}
            m_masks[0] = emit_scores(0)
            m_masks[1] = emit_scores(1)
            last_avmm = None
            for b in range(4):
                last_avmm = emit_av(b, m_masks[b])
                if b + 2 < 4:
                    m_masks[b + 2] = emit_scores(b + 2)
            m_e2_used = last_avmm


        m_stacks_used = S.marker("tensor")
        m_xt_used[buf] = S.marker("tensor")


    m_usq_rd = [("vector", 0), ("vector", 0)]

    for tile in range(NT):
        ub = tile % 2
        S.emit("scalar", lambda e, tile=tile, ub=ub: e.activation(
            usq[:, 1024 * ub: 1024 * ub + 1024],
            u[:, 1024 * tile: 1024 * tile + 1024], AF.Square),
            deps=[m_av_used[0], m_av_used[1]] + [m_usq_rd[ub]])
        m_usq_w = S.marker("scalar")
        S.emit("vector", lambda e, tile=tile: e.reduce_sum(
            lnS[:, 0:16],
            bass.AP(u, 1024 * tile, [[NT * 1024, 128], [64, 16], [1, 64]]),
            mybir.AxisListType.X))
        S.emit("vector", lambda e, ub=ub: e.reduce_sum(
            lnS[:, 16:32], bass.AP(usq, 1024 * ub, [[2048, 128], [64, 16], [1, 64]]),
            mybir.AxisListType.X), deps=[m_usq_w])
        m_usq_rd[ub] = S.marker("vector")
        m_s2 = S.marker("vector")
        S.emit("tensor", lambda e: e.matmul(
            p_stbc[0:8, 0:32], sb_sel8[:, :], lnS[:, :], start=True, stop=True),
            deps=[m_s2, m_consts])
        m_stmm = S.marker("tensor")
        S.emit("vector", lambda e: e.tensor_scalar_mul(lnT[:, :], p_stbc[0:8, 0:32], 1.0 / 1024.0),
               deps=[m_stmm])
        S.emit("vector", lambda e: e.tensor_mul(lnM[:, :], lnT[:, 0:16], lnT[:, 0:16]))
        S.emit("vector", lambda e: e.tensor_sub(lnM[:, :], lnT[:, 16:32], lnM[:, :]))
        m_var = S.marker("vector")
        S.emit("scalar", lambda e: e.activation(
            lnM[:, :], lnM[:, :], AF.Sqrt, bias=sb_eps[0:8, 0:1]), deps=[m_var, m_consts])
        m_std = S.marker("scalar")
        S.emit("vector", lambda e: e.reciprocal(lnT[:, 16:32], lnM[:, :]),
               deps=[m_std])
        S.emit("vector", lambda e: e.tensor_mul(lnT[:, 0:16], lnT[:, 0:16], lnT[:, 16:32]))
        m_mrs = S.marker("vector")
        S.emit("tensor", lambda e, tile=tile: e.matmul(
            p_stbc[:, 32:64], sb_sel8t[:, :], lnT[:, :], start=True, stop=True),
            deps=[m_mrs])
        m_bcmm = S.marker("tensor")
        S.emit("vector", lambda e, tile=tile: e.tensor_copy(
            bcS[:, 32 * tile: 32 * tile + 32], p_stbc[:, 32:64]), deps=[m_bcmm])
    for tile in range(NT):
        S.emit("vector", lambda e, tile=tile: e.tensor_tensor(
            bass.AP(u, 1024 * tile, [[NT * 1024, 128], [64, 16], [1, 64]]),
            bass.AP(u, 1024 * tile, [[NT * 1024, 128], [64, 16], [1, 64]]),
            bass.AP(bcS, 32 * tile + 16, [[512, 128], [1, 16], [0, 64]]), ALU.mult))
        S.emit("vector", lambda e, tile=tile: e.tensor_tensor(
            bass.AP(u, 1024 * tile, [[NT * 1024, 128], [64, 16], [1, 64]]),
            bass.AP(u, 1024 * tile, [[NT * 1024, 128], [64, 16], [1, 64]]),
            bass.AP(bcS, 32 * tile, [[512, 128], [1, 16], [0, 64]]), ALU.subtract))

    m_attn_pe = S.marker("tensor")
    m_attn_v = S.marker("vector")
    m_attn_s = S.marker("scalar")

    esA.close()

    if DEBUG_A:
        total_sync = S.cnt["sync"]
        if total_sync:
            S.ops["sync"].append(lambda e, v=16 * total_sync: e.wait_ge(S.sems["sync"], v))
        with nc.Block() as block:
            @block.sync
            def _(eng):
                for fn in S.ops["sync"]:
                    fn(eng)

            @block.gpsimd
            def _(eng):
                for fn in S.ops["gpsimd"]:
                    fn(eng)

            @block.tensor
            def _(eng):
                for fn in S.ops["tensor"]:
                    fn(eng)

            @block.vector
            def _(eng):
                for fn in S.ops["vector"]:
                    fn(eng)

            @block.scalar
            def _(eng):
                for fn in S.ops["scalar"]:
                    fn(eng)
        esA.close()
        es_top.close()
        return nc
    esB = ExitStack()
    u1p16 = esB.enter_context(nc.sbuf_tensor("u1p16", [64, 16 * RANK], BF16))
    v116 = esB.enter_context(nc.sbuf_tensor("v116", [128, 2 * DFF], BF16))
    u216 = esB.enter_context(nc.sbuf_tensor("u216", [128, 32 * RANK], BF16))
    v2p16 = esB.enter_context(nc.sbuf_tensor("v2p16", [128, 2 * 1024], BF16))
    y16 = esB.enter_context(nc.sbuf_tensor("y16", [128, 4096], BF16))
    YTl = esB.enter_context(nc.sbuf_tensor("YTl", [64, 4096], BF16))
    YTh = esB.enter_context(nc.sbuf_tensor("YTh", [64, 4096], BF16))
    P16 = esB.enter_context(nc.sbuf_tensor("P16", [128, 1024], BF16))
    H16 = esB.enter_context(nc.sbuf_tensor("H16", [128, 32 * 512], BF16))
    Q2T = esB.enter_context(nc.sbuf_tensor("Q2T", [128, 1024], BF16))
    FTs = esB.enter_context(nc.sbuf_tensor("FTs", [128, 4096], F32))

    p_a = esB.enter_context(nc.psum_tensor("p_a", [128, 512], F32))
    p_b = esB.enter_context(nc.psum_tensor("p_b", [128, 512], F32))
    p_c = esB.enter_context(nc.psum_tensor("p_c", [128, 512], F32))
    p_d = esB.enter_context(nc.psum_tensor("p_d", [128, 512], F32))
    p_fs = esB.enter_context(nc.psum_tensor("p_fs", [128, 256], F32))
    p_yt = esB.enter_context(nc.psum_tensor("p_yt", [64, 256], F32))
    p_st2 = esB.enter_context(nc.psum_tensor("p_st2", [128, 64], F32))

    gate = [m_attn_pe, m_attn_v, m_attn_s]
    S.emit("gpsimd", lambda e: e.dma_start(u1p16[:, :], d_u1.ap()), deps=gate)
    S.emit("gpsimd", lambda e: e.dma_start(
        bass.AP(v116, 0, [[2 * DFF, 128], [DFF, 2], [1, DFF]]),
        bass.AP(d_v1, 0, [[DFF, 128], [128 * DFF, 2], [1, DFF]])))
    S.emit("gpsimd", lambda e: e.dma_start(
        bass.AP(u216, 0, [[32 * RANK, 128], [RANK, 32], [1, RANK]]),
        bass.AP(d_u2, 0, [[RANK, 128], [128 * RANK, 32], [1, RANK]])))
    S.emit("gpsimd", lambda e: e.dma_start(
        bass.AP(v2p16, 0, [[2 * 1024, 128], [1024, 2], [1, 1024]]),
        bass.AP(d_v2, 0, [[1024, 128], [128 * 1024, 2], [1, 1024]])))
    m_wffn = S.marker("gpsimd")

    m_y16_used = ("tensor", 0)
    m_yt_used = ("vector", 0)
    m_yts_used = ("tensor", 0)
    m_p16_used = ("tensor", 0)
    m_h16_used = ("tensor", 0)
    m_q2t_used = ("tensor", 0)
    m_ft_used = ("tensor", 0)
    m_ftv_used = ("vector", 0)
    m_pab_used = ("vector", 0)
    m_pcd_used = ("scalar", 0)
    m_fs_used = ("vector", 0)
    m_outS_used = [("sync", 0), ("sync", 0)]
    n_store = 0

    for f in range(FSLAB):
        c0 = 4096 * f
        S.emit("vector", lambda e, c0=c0: e.tensor_copy(
            y16[:, :], u[:, c0: c0 + 4096]), deps=[m_y16_used])
        m_y16 = S.marker("vector")

        for hi in range(2):
            ytd = YTl if hi == 0 else YTh
            for blk in range(16):
                for gg in range(4):
                    gidx = 4 * blk + gg
                    S.emit("tensor", lambda e, hi=hi, gidx=gidx, gg=gg:
                           e.matmul(p_yt[:, 64 * gg: 64 * gg + 64],
                                    y16[64 * hi: 64 * hi + 64, 64 * gidx: 64 * gidx + 64],
                                    sb_id16[64 * hi: 64 * hi + 64, :],
                                    start=True, stop=True),
                           deps=([m_y16, m_yt_used, m_consts] if gg == 0 else []), inc=(gg == 3))
                m_ytp = S.marker("tensor")
                S.emit("vector", lambda e, ytd=ytd, blk=blk: e.tensor_copy(
                    ytd[:, 256 * blk: 256 * blk + 256], p_yt[:, :]),
                    deps=[m_ytp, m_yts_used])
                m_yt_used = S.marker("vector")
        m_yt_ready = S.marker("vector")

        for rh in range(2):
            ps = p_a if rh == 0 else p_b
            for hc in range(16):
                hi, ch = hc // 8, hc % 8
                yts = YTl if hi == 0 else YTh
                S.emit("tensor", lambda e, rh=rh, hi=hi, ch=ch, ps=ps, yts=yts, hc=hc:
                       e.matmul(ps[:, :],
                                u1p16[:, RANK * (ch + 8 * hi) + 128 * rh:
                                      RANK * (ch + 8 * hi) + 128 * rh + 128],
                                bass.AP(yts, ch, [[4096, 64], [64, 64], [8, 8]]),
                                start=(hc == 0), stop=(hc == 15)),
                       deps=([m_yt_ready, m_wffn, m_pab_used, m_consts] if hc == 0 else []),
                       inc=(hc == 15))
            m_pmm = S.marker("tensor")
            S.emit("vector", lambda e, rh=rh, ps=ps: e.tensor_copy(
                P16[:, 512 * rh: 512 * rh + 512], ps[:, :]), deps=[m_pmm, m_p16_used])
        m_pab_used = S.marker("vector")
        m_p16 = S.marker("vector")

        h_rot = [p_a, p_b, p_c, p_d]
        h_used = [m_pab_used, m_pab_used, m_pcd_used, m_pcd_used]
        for ffc in range(32):
            ps = h_rot[ffc % 4]
            for rc in range(2):
                S.emit("tensor", lambda e, ffc=ffc, rc=rc, ps=ps:
                       e.matmul(ps[:, :],
                                v116[:, DFF * rc + 128 * ffc: DFF * rc + 128 * ffc + 128],
                                P16[:, 512 * rc: 512 * rc + 512],
                                start=(rc == 0), stop=(rc == 1)),
                       deps=([m_p16, h_used[ffc % 4]] if rc == 0 else []), inc=(rc == 1))
            m_hmm = S.marker("tensor")
            S.emit("scalar", lambda e, ffc=ffc, ps=ps: e.activation(
                H16[:, 512 * ffc: 512 * ffc + 512], ps[:, :], AF.Gelu_apprx_tanh,
                bias=sb_b1[:, ffc: ffc + 1]), deps=[m_hmm, m_h16_used, m_consts])
            h_used[ffc % 4] = S.marker("scalar")
            if ffc % 2 == 1:
                m_pcd_used = S.marker("scalar")
        m_h16 = S.marker("scalar")

        for rh in range(2):
            ps = p_a if rh == 0 else p_b
            for ffc in range(32):
                S.emit("tensor", lambda e, rh=rh, ffc=ffc, ps=ps:
                       e.matmul(ps[:, :],
                                u216[:, RANK * ffc + 128 * rh: RANK * ffc + 128 * rh + 128],
                                H16[:, 512 * ffc: 512 * ffc + 512],
                                start=(ffc == 0), stop=(ffc == 31)),
                       deps=([m_h16, m_pab_used] if ffc == 0 else []), inc=(ffc == 31))
            m_qmm = S.marker("tensor")
            S.emit("vector", lambda e, rh=rh, ps=ps: e.tensor_copy(
                Q2T[:, 512 * rh: 512 * rh + 512], ps[:, :]), deps=[m_qmm, m_q2t_used])
        m_pab_used = S.marker("vector")
        m_q2t = S.marker("vector")

        for ch in range(8):
            ps = p_c if ch % 2 == 0 else p_d
            for rc in range(2):
                S.emit("tensor", lambda e, ch=ch, rc=rc, ps=ps:
                       e.matmul(ps[:, :],
                                v2p16[:, 1024 * rc + 128 * ch: 1024 * rc + 128 * ch + 128],
                                Q2T[:, 512 * rc: 512 * rc + 512],
                                start=(rc == 0), stop=(rc == 1)),
                       deps=([m_q2t, m_pcd_used, m_ftv_used] if rc == 0 else []), inc=(rc == 1))
            m_fmm = S.marker("tensor")
            S.emit("scalar", lambda e, ch=ch, ps=ps:
                   e.activation(
                       bass.AP(FTs, ch, [[4096, 64], [1024, 4], [8, 128]]),
                       ps[0:64, :], AF.Identity, bias=sb_b2[0:64, ch: ch + 1]),
                   deps=[m_fmm, m_ft_used, m_consts])
            S.emit("vector", lambda e, ch=ch, ps=ps:
                   e.tensor_scalar_add(
                       bass.AP(FTs, 64 * 4096 + ch, [[4096, 64], [1024, 4], [8, 128]]),
                       ps[64:128, :], sb_b2[64:128, ch: ch + 1]),
                   deps=[m_fmm, m_ft_used, m_consts])
            m_ftv_used = S.marker("vector")
            if ch % 2 == 1:
                m_pcd_used = S.marker("scalar")
        m_ft = S.marker("scalar")
        m_ft_v = S.marker("vector")

        for tp in range(4):
            tile = 4 * f + tp
            for gb in range(4):
                for hi in range(2):
                    for gg in range(4):
                        g = 4 * gb + gg
                        S.emit("tensor", lambda e, hi=hi, g=g, gg=gg, tp=tp:
                               e.matmul(p_fs[64 * hi: 64 * hi + 64, 64 * gg: 64 * gg + 64],
                                        FTs[64 * hi: 64 * hi + 64,
                                            1024 * tp + 64 * g: 1024 * tp + 64 * g + 64],
                                        sb_idf[64 * hi: 64 * hi + 64, :],
                                        start=True, stop=True),
                               deps=([m_ft, m_ft_v, m_fs_used, m_consts] if (hi == 0 and gg == 0) else []),
                               inc=(hi == 1 and gg == 3))
                m_fsmm = S.marker("tensor")
                S.emit("vector", lambda e, tile=tile, gb=gb: e.scalar_tensor_tensor(
                    u[:, 1024 * tile + 256 * gb: 1024 * tile + 256 * gb + 256],
                    p_fs[:, :], 1.0,
                    u[:, 1024 * tile + 256 * gb: 1024 * tile + 256 * gb + 256],
                    ALU.mult, ALU.add), deps=[m_fsmm])
                m_fs_used = S.marker("vector")


    m_usq_rd = [("vector", 0), ("vector", 0)]

    for tile in range(NT):
        ub = tile % 2
        S.emit("scalar", lambda e, tile=tile, ub=ub: e.activation(
            usq[:, 1024 * ub: 1024 * ub + 1024],
            u[:, 1024 * tile: 1024 * tile + 1024], AF.Square),
            deps=[m_fs_used] + [m_usq_rd[ub]])
        m_usq_w = S.marker("scalar")
        S.emit("vector", lambda e, tile=tile: e.reduce_sum(
            lnS[:, 0:16],
            bass.AP(u, 1024 * tile, [[NT * 1024, 128], [64, 16], [1, 64]]),
            mybir.AxisListType.X), deps=[m_fs_used])
        S.emit("vector", lambda e, ub=ub: e.reduce_sum(
            lnS[:, 16:32], bass.AP(usq, 1024 * ub, [[2048, 128], [64, 16], [1, 64]]),
            mybir.AxisListType.X), deps=[m_usq_w])
        m_usq_rd[ub] = S.marker("vector")
        m_s2 = S.marker("vector")
        S.emit("tensor", lambda e: e.matmul(
            p_st2[0:8, 0:32], sb_sel8[:, :], lnS[:, :], start=True, stop=True),
            deps=[m_s2, m_consts])
        m_stmm = S.marker("tensor")
        S.emit("vector", lambda e: e.tensor_scalar_mul(lnT[:, :], p_st2[0:8, 0:32], 1.0 / 1024.0),
               deps=[m_stmm])
        S.emit("vector", lambda e: e.tensor_mul(lnM[:, :], lnT[:, 0:16], lnT[:, 0:16]))
        S.emit("vector", lambda e: e.tensor_sub(lnM[:, :], lnT[:, 16:32], lnM[:, :]))
        m_var = S.marker("vector")
        S.emit("scalar", lambda e: e.activation(
            lnM[:, :], lnM[:, :], AF.Sqrt, bias=sb_eps[0:8, 0:1]), deps=[m_var, m_consts])
        m_std = S.marker("scalar")
        S.emit("vector", lambda e: e.reciprocal(lnT[:, 16:32], lnM[:, :]),
               deps=[m_std])
        S.emit("vector", lambda e: e.tensor_mul(lnT[:, 0:16], lnT[:, 0:16], lnT[:, 16:32]))
        m_mrs = S.marker("vector")
        S.emit("tensor", lambda e, tile=tile: e.matmul(
            p_st2[:, 32:64], sb_sel8t[:, :], lnT[:, :], start=True, stop=True),
            deps=[m_mrs])
        m_bcmm = S.marker("tensor")
        S.emit("vector", lambda e, tile=tile: e.tensor_copy(
            bcS[:, 32 * tile: 32 * tile + 32], p_st2[:, 32:64]), deps=[m_bcmm])
    for tile in range(NT):
        ob = tile % 2
        S.emit("vector", lambda e, tile=tile, ob=ob: e.tensor_tensor(
            bass.AP(outS, 1024 * ob, [[2048, 128], [64, 16], [1, 64]]),
            bass.AP(u, 1024 * tile, [[NT * 1024, 128], [64, 16], [1, 64]]),
            bass.AP(bcS, 32 * tile + 16, [[512, 128], [1, 16], [0, 64]]), ALU.mult),
            deps=[m_outS_used[ob]])
        S.emit("vector", lambda e, tile=tile, ob=ob: e.tensor_tensor(
            bass.AP(outS, 1024 * ob, [[2048, 128], [64, 16], [1, 64]]),
            bass.AP(outS, 1024 * ob, [[2048, 128], [64, 16], [1, 64]]),
            bass.AP(bcS, 32 * tile, [[512, 128], [1, 16], [0, 64]]), ALU.subtract))
        m_norm = S.marker("vector")
        S.emit("sync", lambda e, tile=tile, ob=ob: e.dma_start(
            bass.AP(d_out, 128 * 1024 * tile, [[1024, 128], [1, 1024]]),
            outS[:, 1024 * ob: 1024 * ob + 1024]), deps=[m_norm])
        m_outS_used[ob] = S.marker("sync")

    total_sync = S.cnt["sync"]
    S.ops["sync"].append(lambda e, v=16 * total_sync: e.wait_ge(S.sems["sync"], v))

    with nc.Block() as block:
        @block.sync
        def _(eng):
            for fn in S.ops["sync"]:
                fn(eng)

        @block.gpsimd
        def _(eng):
            for fn in S.ops["gpsimd"]:
                fn(eng)

        @block.tensor
        def _(eng):
            for fn in S.ops["tensor"]:
                fn(eng)

        @block.vector
        def _(eng):
            for fn in S.ops["vector"]:
                fn(eng)

        @block.scalar
        def _(eng):
            for fn in S.ops["scalar"]:
                fn(eng)

    esB.close()
    es_top.close()
    return nc



def _rope_tables():
    freqs = (10000.0 ** (-(np.arange(0, DK, 2, dtype=np.float64) / D))).astype(np.float32)
    ang = (np.arange(NCORES * T, dtype=np.float32)[:, None] * freqs[None, :]).astype(np.float32)
    libm = ctypes.CDLL(ctypes.util.find_library("m"))
    libm.cosf.restype = ctypes.c_float
    libm.cosf.argtypes = [ctypes.c_float]
    libm.sinf.restype = ctypes.c_float
    libm.sinf.argtypes = [ctypes.c_float]
    flat = ang.ravel()
    cos = np.fromiter((libm.cosf(float(v)) for v in flat), np.float32, flat.size).reshape(ang.shape)
    sin = np.fromiter((libm.sinf(float(v)) for v in flat), np.float32, flat.size).reshape(ang.shape)
    return cos, sin


_NC_CACHE = {}


def kernel(L0_attn_x, L0_attn_Wq, L0_attn_bq, L0_attn_Wk, L0_attn_bk,
           L0_attn_Wv, L0_attn_bv, L0_ffn_U1, L0_ffn_V1, L0_ffn_b1,
           L0_ffn_U2, L0_ffn_V2, L0_ffn_b2):
    x = np.ascontiguousarray(np.asarray(L0_attn_x, dtype=np.float32).reshape(NCORES * T, D))

    perm = np.concatenate([
        np.concatenate([np.arange(64 * c, 64 * c + 64),
                        np.arange(64 * (c + 8), 64 * (c + 8) + 64)])
        for c in range(8)])
    wq = np.ascontiguousarray(np.asarray(L0_attn_Wq, np.float32)[:, perm])
    wk = np.ascontiguousarray(np.asarray(L0_attn_Wk, np.float32)[:, perm])
    wv = np.ascontiguousarray(np.asarray(L0_attn_Wv, np.float32)[:, perm])
    v2p = np.ascontiguousarray(np.asarray(L0_ffn_V2, np.float32)[:, perm])

    def bias_chunks(b):
        return np.ascontiguousarray(np.asarray(b, np.float32)[perm].reshape(8, 128).T)

    bqp, bkp, bvp = (bias_chunks(b) for b in (L0_attn_bq, L0_attn_bk, L0_attn_bv))
    b2p = bias_chunks(L0_ffn_b2)
    u1p = np.ascontiguousarray(
        np.asarray(L0_ffn_U1, np.float32).reshape(16, 64, RANK).transpose(1, 0, 2).reshape(64, 16 * RANK))
    v1r = np.ascontiguousarray(np.asarray(L0_ffn_V1, np.float32))
    u2r = np.ascontiguousarray(np.asarray(L0_ffn_U2, np.float32))
    b1p = np.ascontiguousarray(np.asarray(L0_ffn_b1, np.float32).reshape(32, 128).T)

    cos, sin = _rope_tables()

    mask01 = np.zeros((128, 128), np.float32)
    for r in range(128):
        for n in range(128):
            if (r % 8) == ((n % 64) >> 3):
                mask01[r, n] = 1.0
    mask01 = mask01.astype(ml_dtypes.bfloat16)
    sel8 = np.zeros((128, 8), np.float32)
    for row in range(128):
        sel8[row, (row % 64) >> 3] = 1.0
    sel8t = np.ascontiguousarray(sel8.T)
    id16 = np.concatenate([np.eye(64, dtype=ml_dtypes.bfloat16)] * 2, 0)
    idf = np.concatenate([np.eye(64, dtype=np.float32)] * 2, 0)
    epsc = np.full((128, 1), 1e-5, np.float32)

    if "nc" not in _NC_CACHE:
        _NC_CACHE["nc"] = build_nc()
    nc = _NC_CACHE["nc"]

    in_maps = []
    for c in range(NCORES):
        xc = x[T * c: T * (c + 1)]
        xt = np.ascontiguousarray(xc.T)
        v = xc.reshape(NT, 16, 8, 2, 8, 64)
        xs = np.ascontiguousarray(v.transpose(0, 3, 2, 4, 1, 5).reshape(T, D))
        cos128 = np.ascontiguousarray(
            np.tile(np.repeat(cos[T * c: T * (c + 1)].T, 2, axis=0), (2, 1)))
        sin128 = np.ascontiguousarray(
            np.tile(np.repeat(sin[T * c: T * (c + 1)].T, 2, axis=0), (2, 1)))
        in_maps.append({
            "xt": xt, "xs": xs, "wq": wq, "wk": wk, "wv": wv,
            "bqp": bqp, "bkp": bkp, "bvp": bvp,
            "u1p": u1p, "v1r": v1r, "b1p": b1p, "u2r": u2r, "v2p": v2p, "b2p": b2p,
            "cos128": cos128, "sin128": sin128,
            "mask01": mask01, "sel8": sel8, "sel8t": sel8t,
            "id16": id16, "idf": idf, "epsc": epsc,
        })

    res = run_bass_kernel_spmd(nc, in_maps, core_ids=list(range(NCORES)))

    out = np.empty((NCORES * T, D), np.float32)
    for c in range(NCORES):
        o = np.asarray(res.results[c]["outs"], np.float32)
        v = o.reshape(NT, 2, 8, 8, 16, 64)
        out[T * c: T * (c + 1)] = v.transpose(0, 4, 2, 1, 3, 5).reshape(T, D)
    return out.reshape(1, NCORES * T, D)
